# version 10
# baseline (speedup 1.0000x reference)
import contextlib
import numpy as np
import concourse.bass as bass
import concourse.mybir as mybir
from concourse.bass_utils import run_bass_kernel_spmd

F32 = mybir.dt.float32
BF16 = mybir.dt.bfloat16
U8 = mybir.dt.uint8
AF = mybir.ActivationFunctionType
ALU = mybir.AluOpType
AX = mybir.AxisListType

NCORES = 8
B, T, D = 16, 2048, 1024
BL = B // NCORES
I = 2048
S = 16
TT = 512
NT = BL * T // TT
NTB = TT // 128
ND = D // 128
NI = I // 128
NST = NT * NTB
QB = 128.0
CLIPV = float(np.float32(QB - 1e-5))
LN1_EPS = 1e-5
LN2_EPS = 1e-3
W_EPS = 1e-5
KB = 1024


class Ctr:
    def __init__(self):
        self.v = {}

    def inc(self, name, n):
        self.v[name] = self.v.get(name, 0) + n
        return self.v[name]

    def val(self, name):
        return self.v.get(name, 0)


class Carver:
    def __init__(self, arena_ap, nbytes):
        self.ap = arena_ap
        self.n = nbytes
        self.off = 0

    def reset(self):
        self.off = 0

    def take(self, name, cols, dtype):
        esz = 2 if dtype == BF16 else 4
        nb = cols * esz
        off = (self.off + 63) & ~63
        assert off + nb <= self.n, f"arena overflow at {name}: {off}+{nb}>{self.n}"
        self.off = off + nb
        v = self.ap[:, off:off + nb].bitcast(dtype)
        assert v.shape == (128, cols), (name, v.shape)
        return v

    def at(self, off, name, cols, dtype):
        esz = 2 if dtype == BF16 else 4
        nb = cols * esz
        assert off % 64 == 0 and off + nb <= self.n, name
        v = self.ap[:, off:off + nb].bitcast(dtype)
        assert v.shape == (128, cols), (name, v.shape)
        return v


def build_program():
    nc = bass.Bass("TRN2", target_bir_lowering=False, debug=False)

    x_h = nc.dram_tensor("x", [BL, T, D], F32, kind="ExternalInput")
    w_in_h = nc.dram_tensor("w_in", [D, 2 * I], F32, kind="ExternalInput")
    conv_w_h = nc.dram_tensor("conv_w", [4, 1, I], F32, kind="ExternalInput")
    conv_b_h = nc.dram_tensor("conv_b", [I], F32, kind="ExternalInput")
    w_x_h = nc.dram_tensor("w_x", [I, D + 2 * S], F32, kind="ExternalInput")
    w_dt_h = nc.dram_tensor("w_dt", [D, I], F32, kind="ExternalInput")
    b_dt_h = nc.dram_tensor("b_dt", [I], F32, kind="ExternalInput")
    a_log_h = nc.dram_tensor("a_log", [I, S], F32, kind="ExternalInput")
    w_out_h = nc.dram_tensor("w_out", [I, D], F32, kind="ExternalInput")
    fw_h = nc.dram_tensor("fw", [D, D], F32, kind="ExternalInput")
    lng_h = nc.dram_tensor("lng", [D], F32, kind="ExternalInput")
    lnb_h = nc.dram_tensor("lnb", [D], F32, kind="ExternalInput")
    out_h = nc.dram_tensor("out", [BL, T, D], F32, kind="ExternalOutput")

    u_hbm = nc.dram_tensor("u_scr", [NT * 128, NI * TT], BF16)
    y2_hbm = nc.dram_tensor("y2_scr", [NT * 128, NI * TT], BF16)

    idf = nc.alloc_sbuf_tensor("idf", [128, 128], F32).ap()
    idbf = nc.alloc_sbuf_tensor("idbf", [128, 128], BF16).ap()
    ones1 = nc.alloc_sbuf_tensor("ones1", [1, 128], F32).ap()
    onesc = nc.alloc_sbuf_tensor("onesc", [128, 1], F32).ap()
    g_rep = nc.alloc_sbuf_tensor("g_rep", [128, D], F32).ap()
    b_rep = nc.alloc_sbuf_tensor("b_rep", [128, D], F32).ap()
    stash = nc.alloc_sbuf_tensor("stash", [128, 3 * NST], F32).ap()
    hc = nc.alloc_sbuf_tensor("hc", [128, NI * S], F32).ap()
    a_sb = nc.alloc_sbuf_tensor("a_sb", [128, NI * S], F32).ap()
    cw_sb = nc.alloc_sbuf_tensor("cw_sb", [128, 4 * NI], F32).ap()
    cb_sb = nc.alloc_sbuf_tensor("cb_sb", [128, NI], F32).ap()
    bdt_sb = nc.alloc_sbuf_tensor("bdt_sb", [128, NI], F32).ap()
    tails = nc.alloc_sbuf_tensor("tails", [128, 3 * NI], F32).ap()
    grun = nc.alloc_sbuf_tensor("grun", [128, 2], F32).ap()
    smll = nc.alloc_sbuf_tensor("smll", [1, 128], F32).ap()
    vec4 = nc.alloc_sbuf_tensor("vec4", [128, 4], F32).ap()
    tvec = nc.alloc_sbuf_tensor("tvec", [128, 24], F32).ap()
    cbank = nc.alloc_sbuf_tensor("cbank", [128, 4], F32).ap()
    fbv = nc.alloc_sbuf_tensor("fbv", [1, 4], F32).ap()
    fba = nc.alloc_sbuf_tensor("fba", [1, 4], F32).ap()

    arena_bytes = (nc.sbuf_bytes_remaining - 192) & ~63
    arena_t = nc.alloc_sbuf_tensor("arena", [128, arena_bytes], U8)
    ar = Carver(arena_t.ap(), arena_bytes)

    pf = [nc.alloc_psum_tensor(f"pf{i}", [128, 512], F32).ap() for i in range(6)]
    pb = [nc.alloc_psum_tensor(f"pb{i}", [128, 1024], BF16).ap() for i in range(2)]

    es = contextlib.ExitStack()
    sem = {}
    for name in (["md", "ve", "ac", "pe", "gp", "ph", "us", "ys", "yl", "fs"]
                 + [f"xd{i}" for i in range(5)] + [f"ws{i}" for i in range(2)]
                 + [f"ud{i}" for i in range(2)] + [f"od{i}" for i in range(4)]):
        sem[name] = es.enter_context(nc.semaphore(name))
    block = es.enter_context(nc.Block())
    c = Ctr()

    def emit(engname, fn):
        if engname == "vector":
            def wrapped(e, fn=fn):
                fn(e)
                e.wait_ge(sem["ve"], c.val("ve"))
                TK(e.memset(fbv[0:1, 0:1], 0.0), "ve")
            getattr(block, engname)(wrapped)
        elif engname == "scalar":
            def wrapped(e, fn=fn):
                fn(e)
                e.wait_ge(sem["ac"], c.val("ac"))
                TK(e.activation(fba[0:1, 0:1], fba[0:1, 1:2], AF.Copy,
                                scale=0.0), "ac")
            getattr(block, engname)(wrapped)
        else:
            getattr(block, engname)(fn)

    def W(e, name, val):
        assert val <= c.val(name), f"wait on future {name}: {val} > {c.val(name)}"
        if val > 0:
            e.wait_ge(sem[name], val)

    def TK(instr, s):
        instr.then_inc(sem[s], 1)
        return c.inc(s, 1)

    def SW(e, s):
        e.wait_ge(sem[s], c.val(s))



    def weight_load_cast(dst_bf, src_ap_fn, stage_f32, half_cols, kblks, row_cols):
        assert row_cols <= half_cols
        for kb in range(kblks):
            h0 = (kb % 2) * half_cols
            ws = f"ws{kb % 2}"

            def ld(e, kb=kb, h0=h0, ws=ws):
                W(e, "ac", c.val("ac"))
                e.dma_start(out=stage_f32[:, h0:h0 + row_cols],
                            in_=src_ap_fn(kb)).then_inc(sem[ws], 16)
            wd_after = c.inc(ws, 16)
            emit("sync", ld)

            def cast(e, kb=kb, h0=h0, wd_after=wd_after, ws=ws):
                W(e, ws, wd_after)
                SW(e, "ac")
                if kb == 0:
                    W(e, "ve", c.val("ve"))
                    W(e, "pe", c.val("pe"))
                TK(e.copy(dst_bf[:, kb * row_cols:(kb + 1) * row_cols],
                          stage_f32[:, h0:h0 + row_cols]), "ac")
            emit("scalar", cast)

    def barrier():
        target = c.val("ph") + 5
        for eng in ["sync", "vector", "scalar", "tensor", "gpsimd"]:
            emit(eng, lambda e: e.drain().then_inc(sem["ph"], 1))
        c.inc("ph", 5)
        for eng in ["sync", "vector", "scalar", "tensor", "gpsimd"]:
            emit(eng, lambda e, t=target: e.wait_ge(sem["ph"], t))

    iotat = ar.take("iotat", 128, F32)
    lngr_c = ar.take("lngr", D, F32)
    lnbr_c = ar.take("lnbr", D, F32)
    lngr = lngr_c[0:1, :]
    lnbr = lnbr_c[0:1, :]

    def p_dma(e):
        for ib in range(NI):
            e.dma_start(out=a_sb[:, ib * S:(ib + 1) * S],
                        in_=a_log_h.ap()[ib * 128:(ib + 1) * 128, :]).then_inc(sem["md"], 16)
            e.dma_start(out=cb_sb[:, ib:ib + 1],
                        in_=conv_b_h.ap()[ib * 128:(ib + 1) * 128].unsqueeze(1)).then_inc(sem["md"], 16)
            e.dma_start(out=bdt_sb[:, ib:ib + 1],
                        in_=b_dt_h.ap()[ib * 128:(ib + 1) * 128].unsqueeze(1)).then_inc(sem["md"], 16)
            for k in range(4):
                e.dma_start(out=cw_sb[:, k * NI + ib:k * NI + ib + 1],
                            in_=conv_w_h.ap()[k, 0, ib * 128:(ib + 1) * 128].unsqueeze(1)).then_inc(sem["md"], 16)
        e.dma_start(out=lngr, in_=lng_h.ap().unsqueeze(0)).then_inc(sem["md"], 16)
        e.dma_start(out=lnbr, in_=lnb_h.ap().unsqueeze(0)).then_inc(sem["md"], 16)
    n_pre = NI * 7 + 2
    md_pre = c.inc("md", 16 * n_pre)
    emit("sync", p_dma)

    def p_iota(e):
        e.iota(iotat, [[1, 128]], channel_multiplier=-1,
               allow_small_or_imprecise_dtypes=True).then_inc(sem["gp"], 1)
    gp_iota = c.inc("gp", 1)
    emit("gpsimd", p_iota)

    def p_ids(e):
        W(e, "gp", gp_iota)
        TK(e.tensor_scalar(idf, iotat, 0.0, None, ALU.is_equal), "ve")
        SW(e, "ve")
        TK(e.tensor_copy(idbf, idf), "ve")
        TK(e.memset(ones1, 1.0), "ve")
        TK(e.memset(onesc, 1.0), "ve")
        TK(e.memset(grun, 0.0), "ve")
        TK(e.memset(cbank[:, 0:1], LN1_EPS), "ve")
        TK(e.memset(cbank[:, 1:2], LN2_EPS), "ve")
        TK(e.memset(cbank[:, 2:3], -0.5), "ve")
        TK(e.memset(cbank[:, 3:4], 1.0 / D), "ve")
    emit("vector", p_ids)
    ve_ids = c.val("ve")

    def p_aexp(e):
        W(e, "md", md_pre)
        TK(e.activation(a_sb, a_sb, AF.Exp), "ac")
    emit("scalar", p_aexp)
    ac_aexp = c.val("ac")

    def p_aneg(e):
        W(e, "ac", ac_aexp)
        SW(e, "ve")
        TK(e.tensor_scalar(a_sb, a_sb, -1.0, None, ALU.mult), "ve")
    emit("vector", p_aneg)

    def p_grep_mm(e):
        W(e, "md", md_pre)
        W(e, "ve", ve_ids)
        e.matmul(pf[2], ones1, lngr[0:1, 0:512], start=True, stop=True)
        e.matmul(pf[3], ones1, lngr[0:1, 512:1024], start=True, stop=True)
        e.matmul(pf[4], ones1, lnbr[0:1, 0:512], start=True, stop=True)
        TK(e.matmul(pf[5], ones1, lnbr[0:1, 512:1024], start=True, stop=True), "pe")
    emit("tensor", p_grep_mm)
    pe_grep = c.val("pe")

    def p_grep_ev(e):
        W(e, "pe", pe_grep)
        SW(e, "ve")
        TK(e.tensor_copy(g_rep[:, 0:512], pf[2]), "ve")
        TK(e.tensor_copy(g_rep[:, 512:1024], pf[3]), "ve")
        TK(e.tensor_copy(b_rep[:, 0:512], pf[4]), "ve")
        TK(e.tensor_copy(b_rep[:, 512:1024], pf[5]), "ve")
    emit("vector", p_grep_ev)

    ar.reset()
    w_in_x = ar.take("w_in_x", ND * I, BF16)
    xbuf = [ar.take(f"x{i}", D, F32) for i in range(5)]
    xnbf = [ar.take(f"xnbf{i}", D, BF16) for i in range(4)]
    xnsq = ar.take("xnsq", D, BF16)
    g1s = ar.take("g1s", D, F32)
    g2s = ar.take("g2s", D, BF16)
    xnf = ar.take("xnf", ND * TT, BF16)
    ubf_off = (ar.off + 63) & ~63
    ubf = ar.take("ubf", NI * TT, BF16)
    cacc = [ar.take(f"cacc{i}", 512, F32) for i in range(2)]
    wstage = ar.at(ubf_off, "wstage", 2 * 2048, F32)

    weight_load_cast(w_in_x, lambda kb: w_in_h.ap()[kb * 128:(kb + 1) * 128, 0:I],
                     wstage, 2048, ND, I)
    ac_wa = c.val("ac")

    for ti in range(NT):
        b = ti // (NT // BL)
        t0 = (ti % (NT // BL)) * TT
        first = (ti % (NT // BL)) == 0

        for tb in range(NTB):
            sti = ti * NTB + tb
            xb = xbuf[sti % 5]
            xs = f"xd{sti % 5}"

            def ld(e, b=b, t0=t0, tb=tb, xb=xb, sti=sti, xs=xs):
                W(e, "ve", c.v.get(f"a_xn_{sti - 5}", 0))
                e.dma_start(out=xb, in_=x_h.ap()[b, t0 + tb * 128:t0 + (tb + 1) * 128, :]
                            ).then_inc(sem[xs], 16)
            c.v[f"a_x_{sti}"] = c.inc(xs, 16)
            c.v[f"a_xsem_{sti}"] = xs
            emit("sync", ld)

        for tb in range(NTB):
            sti = ti * NTB + tb
            xb = xbuf[sti % 5]
            tvb = tb * 6

            def ve_ln(e, xb=xb, sti=sti, tvb=tvb, tb=tb):
                W(e, c.v[f"a_xsem_{sti}"], c.v[f"a_x_{sti}"])
                if tb == 0:
                    W(e, "ac", c.v.get(f"a_r2_{sti - 1}", 0))
                SW(e, "ve")
                T = tvec
                TK(e.scalar_tensor_tensor(xnsq, xb, 1.0, xb, ALU.mult, ALU.mult,
                                          accum_out=T[:, tvb:tvb + 1]), "ve")
                TK(e.reduce_sum(T[:, tvb + 1:tvb + 2], xb, AX.X), "ve")
                SW(e, "ve")
                TK(e.tensor_scalar(stash[:, sti:sti + 1], T[:, tvb + 1:tvb + 2],
                                   -1.0 / D, None, ALU.mult), "ve")
                TK(e.tensor_scalar(T[:, tvb + 2:tvb + 3], T[:, tvb:tvb + 1],
                                   1.0 / D, None, ALU.mult), "ve")
                SW(e, "ve")
                TK(e.scalar_tensor_tensor(T[:, tvb + 3:tvb + 4], stash[:, sti:sti + 1],
                                          1.0, stash[:, sti:sti + 1], ALU.mult,
                                          ALU.mult), "ve")
                SW(e, "ve")
                TK(e.tensor_sub(T[:, tvb + 2:tvb + 3], T[:, tvb + 2:tvb + 3],
                                T[:, tvb + 3:tvb + 4]), "ve")
            emit("vector", ve_ln)
            c.v[f"a_var_{sti}"] = c.val("ve")

        def ac_ln1(e, ti=ti):
            for tb in range(NTB):
                sti = ti * NTB + tb
                tvb = tb * 6
                W(e, "ve", c.v[f"a_var_{sti}"])
                SW(e, "ac")
                TK(e.activation(tvec[:, tvb + 3:tvb + 4], tvec[:, tvb + 2:tvb + 3],
                                AF.Ln, bias=cbank[:, 0:1]), "ac")
        emit("scalar", ac_ln1)

        def ac_exp1(e, ti=ti):
            for tb in range(NTB):
                sti = ti * NTB + tb
                tvb = tb * 6
                SW(e, "ac")
                TK(e.activation(stash[:, NST + sti:NST + sti + 1],
                                tvec[:, tvb + 3:tvb + 4], AF.Exp,
                                scale=cbank[:, 2:3]), "ac")
                c.v[f"a_rstd_{sti}"] = c.val("ac")
        emit("scalar", ac_exp1)

        for tb in range(NTB):
            sti = ti * NTB + tb
            xb = xbuf[sti % 5]
            xn = xnbf[tb]
            tvb = tb * 6

            def ve_xn(e, xb=xb, xn=xn, sti=sti, tvb=tvb):
                W(e, "ac", c.v[f"a_rstd_{sti}"])
                W(e, "pe", c.v.get(f"a_tp_{sti - 4}", 0))
                SW(e, "ve")
                TK(e.tensor_scalar(xn, xb, stash[:, sti:sti + 1],
                                   stash[:, NST + sti:NST + sti + 1],
                                   ALU.add, ALU.mult), "ve")
                SW(e, "ve")
                TK(e.scalar_tensor_tensor(xnsq, xn, 1.0, xn, ALU.mult, ALU.mult,
                                          accum_out=tvec[:, tvb + 4:tvb + 5]), "ve")
            emit("vector", ve_xn)
            c.v[f"a_xn_{sti}"] = c.val("ve")
            c.v[f"a_v2_{sti}"] = c.val("ve")

        def ac_ln2(e, ti=ti):
            for tb in range(NTB):
                sti = ti * NTB + tb
                tvb = tb * 6
                W(e, "ve", c.v[f"a_v2_{sti}"])
                SW(e, "ac")
                TK(e.activation(tvec[:, tvb + 5:tvb + 6], tvec[:, tvb + 4:tvb + 5],
                                AF.Ln, bias=cbank[:, 1:2], scale=cbank[:, 3:4]),
                   "ac")
        emit("scalar", ac_ln2)

        def ac_exp2(e, ti=ti):
            for tb in range(NTB):
                sti = ti * NTB + tb
                tvb = tb * 6
                SW(e, "ac")
                TK(e.activation(stash[:, 2 * NST + sti:2 * NST + sti + 1],
                                tvec[:, tvb + 5:tvb + 6], AF.Exp,
                                scale=cbank[:, 2:3]), "ac")
                c.v[f"a_r2_{sti}"] = c.val("ac")
        emit("scalar", ac_exp2)

        for tb in range(NTB):
            sti = ti * NTB + tb
            xn = xnbf[tb]
            pbk = pb[tb % 2]

            def pe_tp(e, xn=xn, pbk=pbk, sti=sti, tb=tb, ti=ti):
                W(e, "ve", c.v[f"a_xn_{sti}"])
                W(e, "ve", c.v.get(f"a_tpev_{ti}_{tb - 2}",
                                   c.v.get(f"a_tpev_{ti - 1}_{tb + 2}", 0)))
                for db in range(ND):
                    TK(e.transpose(pbk[:, db * 128:(db + 1) * 128],
                                   xn[:, db * 128:(db + 1) * 128], idbf), "pe")
            emit("tensor", pe_tp)
            c.v[f"a_tp_{sti}"] = c.val("pe")

            def ve_tpev(e, pbk=pbk, tb=tb, sti=sti):
                W(e, "pe", c.v[f"a_tp_{sti}"])
                SW(e, "ve")
                src_ = pbk.rearrange("p (d t) -> p d t", d=ND)
                dst = xnf.rearrange("p (d t) -> p d t", d=ND)[:, :, tb * 128:(tb + 1) * 128]
                TK(e.tensor_copy(dst, src_), "ve")
            emit("vector", ve_tpev)
            c.v[f"a_tpev_{ti}_{tb}"] = c.val("ve")

        for tb in range(NTB):
            sti = ti * NTB + tb
            xn = xnbf[tb]

            def ve_g(e, xn=xn, sti=sti):
                W(e, "ac", c.v[f"a_r2_{sti}"])
                SW(e, "ve")
                TK(e.scalar_tensor_tensor(g1s, xn,
                                          stash[:, 2 * NST + sti:2 * NST + sti + 1],
                                          g_rep, ALU.mult, ALU.mult), "ve")
                SW(e, "ve")
                TK(e.tensor_add(g2s, g1s, b_rep), "ve")
                SW(e, "ve")
                TK(e.tensor_reduce(grun[:, 1:2], g2s, AX.X, ALU.max,
                                   apply_absolute_value=True), "ve")
                SW(e, "ve")
                TK(e.tensor_tensor(grun[:, 0:1], grun[:, 0:1], grun[:, 1:2],
                                   ALU.max), "ve")
            emit("vector", ve_g)
        ve_xnf_done = c.val("ve")

        for ib in range(NI):
            pwk = pf[ib % 2]

            def pe_mm(e, ib=ib, pwk=pwk, ve_xnf_done=ve_xnf_done, ti=ti):
                W(e, "ac", ac_wa)
                W(e, "ve", ve_xnf_done)
                W(e, "ve", c.v.get(f"a_conv_{ti}_{ib - 2}",
                                   c.v.get(f"a_conv_{ti - 1}_{ib - 2 + NI}", 0)))
                for kb in range(ND):
                    mm = e.matmul(pwk, w_in_x[:, (kb * NI + ib) * 128:(kb * NI + ib + 1) * 128],
                                  xnf[:, kb * TT:(kb + 1) * TT],
                                  start=(kb == 0), stop=(kb == ND - 1))
                TK(mm, "pe")
            emit("tensor", pe_mm)
            c.v[f"a_mm_{ti}_{ib}"] = c.val("pe")

            def ve_conv(e, ib=ib, pwk=pwk, first=first, ti=ti):
                W(e, "pe", c.v[f"a_mm_{ti}_{ib}"])
                W(e, "ac", c.v.get(f"a_u_{ti}_{ib - 1}",
                                   c.v.get(f"a_u_{ti - 1}_{NI - 1}", 0)))
                SW(e, "ve")
                if first and ib == 0:
                    TK(e.memset(tails, 0.0), "ve")
                    SW(e, "ve")
                a0, a1 = cacc[0], cacc[1]
                tl = tails[:, ib * 3:(ib + 1) * 3]
                TK(e.tensor_scalar(a0, pwk, cw_sb[:, 3 * NI + ib:3 * NI + ib + 1],
                                  cb_sb[:, ib:ib + 1], ALU.mult, ALU.add), "ve")
                SW(e, "ve")
                TK(e.scalar_tensor_tensor(a1[:, 1:512], pwk[:, 0:511],
                                         cw_sb[:, 2 * NI + ib:2 * NI + ib + 1],
                                         a0[:, 1:512], ALU.mult, ALU.add), "ve")
                TK(e.scalar_tensor_tensor(a1[:, 0:1], tl[:, 2:3],
                                         cw_sb[:, 2 * NI + ib:2 * NI + ib + 1],
                                         a0[:, 0:1], ALU.mult, ALU.add), "ve")
                SW(e, "ve")
                TK(e.scalar_tensor_tensor(a0[:, 2:512], pwk[:, 0:510],
                                         cw_sb[:, 1 * NI + ib:1 * NI + ib + 1],
                                         a1[:, 2:512], ALU.mult, ALU.add), "ve")
                TK(e.scalar_tensor_tensor(a0[:, 0:2], tl[:, 1:3],
                                         cw_sb[:, 1 * NI + ib:1 * NI + ib + 1],
                                         a1[:, 0:2], ALU.mult, ALU.add), "ve")
                SW(e, "ve")
                TK(e.scalar_tensor_tensor(a1[:, 3:512], pwk[:, 0:509],
                                         cw_sb[:, 0 * NI + ib:0 * NI + ib + 1],
                                         a0[:, 3:512], ALU.mult, ALU.add), "ve")
                TK(e.scalar_tensor_tensor(a1[:, 0:3], tl[:, 0:3],
                                         cw_sb[:, 0 * NI + ib:0 * NI + ib + 1],
                                         a0[:, 0:3], ALU.mult, ALU.add), "ve")
                SW(e, "ve")
                TK(e.tensor_copy(tl, pwk[:, 509:512]), "ve")
            emit("vector", ve_conv)
            c.v[f"a_conv_{ti}_{ib}"] = c.val("ve")

            def ac_silu(e, ib=ib, ti=ti):
                W(e, "ve", c.v[f"a_conv_{ti}_{ib}"])
                W(e, "us", c.v.get(f"a_udma_{ti - 1}", 0))
                SW(e, "ac")
                TK(e.activation(ubf[:, ib * TT:(ib + 1) * TT], cacc[1], AF.Silu),
                  "ac")
            emit("scalar", ac_silu)
            c.v[f"a_u_{ti}_{ib}"] = c.val("ac")

        def st_u(e, ti=ti):
            W(e, "ac", c.v[f"a_u_{ti}_{NI - 1}"])
            e.dma_start(out=u_hbm.ap()[ti * 128:(ti + 1) * 128, :], in_=ubf
                        ).then_inc(sem["us"], 16)
        c.v[f"a_udma_{ti}"] = c.inc("us", 16)
        emit("sync", st_u)

    emit("sync", lambda e, v=c.val("us"): e.wait_ge(sem["us"], v))
    barrier()

    ar.reset()
    w_x = ar.take("w_x", NI * 1056, BF16)
    w_dt = ar.take("w_dt", ND * I, BF16)
    uin = [ar.take(f"uin{i}", NI * TT, BF16) for i in range(2)]
    dtin = ar.take("dtin", ND * TT, BF16)
    bcb = ar.take("bcb", 512, BF16)
    iotae_c = ar.take("iotae", 128, F32)
    eb_c = ar.take("eb", 2 * S * 128, BF16)
    iotae = iotae_c[0:32, :]
    eb = eb_c[0:32, :]
    brep_off = (ar.off + 63) & ~63
    brep = ar.take("brep", S * TT, BF16)
    crep = ar.take("crep", S * TT, BF16)
    wda = ar.take("wda", 4 * 512, F32)
    dub = ar.take("dub", 512, BF16)
    da4 = [ar.take(f"da4{i}", 4 * 512, F32) for i in range(2)]
    dbu4 = ar.take("dbu4", 4 * 512, BF16)
    h4 = ar.take("h4", 4 * 512, BF16)
    hc4 = [ar.take(f"hc4{i}", 4 * 512, BF16) for i in range(2)]
    y2st = ar.take("y2st", 4 * 512, BF16)
    wstage_b = ar.at(brep_off, "wstage_b", 2 * 2048, F32)

    def b_iota(e):
        e.iota(iotae, [[0, 128]], channel_multiplier=1,
               allow_small_or_imprecise_dtypes=True).then_inc(sem["gp"], 1)
    gp_eb = c.inc("gp", 1)
    emit("gpsimd", b_iota)

    def b_eb(e):
        W(e, "gp", gp_eb)
        SW(e, "ve")
        for s in range(2 * S):
            TK(e.tensor_scalar(eb[:, s * 128:(s + 1) * 128], iotae, float(s), None,
                              ALU.is_equal), "ve")
    emit("vector", b_eb)

    weight_load_cast(w_x, lambda kb: w_x_h.ap()[kb * 128:(kb + 1) * 128, :],
                     wstage_b, 2048, NI, 1056)
    weight_load_cast(w_dt, lambda kb: w_dt_h.ap()[kb * 128:(kb + 1) * 128, :],
                     wstage_b, 2048, ND, I)
    ve_wb = c.val("ve")
    ac_wb = c.val("ac")

    def p1b_init(e):
        SW(e, "ve")
        TK(e.memset(hc, 0.0), "ve")
    emit("vector", p1b_init)

    for ti in range(NT):
        ub = uin[ti % 2]
        first = (ti % (NT // BL)) == 0
        usem = f"ud{ti % 2}"

        def ld_u(e, ti=ti, ub=ub, usem=usem):
            W(e, "ve", c.v.get(f"b_y2_{ti - 2}_{NI - 1}", 0))
            e.dma_start(out=ub, in_=u_hbm.ap()[ti * 128:(ti + 1) * 128, :]
                        ).then_inc(sem[usem], 16)
        c.v[f"b_u_{ti}"] = c.inc(usem, 16)
        c.v[f"b_usem_{ti}"] = usem
        emit("sync", ld_u)

        if first and ti > 0:
            def rhc(e):
                SW(e, "ve")
                TK(e.memset(hc, 0.0), "ve")
            emit("vector", rhc)

        for nb in range(ND):
            pwk = pf[nb % 2]

            def pe_wx(e, nb=nb, pwk=pwk, ti=ti):
                W(e, "ac", ac_wb)
                W(e, c.v[f"b_usem_{ti}"], c.v[f"b_u_{ti}"])
                W(e, "ac", c.v.get(f"b_dtin_{ti}_{nb - 2}",
                                   c.v.get(f"b_sp_{ti - 1}_{NI - 1}", 0)))
                for kb in range(NI):
                    mm = e.matmul(pwk, w_x[:, kb * 1056 + nb * 128:kb * 1056 + (nb + 1) * 128],
                                  ub[:, kb * TT:(kb + 1) * TT],
                                  start=(kb == 0), stop=(kb == NI - 1))
                TK(mm, "pe")
            emit("tensor", pe_wx)
            c.v[f"b_wx_{ti}_{nb}"] = c.val("pe")

            def ac_dtin(e, nb=nb, pwk=pwk, ti=ti):
                W(e, "pe", c.v[f"b_wx_{ti}_{nb}"])
                SW(e, "ac")
                TK(e.copy(dtin[:, nb * TT:(nb + 1) * TT], pwk), "ac")
            emit("scalar", ac_dtin)
            c.v[f"b_dtin_{ti}_{nb}"] = c.val("ac")

        def pe_bc(e, ti=ti):
            W(e, "ve", c.v.get(f"b_bcev_{ti - 1}", 0))
            for kb in range(NI):
                mm = e.matmul(pf[2][0:32, :], w_x[:, kb * 1056 + 1024:kb * 1056 + 1056],
                              ub[:, kb * TT:(kb + 1) * TT],
                              start=(kb == 0), stop=(kb == NI - 1))
            TK(mm, "pe")
        emit("tensor", pe_bc)
        c.v[f"b_bc_{ti}"] = c.val("pe")

        def ve_bcev(e, ti=ti):
            W(e, "pe", c.v[f"b_bc_{ti}"])
            SW(e, "ve")
            TK(e.tensor_copy(bcb[0:32, 0:512], pf[2][0:32, :]), "ve")
        emit("vector", ve_bcev)
        c.v[f"b_bcev_{ti}"] = c.val("ve")

        for s in range(S):
            def pe_br(e, s=s, ti=ti):
                W(e, "ve", c.v[f"b_bcev_{ti}"])
                W(e, "ve", c.v.get(f"b_brev_{ti}_{s - 1}", 0))
                e.matmul(pf[2], eb[:, s * 128:(s + 1) * 128], bcb[0:32, 0:512],
                         start=True, stop=True)
                TK(e.matmul(pf[3], eb[:, (S + s) * 128:(S + s + 1) * 128],
                           bcb[0:32, 0:512], start=True, stop=True), "pe")
            emit("tensor", pe_br)
            c.v[f"b_br_{ti}_{s}"] = c.val("pe")

            def ve_brev(e, s=s, ti=ti):
                W(e, "pe", c.v[f"b_br_{ti}_{s}"])
                SW(e, "ve")
                TK(e.tensor_copy(brep[:, s * TT:(s + 1) * TT], pf[2]), "ve")
                TK(e.tensor_copy(crep[:, s * TT:(s + 1) * TT], pf[3]), "ve")
            emit("vector", ve_brev)
            c.v[f"b_brev_{ti}_{s}"] = c.val("ve")
        ve_brep_done = c.val("ve")

        for q in range(4):
            for ib in range(4 * q, 4 * q + 4):
                pwk = pf[ib % 2]

                def pe_wdt(e, ib=ib, pwk=pwk, ti=ti):
                    W(e, "ac", c.v[f"b_dtin_{ti}_{ND - 1}"])
                    W(e, "ac", c.v.get(f"b_sp_{ti}_{ib - 2}", 0))
                    for kb in range(ND):
                        mm = e.matmul(pwk, w_dt[:, kb * I + ib * 128:kb * I + (ib + 1) * 128],
                                      dtin[:, kb * TT:(kb + 1) * TT],
                                      start=(kb == 0), stop=(kb == ND - 1))
                    TK(mm, "pe")
                emit("tensor", pe_wdt)
                c.v[f"b_wdt_{ti}_{ib}"] = c.val("pe")

                def ac_spexp(e, ib=ib, pwk=pwk, ti=ti):
                    W(e, "pe", c.v[f"b_wdt_{ti}_{ib}"])
                    W(e, "ve", c.v.get(f"b_du_{ti}_{ib - 4}",
                                       c.v.get(f"b_du_{ti - 1}_{ib + 12}", 0)))
                    SW(e, "ac")
                    TK(e.activation(wda[:, (ib % 4) * 512:(ib % 4 + 1) * 512], pwk,
                                    AF.Exp, bias=bdt_sb[:, ib:ib + 1]), "ac")
                emit("scalar", ac_spexp)
                c.v[f"b_sp_{ti}_{ib}"] = c.val("ac")

            def ac_lnq(e, q=q, ti=ti):
                SW(e, "ac")
                TK(e.activation(wda, wda, AF.Ln, bias=1.0), "ac")
            emit("scalar", ac_lnq)
            c.v[f"b_lnq_{ti}_{q}"] = c.val("ac")

            for ib in range(4 * q, 4 * q + 4):
                dsl = wda[:, (ib % 4) * 512:(ib % 4 + 1) * 512]

                def ve_du(e, ib=ib, ti=ti, q=q, dsl=dsl):
                    W(e, "ac", c.v[f"b_lnq_{ti}_{q}"])
                    W(e, c.v[f"b_usem_{ti}"], c.v[f"b_u_{ti}"])
                    SW(e, "ve")
                    TK(e.tensor_tensor(dub, dsl, ub[:, ib * TT:(ib + 1) * TT],
                                       ALU.mult), "ve")
                emit("vector", ve_du)
                c.v[f"b_du_{ti}_{ib}"] = c.val("ve")

                for g in range(4):
                    dab = da4[g % 2]

                    def ac_da(e, ib=ib, g=g, dab=dab, ti=ti, dsl=dsl):
                        W(e, "ve", c.v.get(f"b_scan_{ti}_{ib}_{g - 2}",
                                           c.v.get(f"b_scan_{ti}_{ib - 1}_{g + 2}",
                                                   c.v.get(f"b_scan_{ti - 1}_{NI - 1}_{g + 2}", 0))))
                        SW(e, "ac")
                        for s in range(4):
                            TK(e.activation(dab[:, s * 512:(s + 1) * 512], dsl, AF.Exp,
                                            scale=a_sb[:, ib * S + g * 4 + s:ib * S + g * 4 + s + 1]),
                               "ac")
                    emit("scalar", ac_da)
                    c.v[f"b_da_{ti}_{ib}_{g}"] = c.val("ac")

                    def ve_scan(e, ib=ib, g=g, dab=dab, ti=ti):
                        SW(e, "ve")
                        bs = brep[:, (g * 4) * TT:(g * 4 + 4) * TT].rearrange(
                            "p (s t) -> p s t", s=4)
                        TK(e.tensor_tensor(dbu4.rearrange("p (s t) -> p s t", s=4),
                                           dub.unsqueeze(1).to_broadcast([128, 4, TT]),
                                           bs, ALU.mult), "ve")
                        SW(e, "ve")
                        W(e, "ac", c.v[f"b_da_{ti}_{ib}_{g}"])
                        for s in range(4):
                            TK(e.tensor_tensor_scan(
                                h4[:, s * 512:(s + 1) * 512],
                                dab[:, s * 512:(s + 1) * 512],
                                dbu4[:, s * 512:(s + 1) * 512],
                                hc[:, ib * S + g * 4 + s:ib * S + g * 4 + s + 1],
                                ALU.mult, ALU.add), "ve")
                        SW(e, "ve")
                        TK(e.tensor_copy(hc[:, ib * S + g * 4:ib * S + g * 4 + 4],
                                         h4.rearrange("p (s t) -> p s t", s=4)[:, :, 511]),
                           "ve")
                    emit("vector", ve_scan)
                    c.v[f"b_scan_{ti}_{ib}_{g}"] = c.val("ve")

                    hck = hc4[g % 2]

                    def ve_ym(e, ib=ib, g=g, hck=hck, ti=ti):
                        W(e, "pe", c.v.get(f"b_idmm_{ti}_{ib}_{g - 2}",
                                           c.v.get(f"b_idmm_{ti}_{ib - 1}_{g + 2}",
                                                   c.v.get(f"b_idmm_{ti - 1}_{NI - 1}_{g + 2}", 0))))
                        SW(e, "ve")
                        TK(e.tensor_tensor(hck, h4,
                                           crep[:, (g * 4) * TT:(g * 4 + 4) * TT],
                                           ALU.mult), "ve")
                    emit("vector", ve_ym)
                    c.v[f"b_ym_{ti}_{ib}_{g}"] = c.val("ve")

                    pyk = pf[4 + (ib % 2)]

                    def pe_id(e, ib=ib, g=g, hck=hck, pyk=pyk, ti=ti):
                        W(e, "ve", c.v[f"b_ym_{ti}_{ib}_{g}"])
                        if g == 0:
                            W(e, "ve", c.v.get(f"b_y2_{ti}_{ib - 2}",
                                               c.v.get(f"b_y2_{ti - 1}_{ib - 2 + NI}", 0)))
                        for s in range(4):
                            mm = e.matmul(pyk, idbf, hck[:, s * 512:(s + 1) * 512],
                                          start=(g == 0 and s == 0),
                                          stop=(g == 3 and s == 3))
                        TK(mm, "pe")
                    emit("tensor", pe_id)
                    c.v[f"b_idmm_{ti}_{ib}_{g}"] = c.val("pe")

                def ve_y2(e, ib=ib, ti=ti):
                    pyk = pf[4 + (ib % 2)]
                    W(e, "pe", c.v[f"b_idmm_{ti}_{ib}_{3}"])
                    W(e, "ys", c.v.get(f"b_ydma_{ti}_{ib // 4 - 1}",
                                       c.v.get(f"b_ydma_{ti - 1}_{3}", 0)))
                    W(e, c.v[f"b_usem_{ti}"], c.v[f"b_u_{ti}"])
                    SW(e, "ve")
                    TK(e.scalar_tensor_tensor(y2st[:, (ib % 4) * 512:(ib % 4 + 1) * 512],
                                              pyk, 1.0, ub[:, ib * TT:(ib + 1) * TT],
                                              ALU.mult, ALU.add), "ve")
                emit("vector", ve_y2)
                c.v[f"b_y2_{ti}_{ib}"] = c.val("ve")

                if ib % 4 == 3:
                    def st_y2(e, ib=ib, ti=ti):
                        W(e, "ve", c.v[f"b_y2_{ti}_{ib}"])
                        e.dma_start(
                            out=y2_hbm.ap()[ti * 128:(ti + 1) * 128,
                                            (ib - 3) * TT:(ib + 1) * TT],
                            in_=y2st).then_inc(sem["ys"], 16)
                    c.v[f"b_ydma_{ti}_{ib // 4}"] = c.inc("ys", 16)
                    emit("sync", st_y2)

    emit("sync", lambda e, v=c.val("ys"): e.wait_ge(sem["ys"], v))
    barrier()

    ar.reset()
    w_in_z = ar.take("w_in_z", ND * I, BF16)
    w_o = ar.take("w_o", NI * D, BF16)
    wq = ar.take("wq", ND * D, BF16)
    xb2 = [ar.take(f"x2_{i}", D, F32) for i in range(5)]
    xnbf2 = [ar.take(f"xnbf2_{i}", D, BF16) for i in range(2)]
    xnf2 = ar.take("xnf2", ND * TT, BF16)
    xqf = ar.take("xqf", ND * TT, BF16)
    y2b_off = (ar.off + 63) & ~63
    y2b = ar.take("y2b", NI * TT, BF16)
    yop = ar.take("yop", NI * TT, BF16)
    szp = [ar.take(f"szp{i}", 512, BF16) for i in range(2)]
    xq_t = [ar.take(f"xqt{i}", D, BF16) for i in range(2)]
    q1s = ar.take("q1s", D, F32)
    attn_sb = ar.take("attn_sb", 512, F32)
    o_f = ar.take("o_f", 512, F32)
    outsb = [ar.take(f"outsb{i}", D, F32) for i in range(4)]
    wstage2 = ar.at(y2b_off, "wstage2", 2 * 2048, F32)
    fwstage = ar.at(y2b_off + 16 * KB, "fwstage", D, F32)
    awstage = ar.at(y2b_off + 20 * KB, "awstage", D, F32)
    bsum = ar.take("bsum", 16, F32)

    weight_load_cast(w_in_z, lambda kb: w_in_h.ap()[kb * 128:(kb + 1) * 128, I:2 * I],
                     wstage2, 2048, ND, I)
    weight_load_cast(w_o, lambda kb: w_out_h.ap()[kb * 128:(kb + 1) * 128, :],
                     wstage2, 2048, NI, D)
    ve_w2 = c.val("ve")
    ac_w2 = c.val("ac")

    def g_dma(e):
        e.dma_start(out=smll[0:1, 0:128], in_=grun[:, 0:1]).then_inc(sem["md"], 16)
    md_g = c.inc("md", 16)
    emit("sync", g_dma)

    def g_fin(e, md_g=md_g):
        W(e, "md", md_g)
        SW(e, "ve")
        TK(e.tensor_reduce(tvec[0:1, 0:1], smll[0:1, 0:128], AX.X, ALU.max), "ve")
        SW(e, "ve")
        TK(e.reciprocal(tvec[0:1, 1:2], tvec[0:1, 0:1]), "ve")
        SW(e, "ve")
        TK(e.tensor_scalar(tvec[0:1, 1:2], tvec[0:1, 1:2], QB, None, ALU.mult), "ve")
    emit("vector", g_fin)

    for kb in range(ND):
        def fw_ld(e, kb=kb):
            W(e, "ve", c.v.get(f"c_abs_{kb - 1}", 0))
            e.dma_start(out=fwstage, in_=fw_h.ap()[kb * 128:(kb + 1) * 128, :]
                        ).then_inc(sem["fs"], 16)
        wd_v = c.inc("fs", 16)
        emit("sync", fw_ld)

        def ve_abs(e, kb=kb, wd_v=wd_v):
            W(e, "fs", wd_v)
            SW(e, "ve")
            TK(e.scalar_tensor_tensor(awstage, fwstage, -1.0, fwstage,
                                     ALU.mult, ALU.max), "ve")
            SW(e, "ve")
            TK(e.tensor_reduce(bsum[:, kb:kb + 1], awstage, AX.X, ALU.add), "ve")
        emit("vector", ve_abs)
        c.v[f"c_abs_{kb}"] = c.val("ve")

    def ve_bsum(e):
        SW(e, "ve")
        TK(e.reduce_sum(bsum[:, 8:9], bsum[:, 0:8], AX.X), "ve")
    emit("vector", ve_bsum)
    ve_bs = c.val("ve")

    def pe_bsum(e, ve_bs=ve_bs):
        W(e, "ve", ve_bs)
        TK(e.matmul(pf[2][0:1, 0:1], onesc, bsum[:, 8:9], start=True, stop=True),
          "pe")
    emit("tensor", pe_bsum)
    pe_bs = c.val("pe")

    def ve_beta(e, pe_bs=pe_bs):
        W(e, "pe", pe_bs)
        SW(e, "ve")
        TK(e.tensor_scalar(tvec[0:1, 2:3], pf[2][0:1, 0:1], 1.0 / (D * D), None,
                          ALU.mult), "ve")
        SW(e, "ve")
        TK(e.tensor_scalar(tvec[0:1, 3:4], tvec[0:1, 2:3], W_EPS, 0.5,
                          ALU.add, ALU.mult), "ve")
        TK(e.tensor_tensor(tvec[0:1, 4:5], tvec[0:1, 2:3], tvec[0:1, 0:1],
                          ALU.mult), "ve")
        SW(e, "ve")
        TK(e.tensor_scalar(tvec[0:1, 4:5], tvec[0:1, 4:5], 1.0 / QB, None,
                          ALU.mult), "ve")
    emit("vector", ve_beta)
    ve_beta_v = c.val("ve")

    def pe_bcastv(e, ve_beta_v=ve_beta_v):
        W(e, "ve", ve_beta_v)
        TK(e.matmul(pf[2][:, 0:5], ones1, tvec[0:1, 0:5], start=True, stop=True),
          "pe")
    emit("tensor", pe_bcastv)
    pe_bv = c.val("pe")

    def ve_vecs(e, pe_bv=pe_bv):
        W(e, "pe", pe_bv)
        SW(e, "ve")
        TK(e.tensor_copy(vec4[:, 0:1], pf[2][:, 1:2]), "ve")
        TK(e.tensor_copy(vec4[:, 1:2], pf[2][:, 3:4]), "ve")
        TK(e.tensor_copy(vec4[:, 2:3], pf[2][:, 4:5]), "ve")
        TK(e.tensor_copy(vec4[:, 3:4], pf[2][:, 2:3]), "ve")
        SW(e, "ve")
        TK(e.tensor_scalar(g_rep, g_rep, vec4[:, 0:1], None, ALU.mult), "ve")
        TK(e.tensor_scalar(b_rep, b_rep, vec4[:, 0:1], None, ALU.mult), "ve")
    emit("vector", ve_vecs)
    ve_vecs_v = c.val("ve")

    for kb in range(ND):
        def fw_ld2(e, kb=kb):
            W(e, "ve", c.v.get(f"c_wq_{kb - 1}", ve_vecs_v))
            W(e, "ac", c.v.get(f"c_sgn_{kb - 1}", 0))
            e.dma_start(out=fwstage, in_=fw_h.ap()[kb * 128:(kb + 1) * 128, :]
                        ).then_inc(sem["fs"], 16)
        wd_v = c.inc("fs", 16)
        emit("sync", fw_ld2)

        def ac_sgn(e, kb=kb, wd_v=wd_v):
            W(e, "fs", wd_v)
            SW(e, "ac")
            TK(e.activation(awstage, fwstage, AF.Sign), "ac")
        emit("scalar", ac_sgn)
        c.v[f"c_sgn_{kb}"] = c.val("ac")

        def ve_wq(e, kb=kb, wd_v=wd_v):
            W(e, "fs", wd_v)
            W(e, "ve", ve_vecs_v)
            W(e, "ac", c.v[f"c_sgn_{kb}"])
            SW(e, "ve")
            TK(e.scalar_tensor_tensor(q1s, fwstage, -1.0, fwstage, ALU.mult,
                                     ALU.max), "ve")
            SW(e, "ve")
            TK(e.tensor_scalar(q1s, q1s, vec4[:, 1:2], None, ALU.is_gt), "ve")
            SW(e, "ve")
            TK(e.tensor_tensor(wq[:, kb * D:(kb + 1) * D], q1s, awstage, ALU.mult),
              "ve")
        emit("vector", ve_wq)
        c.v[f"c_wq_{kb}"] = c.val("ve")

    for ti in range(NT):
        b = ti // (NT // BL)
        t0 = (ti % (NT // BL)) * TT

        for tb in range(NTB):
            xb = xb2[(ti * NTB + tb) % 5]
            xs = f"xd{(ti * NTB + tb) % 5}"

            def ld(e, b=b, t0=t0, tb=tb, xb=xb, ti=ti, xs=xs):
                W(e, "ve", c.v.get(f"c_oadd_{ti - 1}", 0))
                e.dma_start(out=xb, in_=x_h.ap()[b, t0 + tb * 128:t0 + (tb + 1) * 128, :]
                            ).then_inc(sem[xs], 16)
            c.v[f"c_x_{ti}_{tb}"] = c.inc(xs, 16)
            c.v[f"c_xsem_{ti}_{tb}"] = xs
            emit("sync", ld)

        def ld_y2(e, ti=ti):
            W(e, "pe", c.v.get(f"c_attn_{ti - 1}_{ND - 1}", 0))
            W(e, "ve", c.v[f"c_wq_{ND - 1}"])
            e.dma_start(out=y2b, in_=y2_hbm.ap()[ti * 128:(ti + 1) * 128, :]
                        ).then_inc(sem["yl"], 16)
        c.v[f"c_y2_{ti}"] = c.inc("yl", 16)
        emit("sync", ld_y2)

        for tb in range(NTB):
            sti = ti * NTB + tb
            xb = xb2[sti % 5]
            xn = xnbf2[tb % 2]

            def ve_xn2(e, xb=xb, xn=xn, sti=sti, ti=ti, tb=tb):
                W(e, c.v[f"c_xsem_{ti}_{tb}"], c.v[f"c_x_{ti}_{tb}"])
                W(e, "pe", c.v.get(f"c_tp_{ti}_{tb - 2}", 0))
                SW(e, "ve")
                TK(e.tensor_scalar(xn, xb, stash[:, sti:sti + 1],
                                  stash[:, NST + sti:NST + sti + 1],
                                  ALU.add, ALU.mult), "ve")
            emit("vector", ve_xn2)
            c.v[f"c_xn_{ti}_{tb}"] = c.val("ve")

            pbk = pb[tb % 2]

            def pe_tp(e, xn=xn, pbk=pbk, ti=ti, tb=tb):
                W(e, "ve", c.v[f"c_xn_{ti}_{tb}"])
                W(e, "ve", c.v.get(f"c_tpev_{ti}_{tb - 2}", 0))
                for db in range(ND):
                    TK(e.transpose(pbk[:, db * 128:(db + 1) * 128],
                                  xn[:, db * 128:(db + 1) * 128], idbf), "pe")
            emit("tensor", pe_tp)
            c.v[f"c_tp_{ti}_{tb}"] = c.val("pe")

            def ve_tpev(e, pbk=pbk, tb=tb, ti=ti):
                W(e, "pe", c.v[f"c_tp_{ti}_{tb}"])
                SW(e, "ve")
                src = pbk.rearrange("p (d t) -> p d t", d=ND)
                dst = xnf2.rearrange("p (d t) -> p d t", d=ND)[:, :, tb * 128:(tb + 1) * 128]
                TK(e.tensor_copy(dst, src), "ve")
            emit("vector", ve_tpev)
            c.v[f"c_tpev_{ti}_{tb}"] = c.val("ve")

        for ib in range(NI):
            pwk = pf[ib % 2]

            def pe_z(e, ib=ib, pwk=pwk, ti=ti):
                W(e, "ac", ac_w2)
                W(e, "ve", c.v[f"c_tpev_{ti}_{NTB - 1}"])
                W(e, "ac", c.v.get(f"c_sz_{ti}_{ib - 2}",
                                   c.v.get(f"c_attnev_{ti - 1}_{ND - 1}", 0)))
                for kb in range(ND):
                    mm = e.matmul(pwk, w_in_z[:, (kb * NI + ib) * 128:(kb * NI + ib + 1) * 128],
                                  xnf2[:, kb * TT:(kb + 1) * TT],
                                  start=(kb == 0), stop=(kb == ND - 1))
                TK(mm, "pe")
            emit("tensor", pe_z)
            c.v[f"c_z_{ti}_{ib}"] = c.val("pe")

            def ac_sz(e, ib=ib, pwk=pwk, ti=ti):
                W(e, "pe", c.v[f"c_z_{ti}_{ib}"])
                W(e, "ve", c.v.get(f"c_yopk_{ti}_{ib - 2}", 0))
                SW(e, "ac")
                TK(e.activation(szp[ib % 2], pwk, AF.Silu), "ac")
            emit("scalar", ac_sz)
            c.v[f"c_sz_{ti}_{ib}"] = c.val("ac")

            def ve_yop(e, ib=ib, ti=ti):
                W(e, "ac", c.v[f"c_sz_{ti}_{ib}"])
                W(e, "yl", c.v[f"c_y2_{ti}"])
                SW(e, "ve")
                TK(e.tensor_tensor(yop[:, ib * TT:(ib + 1) * TT],
                                  y2b[:, ib * TT:(ib + 1) * TT], szp[ib % 2],
                                  ALU.mult), "ve")
            emit("vector", ve_yop)
            c.v[f"c_yopk_{ti}_{ib}"] = c.val("ve")
        c.v[f"c_yop_{ti}"] = c.val("ve")

        for tb in range(NTB):
            sti = ti * NTB + tb
            xn = xnbf2[tb % 2]
            xqt = xq_t[tb % 2]

            def ve_xq(e, xn=xn, sti=sti, xqt=xqt, ti=ti, tb=tb):
                xb = xb2[sti % 5]
                W(e, "pe", c.v.get(f"c_tpq_{ti}_{tb - 2}", 0))
                SW(e, "ve")
                TK(e.tensor_scalar(xn, xb, stash[:, sti:sti + 1],
                                  stash[:, NST + sti:NST + sti + 1],
                                  ALU.add, ALU.mult), "ve")
                SW(e, "ve")
                TK(e.scalar_tensor_tensor(q1s, xn,
                                         stash[:, 2 * NST + sti:2 * NST + sti + 1],
                                         g_rep, ALU.mult, ALU.mult), "ve")
                SW(e, "ve")
                TK(e.tensor_add(q1s, q1s, b_rep), "ve")
                SW(e, "ve")
                TK(e.tensor_scalar(xqt, q1s, CLIPV, -CLIPV, ALU.min, ALU.max), "ve")
            emit("vector", ve_xq)
            c.v[f"c_xq_{ti}_{tb}"] = c.val("ve")

            pbk = pb[tb % 2]

            def pe_tpq(e, xqt=xqt, pbk=pbk, ti=ti, tb=tb):
                W(e, "ve", c.v[f"c_xq_{ti}_{tb}"])
                W(e, "ve", c.v.get(f"c_tpqev_{ti}_{tb - 2}", 0))
                for db in range(ND):
                    TK(e.transpose(pbk[:, db * 128:(db + 1) * 128],
                                  xqt[:, db * 128:(db + 1) * 128], idbf), "pe")
            emit("tensor", pe_tpq)
            c.v[f"c_tpq_{ti}_{tb}"] = c.val("pe")

            def ve_tpqev(e, pbk=pbk, tb=tb, ti=ti):
                W(e, "pe", c.v[f"c_tpq_{ti}_{tb}"])
                SW(e, "ve")
                src = pbk.rearrange("p (d t) -> p d t", d=ND)
                dst = xqf.rearrange("p (d t) -> p d t", d=ND)[:, :, tb * 128:(tb + 1) * 128]
                TK(e.tensor_copy(dst, src), "ve")
            emit("vector", ve_tpqev)
            c.v[f"c_tpqev_{ti}_{tb}"] = c.val("ve")

        for db in range(ND):
            pa = pf[db % 2]
            pq = pf[2 + db % 2]
            ptr = pf[4 + db % 2]

            def pe_attn(e, db=db, pa=pa, ti=ti):
                W(e, "ve", c.v[f"c_yop_{ti}"])
                W(e, "ac", c.v.get(f"c_attnev_{ti}_{db - 2}",
                                   c.v.get(f"c_sz_{ti}_{NI - 1}", 0)))
                for kb in range(NI):
                    mm = e.matmul(pa, w_o[:, kb * D + db * 128:kb * D + (db + 1) * 128],
                                  yop[:, kb * TT:(kb + 1) * TT],
                                  start=(kb == 0), stop=(kb == NI - 1))
                TK(mm, "pe")
            emit("tensor", pe_attn)
            c.v[f"c_attn_{ti}_{db}"] = c.val("pe")

            def pe_ffn(e, db=db, pq=pq, ti=ti):
                W(e, "ve", c.v[f"c_tpqev_{ti}_{NTB - 1}"])
                W(e, "ve", c.v.get(f"c_of_{ti}_{db - 2}",
                                   c.v.get(f"c_of_{ti - 1}_{ND - 1}", 0)))
                for kb in range(ND):
                    mm = e.matmul(pq, wq[:, kb * D + db * 128:kb * D + (db + 1) * 128],
                                  xqf[:, kb * TT:(kb + 1) * TT],
                                  start=(kb == 0), stop=(kb == ND - 1))
                TK(mm, "pe")
            emit("tensor", pe_ffn)
            c.v[f"c_ffn_{ti}_{db}"] = c.val("pe")

            def ac_attnev(e, pa=pa, ti=ti, db=db):
                W(e, "pe", c.v[f"c_attn_{ti}_{db}"])
                W(e, "ve", c.v.get(f"c_of_{ti}_{db - 1}", 0))
                SW(e, "ac")
                TK(e.copy(attn_sb, pa), "ac")
            emit("scalar", ac_attnev)
            c.v[f"c_attnev_{ti}_{db}"] = c.val("ac")

            def ve_of(e, pq=pq, ti=ti, db=db):
                W(e, "pe", c.v[f"c_ffn_{ti}_{db}"])
                W(e, "ac", c.v[f"c_attnev_{ti}_{db}"])
                W(e, "pe", c.v.get(f"c_tro_{ti}_{db - 1}", 0))
                SW(e, "ve")
                TK(e.scalar_tensor_tensor(o_f, pq, vec4[:, 2:3], attn_sb,
                                         ALU.mult, ALU.add), "ve")
            emit("vector", ve_of)
            c.v[f"c_of_{ti}_{db}"] = c.val("ve")

            def pe_tro(e, db=db, ptr=ptr, ti=ti):
                W(e, "ve", c.v[f"c_of_{ti}_{db}"])
                W(e, "ve", c.v.get(f"c_oaddp_{ti}_{db - 2}",
                                   c.v.get(f"c_oaddp_{ti - 1}_{ND - 1}", 0)))
                for tb in range(NTB):
                    TK(e.transpose(ptr[:, tb * 128:(tb + 1) * 128],
                                  o_f[:, tb * 128:(tb + 1) * 128], idf), "pe")
            emit("tensor", pe_tro)
            c.v[f"c_tro_{ti}_{db}"] = c.val("pe")

            def ve_outadd(e, db=db, ptr=ptr, ti=ti):
                W(e, "pe", c.v[f"c_tro_{ti}_{db}"])
                if db == 0:
                    for q in range(NTB):
                        W(e, f"od{q}", c.v.get(f"c_odma_{ti - 1}_{q}", 0))
                SW(e, "ve")
                for tb in range(NTB):
                    xb = xb2[(ti * NTB + tb) % 5]
                    TK(e.tensor_tensor(outsb[tb][:, db * 128:(db + 1) * 128],
                                      xb[:, db * 128:(db + 1) * 128],
                                      ptr[:, tb * 128:(tb + 1) * 128], ALU.add),
                      "ve")
            emit("vector", ve_outadd)
            c.v[f"c_oaddp_{ti}_{db}"] = c.val("ve")
        c.v[f"c_oadd_{ti}"] = c.val("ve")

        for tb in range(NTB):
            def st_out(e, tb=tb, b=b, t0=t0, ti=ti):
                W(e, "ve", c.v[f"c_oadd_{ti}"])
                e.dma_start(out=out_h.ap()[b, t0 + tb * 128:t0 + (tb + 1) * 128, :],
                            in_=outsb[tb]).then_inc(sem[f"od{tb}"], 16)
            c.v[f"c_odma_{ti}_{tb}"] = c.inc(f"od{tb}", 16)
            emit("sync", st_out)

    def fin(e):
        for q in range(NTB):
            W(e, f"od{q}", c.val(f"od{q}"))
    emit("sync", fin)

    es.close()
    return nc


def kernel(**inputs):
    x = np.ascontiguousarray(np.asarray(inputs["L0_attn_inputs"], np.float32))
    nc = build_program()
    common = dict(
        w_in=np.ascontiguousarray(inputs["L0_attn_W_in"], np.float32),
        conv_w=np.ascontiguousarray(inputs["L0_attn_conv_w"], np.float32),
        conv_b=np.ascontiguousarray(inputs["L0_attn_conv_b"], np.float32),
        w_x=np.ascontiguousarray(inputs["L0_attn_W_x"], np.float32),
        w_dt=np.ascontiguousarray(inputs["L0_attn_W_dt"], np.float32),
        b_dt=np.ascontiguousarray(inputs["L0_attn_b_dt"], np.float32),
        a_log=np.ascontiguousarray(inputs["L0_attn_A_log"], np.float32),
        w_out=np.ascontiguousarray(inputs["L0_attn_W_out"], np.float32),
        fw=np.ascontiguousarray(inputs["L0_ffn_weight"], np.float32),
        lng=np.ascontiguousarray(inputs["L0_ffn_ln_gamma"], np.float32),
        lnb=np.ascontiguousarray(inputs["L0_ffn_ln_beta"], np.float32),
    )
    in_maps = [dict(common, x=np.ascontiguousarray(x[2 * cc:2 * cc + 2]))
               for cc in range(NCORES)]
    res = run_bass_kernel_spmd(nc, in_maps, core_ids=list(range(NCORES)))
    return np.concatenate([r["out"] for r in res.results], axis=0)


if __name__ == "__main__":
    build_program()
    print("build OK")


# version 11
# speedup vs baseline: 1.0275x; 1.0275x over previous
import contextlib
import numpy as np
import concourse.bass as bass
import concourse.mybir as mybir
from concourse.bass_utils import run_bass_kernel_spmd

F32 = mybir.dt.float32
BF16 = mybir.dt.bfloat16
U8 = mybir.dt.uint8
AF = mybir.ActivationFunctionType
ALU = mybir.AluOpType
AX = mybir.AxisListType

NCORES = 8
B, T, D = 16, 2048, 1024
BL = B // NCORES
I = 2048
S = 16
TT = 512
NT = BL * T // TT
NTB = TT // 128
ND = D // 128
NI = I // 128
NST = NT * NTB
QB = 128.0
CLIPV = float(np.float32(QB - 1e-5))
LN1_EPS = 1e-5
LN2_EPS = 1e-3
W_EPS = 1e-5
KB = 1024


class Ctr:
    def __init__(self):
        self.v = {}

    def inc(self, name, n):
        self.v[name] = self.v.get(name, 0) + n
        return self.v[name]

    def val(self, name):
        return self.v.get(name, 0)


class Carver:
    def __init__(self, arena_ap, nbytes):
        self.ap = arena_ap
        self.n = nbytes
        self.off = 0

    def reset(self):
        self.off = 0

    def take(self, name, cols, dtype):
        esz = 2 if dtype == BF16 else 4
        nb = cols * esz
        off = (self.off + 63) & ~63
        assert off + nb <= self.n, f"arena overflow at {name}: {off}+{nb}>{self.n}"
        self.off = off + nb
        v = self.ap[:, off:off + nb].bitcast(dtype)
        assert v.shape == (128, cols), (name, v.shape)
        return v

    def at(self, off, name, cols, dtype):
        esz = 2 if dtype == BF16 else 4
        nb = cols * esz
        assert off % 64 == 0 and off + nb <= self.n, name
        v = self.ap[:, off:off + nb].bitcast(dtype)
        assert v.shape == (128, cols), (name, v.shape)
        return v


def build_program():
    nc = bass.Bass("TRN2", target_bir_lowering=False, debug=False)

    x_h = nc.dram_tensor("x", [BL, T, D], F32, kind="ExternalInput")
    w_in_h = nc.dram_tensor("w_in", [D, 2 * I], F32, kind="ExternalInput")
    conv_w_h = nc.dram_tensor("conv_w", [4, 1, I], F32, kind="ExternalInput")
    conv_b_h = nc.dram_tensor("conv_b", [I], F32, kind="ExternalInput")
    w_x_h = nc.dram_tensor("w_x", [I, D + 2 * S], F32, kind="ExternalInput")
    w_dt_h = nc.dram_tensor("w_dt", [D, I], F32, kind="ExternalInput")
    b_dt_h = nc.dram_tensor("b_dt", [I], F32, kind="ExternalInput")
    a_log_h = nc.dram_tensor("a_log", [I, S], F32, kind="ExternalInput")
    w_out_h = nc.dram_tensor("w_out", [I, D], F32, kind="ExternalInput")
    fw_h = nc.dram_tensor("fw", [D, D], F32, kind="ExternalInput")
    lng_h = nc.dram_tensor("lng", [D], F32, kind="ExternalInput")
    lnb_h = nc.dram_tensor("lnb", [D], F32, kind="ExternalInput")
    out_h = nc.dram_tensor("out", [BL, T, D], F32, kind="ExternalOutput")

    u_hbm = nc.dram_tensor("u_scr", [NT * 128, NI * TT], BF16)
    y2_hbm = nc.dram_tensor("y2_scr", [NT * 128, NI * TT], BF16)

    idf = nc.alloc_sbuf_tensor("idf", [128, 128], F32).ap()
    idbf = nc.alloc_sbuf_tensor("idbf", [128, 128], BF16).ap()
    ones1 = nc.alloc_sbuf_tensor("ones1", [1, 128], F32).ap()
    onesc = nc.alloc_sbuf_tensor("onesc", [128, 1], F32).ap()
    g_rep = nc.alloc_sbuf_tensor("g_rep", [128, D], F32).ap()
    b_rep = nc.alloc_sbuf_tensor("b_rep", [128, D], F32).ap()
    stash = nc.alloc_sbuf_tensor("stash", [128, 3 * NST], F32).ap()
    hc = nc.alloc_sbuf_tensor("hc", [128, NI * S], F32).ap()
    a_sb = nc.alloc_sbuf_tensor("a_sb", [128, NI * S], F32).ap()
    cw_sb = nc.alloc_sbuf_tensor("cw_sb", [128, 4 * NI], F32).ap()
    cb_sb = nc.alloc_sbuf_tensor("cb_sb", [128, NI], F32).ap()
    bdt_sb = nc.alloc_sbuf_tensor("bdt_sb", [128, NI], F32).ap()
    tails = nc.alloc_sbuf_tensor("tails", [128, 3 * NI], F32).ap()
    grun = nc.alloc_sbuf_tensor("grun", [128, 2], F32).ap()
    smll = nc.alloc_sbuf_tensor("smll", [1, 128], F32).ap()
    vec4 = nc.alloc_sbuf_tensor("vec4", [128, 4], F32).ap()
    tvec = nc.alloc_sbuf_tensor("tvec", [128, 24], F32).ap()
    cbank = nc.alloc_sbuf_tensor("cbank", [128, 4], F32).ap()
    fbv = nc.alloc_sbuf_tensor("fbv", [1, 4], F32).ap()
    fba = nc.alloc_sbuf_tensor("fba", [1, 4], F32).ap()

    arena_bytes = (nc.sbuf_bytes_remaining - 192) & ~63
    arena_t = nc.alloc_sbuf_tensor("arena", [128, arena_bytes], U8)
    ar = Carver(arena_t.ap(), arena_bytes)

    pf = [nc.alloc_psum_tensor(f"pf{i}", [128, 512], F32).ap() for i in range(6)]
    pb = [nc.alloc_psum_tensor(f"pb{i}", [128, 1024], BF16).ap() for i in range(2)]

    es = contextlib.ExitStack()
    sem = {}
    for name in (["md", "ve", "ac", "pe", "gp", "ph", "us", "ys", "yl", "fs"]
                 + [f"xd{i}" for i in range(5)] + [f"ws{i}" for i in range(2)]
                 + [f"ud{i}" for i in range(2)] + [f"od{i}" for i in range(4)]):
        sem[name] = es.enter_context(nc.semaphore(name))
    block = es.enter_context(nc.Block())
    c = Ctr()

    def emit(engname, fn):
        if engname == "vector":
            def wrapped(e, fn=fn):
                fn(e)
                e.wait_ge(sem["ve"], c.val("ve"))
                TK(e.memset(fbv[0:1, 0:1], 0.0), "ve")
            getattr(block, engname)(wrapped)
        else:
            getattr(block, engname)(fn)

    def W(e, name, val):
        assert val <= c.val(name), f"wait on future {name}: {val} > {c.val(name)}"
        if val > 0:
            e.wait_ge(sem[name], val)

    def TK(instr, s):
        instr.then_inc(sem[s], 1)
        return c.inc(s, 1)

    def SW(e, s):
        e.wait_ge(sem[s], c.val(s))



    def weight_load_cast(dst_bf, src_ap_fn, stage_f32, half_cols, kblks, row_cols):
        assert row_cols <= half_cols
        for kb in range(kblks):
            h0 = (kb % 2) * half_cols
            ws = f"ws{kb % 2}"

            def ld(e, kb=kb, h0=h0, ws=ws):
                W(e, "ac", c.val("ac"))
                e.dma_start(out=stage_f32[:, h0:h0 + row_cols],
                            in_=src_ap_fn(kb)).then_inc(sem[ws], 16)
            wd_after = c.inc(ws, 16)
            emit("sync", ld)

            def cast(e, kb=kb, h0=h0, wd_after=wd_after, ws=ws):
                W(e, ws, wd_after)
                SW(e, "ac")
                if kb == 0:
                    W(e, "ve", c.val("ve"))
                    W(e, "pe", c.val("pe"))
                TK(e.copy(dst_bf[:, kb * row_cols:(kb + 1) * row_cols],
                          stage_f32[:, h0:h0 + row_cols]), "ac")
            emit("scalar", cast)

    def barrier():
        target = c.val("ph") + 5
        for eng in ["sync", "vector", "scalar", "tensor", "gpsimd"]:
            emit(eng, lambda e: e.drain().then_inc(sem["ph"], 1))
        c.inc("ph", 5)
        for eng in ["sync", "vector", "scalar", "tensor", "gpsimd"]:
            emit(eng, lambda e, t=target: e.wait_ge(sem["ph"], t))

    iotat = ar.take("iotat", 128, F32)
    lngr_c = ar.take("lngr", D, F32)
    lnbr_c = ar.take("lnbr", D, F32)
    lngr = lngr_c[0:1, :]
    lnbr = lnbr_c[0:1, :]

    def p_dma(e):
        for ib in range(NI):
            e.dma_start(out=a_sb[:, ib * S:(ib + 1) * S],
                        in_=a_log_h.ap()[ib * 128:(ib + 1) * 128, :]).then_inc(sem["md"], 16)
            e.dma_start(out=cb_sb[:, ib:ib + 1],
                        in_=conv_b_h.ap()[ib * 128:(ib + 1) * 128].unsqueeze(1)).then_inc(sem["md"], 16)
            e.dma_start(out=bdt_sb[:, ib:ib + 1],
                        in_=b_dt_h.ap()[ib * 128:(ib + 1) * 128].unsqueeze(1)).then_inc(sem["md"], 16)
            for k in range(4):
                e.dma_start(out=cw_sb[:, k * NI + ib:k * NI + ib + 1],
                            in_=conv_w_h.ap()[k, 0, ib * 128:(ib + 1) * 128].unsqueeze(1)).then_inc(sem["md"], 16)
        e.dma_start(out=lngr, in_=lng_h.ap().unsqueeze(0)).then_inc(sem["md"], 16)
        e.dma_start(out=lnbr, in_=lnb_h.ap().unsqueeze(0)).then_inc(sem["md"], 16)
    n_pre = NI * 7 + 2
    md_pre = c.inc("md", 16 * n_pre)
    emit("sync", p_dma)

    def p_iota(e):
        e.iota(iotat, [[1, 128]], channel_multiplier=-1,
               allow_small_or_imprecise_dtypes=True).then_inc(sem["gp"], 1)
    gp_iota = c.inc("gp", 1)
    emit("gpsimd", p_iota)

    def p_ids(e):
        W(e, "gp", gp_iota)
        TK(e.tensor_scalar(idf, iotat, 0.0, None, ALU.is_equal), "ve")
        SW(e, "ve")
        TK(e.tensor_copy(idbf, idf), "ve")
        TK(e.memset(ones1, 1.0), "ve")
        TK(e.memset(onesc, 1.0), "ve")
        TK(e.memset(grun, 0.0), "ve")
        TK(e.memset(cbank[:, 0:1], LN1_EPS), "ve")
        TK(e.memset(cbank[:, 1:2], LN2_EPS), "ve")
        TK(e.memset(cbank[:, 2:3], -0.5), "ve")
        TK(e.memset(cbank[:, 3:4], 1.0 / D), "ve")
    emit("vector", p_ids)
    ve_ids = c.val("ve")

    def p_aexp(e):
        W(e, "md", md_pre)
        TK(e.activation(a_sb, a_sb, AF.Exp), "ac")
    emit("scalar", p_aexp)
    ac_aexp = c.val("ac")

    def p_aneg(e):
        W(e, "ac", ac_aexp)
        SW(e, "ve")
        TK(e.tensor_scalar(a_sb, a_sb, -1.0, None, ALU.mult), "ve")
    emit("vector", p_aneg)

    def p_grep_mm(e):
        W(e, "md", md_pre)
        W(e, "ve", ve_ids)
        e.matmul(pf[2], ones1, lngr[0:1, 0:512], start=True, stop=True)
        e.matmul(pf[3], ones1, lngr[0:1, 512:1024], start=True, stop=True)
        e.matmul(pf[4], ones1, lnbr[0:1, 0:512], start=True, stop=True)
        TK(e.matmul(pf[5], ones1, lnbr[0:1, 512:1024], start=True, stop=True), "pe")
    emit("tensor", p_grep_mm)
    pe_grep = c.val("pe")

    def p_grep_ev(e):
        W(e, "pe", pe_grep)
        SW(e, "ve")
        TK(e.tensor_copy(g_rep[:, 0:512], pf[2]), "ve")
        TK(e.tensor_copy(g_rep[:, 512:1024], pf[3]), "ve")
        TK(e.tensor_copy(b_rep[:, 0:512], pf[4]), "ve")
        TK(e.tensor_copy(b_rep[:, 512:1024], pf[5]), "ve")
    emit("vector", p_grep_ev)

    ar.reset()
    w_in_x = ar.take("w_in_x", ND * I, BF16)
    xbuf = [ar.take(f"x{i}", D, F32) for i in range(5)]
    xnbf = [ar.take(f"xnbf{i}", D, BF16) for i in range(4)]
    xnsq = ar.take("xnsq", D, BF16)
    g1s = ar.take("g1s", D, F32)
    g2s = ar.take("g2s", D, BF16)
    xnf = ar.take("xnf", ND * TT, BF16)
    ubf_off = (ar.off + 63) & ~63
    ubf = ar.take("ubf", NI * TT, BF16)
    cacc = [ar.take(f"cacc{i}", 512, F32) for i in range(2)]
    wstage = ar.at(ubf_off, "wstage", 2 * 2048, F32)

    weight_load_cast(w_in_x, lambda kb: w_in_h.ap()[kb * 128:(kb + 1) * 128, 0:I],
                     wstage, 2048, ND, I)
    ac_wa = c.val("ac")

    for ti in range(NT):
        b = ti // (NT // BL)
        t0 = (ti % (NT // BL)) * TT
        first = (ti % (NT // BL)) == 0

        for tb in range(NTB):
            sti = ti * NTB + tb
            xb = xbuf[sti % 5]
            xs = f"xd{sti % 5}"

            def ld(e, b=b, t0=t0, tb=tb, xb=xb, sti=sti, xs=xs):
                W(e, "ve", c.v.get(f"a_xn_{sti - 5}", 0))
                e.dma_start(out=xb, in_=x_h.ap()[b, t0 + tb * 128:t0 + (tb + 1) * 128, :]
                            ).then_inc(sem[xs], 16)
            c.v[f"a_x_{sti}"] = c.inc(xs, 16)
            c.v[f"a_xsem_{sti}"] = xs
            emit("sync", ld)

        for tb in range(NTB):
            sti = ti * NTB + tb
            xb = xbuf[sti % 5]
            tvb = tb * 6

            def ve_ln(e, xb=xb, sti=sti, tvb=tvb, tb=tb):
                W(e, c.v[f"a_xsem_{sti}"], c.v[f"a_x_{sti}"])
                if tb == 0:
                    W(e, "ac", c.v.get(f"a_r2_{sti - 1}", 0))
                SW(e, "ve")
                T = tvec
                TK(e.scalar_tensor_tensor(xnsq, xb, 1.0, xb, ALU.mult, ALU.mult,
                                          accum_out=T[:, tvb:tvb + 1]), "ve")
                TK(e.reduce_sum(T[:, tvb + 1:tvb + 2], xb, AX.X), "ve")
                SW(e, "ve")
                TK(e.tensor_scalar(stash[:, sti:sti + 1], T[:, tvb + 1:tvb + 2],
                                   -1.0 / D, None, ALU.mult), "ve")
                TK(e.tensor_scalar(T[:, tvb + 2:tvb + 3], T[:, tvb:tvb + 1],
                                   1.0 / D, None, ALU.mult), "ve")
                SW(e, "ve")
                TK(e.scalar_tensor_tensor(T[:, tvb + 3:tvb + 4], stash[:, sti:sti + 1],
                                          1.0, stash[:, sti:sti + 1], ALU.mult,
                                          ALU.mult), "ve")
                SW(e, "ve")
                TK(e.tensor_sub(T[:, tvb + 2:tvb + 3], T[:, tvb + 2:tvb + 3],
                                T[:, tvb + 3:tvb + 4]), "ve")
            emit("vector", ve_ln)
            c.v[f"a_var_{sti}"] = c.val("ve")

        def ac_ln1(e, ti=ti):
            for tb in range(NTB):
                sti = ti * NTB + tb
                tvb = tb * 6
                W(e, "ve", c.v[f"a_var_{sti}"])
                SW(e, "ac")
                TK(e.activation(tvec[:, tvb + 3:tvb + 4], tvec[:, tvb + 2:tvb + 3],
                                AF.Ln, bias=cbank[:, 0:1]), "ac")
        emit("scalar", ac_ln1)

        def ac_exp1(e, ti=ti):
            for tb in range(NTB):
                sti = ti * NTB + tb
                tvb = tb * 6
                SW(e, "ac")
                TK(e.activation(stash[:, NST + sti:NST + sti + 1],
                                tvec[:, tvb + 3:tvb + 4], AF.Exp,
                                scale=cbank[:, 2:3]), "ac")
                c.v[f"a_rstd_{sti}"] = c.val("ac")
        emit("scalar", ac_exp1)

        for tb in range(NTB):
            sti = ti * NTB + tb
            xb = xbuf[sti % 5]
            xn = xnbf[tb]
            tvb = tb * 6

            def ve_xn(e, xb=xb, xn=xn, sti=sti, tvb=tvb):
                W(e, "ac", c.v[f"a_rstd_{sti}"])
                W(e, "pe", c.v.get(f"a_tp_{sti - 4}", 0))
                SW(e, "ve")
                TK(e.tensor_scalar(xn, xb, stash[:, sti:sti + 1],
                                   stash[:, NST + sti:NST + sti + 1],
                                   ALU.add, ALU.mult), "ve")
                SW(e, "ve")
                TK(e.scalar_tensor_tensor(xnsq, xn, 1.0, xn, ALU.mult, ALU.mult,
                                          accum_out=tvec[:, tvb + 4:tvb + 5]), "ve")
            emit("vector", ve_xn)
            c.v[f"a_xn_{sti}"] = c.val("ve")
            c.v[f"a_v2_{sti}"] = c.val("ve")

        def ac_ln2(e, ti=ti):
            for tb in range(NTB):
                sti = ti * NTB + tb
                tvb = tb * 6
                W(e, "ve", c.v[f"a_v2_{sti}"])
                SW(e, "ac")
                TK(e.activation(tvec[:, tvb + 5:tvb + 6], tvec[:, tvb + 4:tvb + 5],
                                AF.Ln, bias=cbank[:, 1:2], scale=cbank[:, 3:4]),
                   "ac")
        emit("scalar", ac_ln2)

        def ac_exp2(e, ti=ti):
            for tb in range(NTB):
                sti = ti * NTB + tb
                tvb = tb * 6
                SW(e, "ac")
                TK(e.activation(stash[:, 2 * NST + sti:2 * NST + sti + 1],
                                tvec[:, tvb + 5:tvb + 6], AF.Exp,
                                scale=cbank[:, 2:3]), "ac")
                c.v[f"a_r2_{sti}"] = c.val("ac")
        emit("scalar", ac_exp2)

        for tb in range(NTB):
            sti = ti * NTB + tb
            xn = xnbf[tb]
            pbk = pb[tb % 2]

            def pe_tp(e, xn=xn, pbk=pbk, sti=sti, tb=tb, ti=ti):
                W(e, "ve", c.v[f"a_xn_{sti}"])
                W(e, "ve", c.v.get(f"a_tpev_{ti}_{tb - 2}",
                                   c.v.get(f"a_tpev_{ti - 1}_{tb + 2}", 0)))
                for db in range(ND):
                    TK(e.transpose(pbk[:, db * 128:(db + 1) * 128],
                                   xn[:, db * 128:(db + 1) * 128], idbf), "pe")
            emit("tensor", pe_tp)
            c.v[f"a_tp_{sti}"] = c.val("pe")

            def ve_tpev(e, pbk=pbk, tb=tb, sti=sti):
                W(e, "pe", c.v[f"a_tp_{sti}"])
                SW(e, "ve")
                src_ = pbk.rearrange("p (d t) -> p d t", d=ND)
                dst = xnf.rearrange("p (d t) -> p d t", d=ND)[:, :, tb * 128:(tb + 1) * 128]
                TK(e.tensor_copy(dst, src_), "ve")
            emit("vector", ve_tpev)
            c.v[f"a_tpev_{ti}_{tb}"] = c.val("ve")

        for tb in range(NTB):
            sti = ti * NTB + tb
            xn = xnbf[tb]

            def ve_g(e, xn=xn, sti=sti):
                W(e, "ac", c.v[f"a_r2_{sti}"])
                SW(e, "ve")
                TK(e.scalar_tensor_tensor(g1s, xn,
                                          stash[:, 2 * NST + sti:2 * NST + sti + 1],
                                          g_rep, ALU.mult, ALU.mult), "ve")
                SW(e, "ve")
                TK(e.tensor_add(g2s, g1s, b_rep), "ve")
                SW(e, "ve")
                TK(e.tensor_reduce(grun[:, 1:2], g2s, AX.X, ALU.max,
                                   apply_absolute_value=True), "ve")
                SW(e, "ve")
                TK(e.tensor_tensor(grun[:, 0:1], grun[:, 0:1], grun[:, 1:2],
                                   ALU.max), "ve")
            emit("vector", ve_g)
        ve_xnf_done = c.val("ve")

        for ib in range(NI):
            pwk = pf[ib % 2]

            def pe_mm(e, ib=ib, pwk=pwk, ve_xnf_done=ve_xnf_done, ti=ti):
                W(e, "ac", ac_wa)
                W(e, "ve", ve_xnf_done)
                W(e, "ve", c.v.get(f"a_conv_{ti}_{ib - 2}",
                                   c.v.get(f"a_conv_{ti - 1}_{ib - 2 + NI}", 0)))
                for kb in range(ND):
                    mm = e.matmul(pwk, w_in_x[:, (kb * NI + ib) * 128:(kb * NI + ib + 1) * 128],
                                  xnf[:, kb * TT:(kb + 1) * TT],
                                  start=(kb == 0), stop=(kb == ND - 1))
                TK(mm, "pe")
            emit("tensor", pe_mm)
            c.v[f"a_mm_{ti}_{ib}"] = c.val("pe")

            def ve_conv(e, ib=ib, pwk=pwk, first=first, ti=ti):
                W(e, "pe", c.v[f"a_mm_{ti}_{ib}"])
                W(e, "ac", c.v.get(f"a_u_{ti}_{ib - 1}",
                                   c.v.get(f"a_u_{ti - 1}_{NI - 1}", 0)))
                SW(e, "ve")
                if first and ib == 0:
                    TK(e.memset(tails, 0.0), "ve")
                    SW(e, "ve")
                a0, a1 = cacc[0], cacc[1]
                tl = tails[:, ib * 3:(ib + 1) * 3]
                TK(e.tensor_scalar(a0, pwk, cw_sb[:, 3 * NI + ib:3 * NI + ib + 1],
                                  cb_sb[:, ib:ib + 1], ALU.mult, ALU.add), "ve")
                SW(e, "ve")
                TK(e.scalar_tensor_tensor(a1[:, 1:512], pwk[:, 0:511],
                                         cw_sb[:, 2 * NI + ib:2 * NI + ib + 1],
                                         a0[:, 1:512], ALU.mult, ALU.add), "ve")
                TK(e.scalar_tensor_tensor(a1[:, 0:1], tl[:, 2:3],
                                         cw_sb[:, 2 * NI + ib:2 * NI + ib + 1],
                                         a0[:, 0:1], ALU.mult, ALU.add), "ve")
                SW(e, "ve")
                TK(e.scalar_tensor_tensor(a0[:, 2:512], pwk[:, 0:510],
                                         cw_sb[:, 1 * NI + ib:1 * NI + ib + 1],
                                         a1[:, 2:512], ALU.mult, ALU.add), "ve")
                TK(e.scalar_tensor_tensor(a0[:, 0:2], tl[:, 1:3],
                                         cw_sb[:, 1 * NI + ib:1 * NI + ib + 1],
                                         a1[:, 0:2], ALU.mult, ALU.add), "ve")
                SW(e, "ve")
                TK(e.scalar_tensor_tensor(a1[:, 3:512], pwk[:, 0:509],
                                         cw_sb[:, 0 * NI + ib:0 * NI + ib + 1],
                                         a0[:, 3:512], ALU.mult, ALU.add), "ve")
                TK(e.scalar_tensor_tensor(a1[:, 0:3], tl[:, 0:3],
                                         cw_sb[:, 0 * NI + ib:0 * NI + ib + 1],
                                         a0[:, 0:3], ALU.mult, ALU.add), "ve")
                SW(e, "ve")
                TK(e.tensor_copy(tl, pwk[:, 509:512]), "ve")
            emit("vector", ve_conv)
            c.v[f"a_conv_{ti}_{ib}"] = c.val("ve")

            def ac_silu(e, ib=ib, ti=ti):
                W(e, "ve", c.v[f"a_conv_{ti}_{ib}"])
                W(e, "us", c.v.get(f"a_udma_{ti - 1}", 0))
                SW(e, "ac")
                TK(e.activation(ubf[:, ib * TT:(ib + 1) * TT], cacc[1], AF.Silu),
                  "ac")
            emit("scalar", ac_silu)
            c.v[f"a_u_{ti}_{ib}"] = c.val("ac")

        def st_u(e, ti=ti):
            W(e, "ac", c.v[f"a_u_{ti}_{NI - 1}"])
            e.dma_start(out=u_hbm.ap()[ti * 128:(ti + 1) * 128, :], in_=ubf
                        ).then_inc(sem["us"], 16)
        c.v[f"a_udma_{ti}"] = c.inc("us", 16)
        emit("sync", st_u)

    emit("sync", lambda e, v=c.val("us"): e.wait_ge(sem["us"], v))
    barrier()

    ar.reset()
    w_x = ar.take("w_x", NI * 1056, BF16)
    w_dt = ar.take("w_dt", ND * I, BF16)
    uin = [ar.take(f"uin{i}", NI * TT, BF16) for i in range(2)]
    dtin = ar.take("dtin", ND * TT, BF16)
    bcb = ar.take("bcb", 512, BF16)
    iotae_c = ar.take("iotae", 128, F32)
    eb_c = ar.take("eb", 2 * S * 128, BF16)
    iotae = iotae_c[0:32, :]
    eb = eb_c[0:32, :]
    brep_off = (ar.off + 63) & ~63
    brep = ar.take("brep", S * TT, BF16)
    crep = ar.take("crep", S * TT, BF16)
    wda = ar.take("wda", 4 * 512, F32)
    dub = ar.take("dub", 512, BF16)
    da4 = [ar.take(f"da4{i}", 4 * 512, F32) for i in range(2)]
    dbu4 = ar.take("dbu4", 4 * 512, BF16)
    h4 = ar.take("h4", 4 * 512, BF16)
    hc4 = [ar.take(f"hc4{i}", 4 * 512, BF16) for i in range(2)]
    y2st = ar.take("y2st", 4 * 512, BF16)
    wstage_b = ar.at(brep_off, "wstage_b", 2 * 2048, F32)

    def b_iota(e):
        e.iota(iotae, [[0, 128]], channel_multiplier=1,
               allow_small_or_imprecise_dtypes=True).then_inc(sem["gp"], 1)
    gp_eb = c.inc("gp", 1)
    emit("gpsimd", b_iota)

    def b_eb(e):
        W(e, "gp", gp_eb)
        SW(e, "ve")
        for s in range(2 * S):
            TK(e.tensor_scalar(eb[:, s * 128:(s + 1) * 128], iotae, float(s), None,
                              ALU.is_equal), "ve")
    emit("vector", b_eb)

    weight_load_cast(w_x, lambda kb: w_x_h.ap()[kb * 128:(kb + 1) * 128, :],
                     wstage_b, 2048, NI, 1056)
    weight_load_cast(w_dt, lambda kb: w_dt_h.ap()[kb * 128:(kb + 1) * 128, :],
                     wstage_b, 2048, ND, I)
    ve_wb = c.val("ve")
    ac_wb = c.val("ac")

    def p1b_init(e):
        SW(e, "ve")
        TK(e.memset(hc, 0.0), "ve")
    emit("vector", p1b_init)

    for ti in range(NT):
        ub = uin[ti % 2]
        first = (ti % (NT // BL)) == 0
        usem = f"ud{ti % 2}"

        def ld_u(e, ti=ti, ub=ub, usem=usem):
            W(e, "ve", c.v.get(f"b_y2_{ti - 2}_{NI - 1}", 0))
            e.dma_start(out=ub, in_=u_hbm.ap()[ti * 128:(ti + 1) * 128, :]
                        ).then_inc(sem[usem], 16)
        c.v[f"b_u_{ti}"] = c.inc(usem, 16)
        c.v[f"b_usem_{ti}"] = usem
        emit("sync", ld_u)

        if first and ti > 0:
            def rhc(e):
                SW(e, "ve")
                TK(e.memset(hc, 0.0), "ve")
            emit("vector", rhc)

        for nb in range(ND):
            pwk = pf[nb % 2]

            def pe_wx(e, nb=nb, pwk=pwk, ti=ti):
                W(e, "ac", ac_wb)
                W(e, c.v[f"b_usem_{ti}"], c.v[f"b_u_{ti}"])
                W(e, "ac", c.v.get(f"b_dtin_{ti}_{nb - 2}",
                                   c.v.get(f"b_sp_{ti - 1}_{NI - 1}", 0)))
                for kb in range(NI):
                    mm = e.matmul(pwk, w_x[:, kb * 1056 + nb * 128:kb * 1056 + (nb + 1) * 128],
                                  ub[:, kb * TT:(kb + 1) * TT],
                                  start=(kb == 0), stop=(kb == NI - 1))
                TK(mm, "pe")
            emit("tensor", pe_wx)
            c.v[f"b_wx_{ti}_{nb}"] = c.val("pe")

            def ac_dtin(e, nb=nb, pwk=pwk, ti=ti):
                W(e, "pe", c.v[f"b_wx_{ti}_{nb}"])
                SW(e, "ac")
                TK(e.copy(dtin[:, nb * TT:(nb + 1) * TT], pwk), "ac")
            emit("scalar", ac_dtin)
            c.v[f"b_dtin_{ti}_{nb}"] = c.val("ac")

        def pe_bc(e, ti=ti):
            W(e, "ve", c.v.get(f"b_bcev_{ti - 1}", 0))
            for kb in range(NI):
                mm = e.matmul(pf[2][0:32, :], w_x[:, kb * 1056 + 1024:kb * 1056 + 1056],
                              ub[:, kb * TT:(kb + 1) * TT],
                              start=(kb == 0), stop=(kb == NI - 1))
            TK(mm, "pe")
        emit("tensor", pe_bc)
        c.v[f"b_bc_{ti}"] = c.val("pe")

        def ve_bcev(e, ti=ti):
            W(e, "pe", c.v[f"b_bc_{ti}"])
            SW(e, "ve")
            TK(e.tensor_copy(bcb[0:32, 0:512], pf[2][0:32, :]), "ve")
        emit("vector", ve_bcev)
        c.v[f"b_bcev_{ti}"] = c.val("ve")

        for s in range(S):
            def pe_br(e, s=s, ti=ti):
                W(e, "ve", c.v[f"b_bcev_{ti}"])
                W(e, "ve", c.v.get(f"b_brev_{ti}_{s - 1}", 0))
                e.matmul(pf[2], eb[:, s * 128:(s + 1) * 128], bcb[0:32, 0:512],
                         start=True, stop=True)
                TK(e.matmul(pf[3], eb[:, (S + s) * 128:(S + s + 1) * 128],
                           bcb[0:32, 0:512], start=True, stop=True), "pe")
            emit("tensor", pe_br)
            c.v[f"b_br_{ti}_{s}"] = c.val("pe")

            def ve_brev(e, s=s, ti=ti):
                W(e, "pe", c.v[f"b_br_{ti}_{s}"])
                SW(e, "ve")
                TK(e.tensor_copy(brep[:, s * TT:(s + 1) * TT], pf[2]), "ve")
                TK(e.tensor_copy(crep[:, s * TT:(s + 1) * TT], pf[3]), "ve")
            emit("vector", ve_brev)
            c.v[f"b_brev_{ti}_{s}"] = c.val("ve")
        ve_brep_done = c.val("ve")

        for q in range(4):
            for ib in range(4 * q, 4 * q + 4):
                pwk = pf[ib % 2]

                def pe_wdt(e, ib=ib, pwk=pwk, ti=ti):
                    W(e, "ac", c.v[f"b_dtin_{ti}_{ND - 1}"])
                    W(e, "ac", c.v.get(f"b_sp_{ti}_{ib - 2}", 0))
                    for kb in range(ND):
                        mm = e.matmul(pwk, w_dt[:, kb * I + ib * 128:kb * I + (ib + 1) * 128],
                                      dtin[:, kb * TT:(kb + 1) * TT],
                                      start=(kb == 0), stop=(kb == ND - 1))
                    TK(mm, "pe")
                emit("tensor", pe_wdt)
                c.v[f"b_wdt_{ti}_{ib}"] = c.val("pe")

                def ac_spexp(e, ib=ib, pwk=pwk, ti=ti):
                    W(e, "pe", c.v[f"b_wdt_{ti}_{ib}"])
                    W(e, "ve", c.v.get(f"b_du_{ti}_{ib - 4}",
                                       c.v.get(f"b_du_{ti - 1}_{ib + 12}", 0)))
                    SW(e, "ac")
                    TK(e.activation(wda[:, (ib % 4) * 512:(ib % 4 + 1) * 512], pwk,
                                    AF.Exp, bias=bdt_sb[:, ib:ib + 1]), "ac")
                emit("scalar", ac_spexp)
                c.v[f"b_sp_{ti}_{ib}"] = c.val("ac")

            def ac_lnq(e, q=q, ti=ti):
                SW(e, "ac")
                TK(e.activation(wda, wda, AF.Ln, bias=1.0), "ac")
            emit("scalar", ac_lnq)
            c.v[f"b_lnq_{ti}_{q}"] = c.val("ac")

            for ib in range(4 * q, 4 * q + 4):
                dsl = wda[:, (ib % 4) * 512:(ib % 4 + 1) * 512]

                def ve_du(e, ib=ib, ti=ti, q=q, dsl=dsl):
                    W(e, "ac", c.v[f"b_lnq_{ti}_{q}"])
                    W(e, c.v[f"b_usem_{ti}"], c.v[f"b_u_{ti}"])
                    SW(e, "ve")
                    TK(e.tensor_tensor(dub, dsl, ub[:, ib * TT:(ib + 1) * TT],
                                       ALU.mult), "ve")
                emit("vector", ve_du)
                c.v[f"b_du_{ti}_{ib}"] = c.val("ve")

                for g in range(4):
                    dab = da4[g % 2]

                    def ac_da(e, ib=ib, g=g, dab=dab, ti=ti, dsl=dsl):
                        W(e, "ve", c.v.get(f"b_scan_{ti}_{ib}_{g - 2}",
                                           c.v.get(f"b_scan_{ti}_{ib - 1}_{g + 2}",
                                                   c.v.get(f"b_scan_{ti - 1}_{NI - 1}_{g + 2}", 0))))
                        SW(e, "ac")
                        for s in range(4):
                            TK(e.activation(dab[:, s * 512:(s + 1) * 512], dsl, AF.Exp,
                                            scale=a_sb[:, ib * S + g * 4 + s:ib * S + g * 4 + s + 1]),
                               "ac")
                    emit("scalar", ac_da)
                    c.v[f"b_da_{ti}_{ib}_{g}"] = c.val("ac")

                    def ve_scan(e, ib=ib, g=g, dab=dab, ti=ti):
                        SW(e, "ve")
                        bs = brep[:, (g * 4) * TT:(g * 4 + 4) * TT].rearrange(
                            "p (s t) -> p s t", s=4)
                        TK(e.tensor_tensor(dbu4.rearrange("p (s t) -> p s t", s=4),
                                           dub.unsqueeze(1).to_broadcast([128, 4, TT]),
                                           bs, ALU.mult), "ve")
                        SW(e, "ve")
                        W(e, "ac", c.v[f"b_da_{ti}_{ib}_{g}"])
                        for s in range(4):
                            TK(e.tensor_tensor_scan(
                                h4[:, s * 512:(s + 1) * 512],
                                dab[:, s * 512:(s + 1) * 512],
                                dbu4[:, s * 512:(s + 1) * 512],
                                hc[:, ib * S + g * 4 + s:ib * S + g * 4 + s + 1],
                                ALU.mult, ALU.add), "ve")
                        SW(e, "ve")
                        TK(e.tensor_copy(hc[:, ib * S + g * 4:ib * S + g * 4 + 4],
                                         h4.rearrange("p (s t) -> p s t", s=4)[:, :, 511]),
                           "ve")
                    emit("vector", ve_scan)
                    c.v[f"b_scan_{ti}_{ib}_{g}"] = c.val("ve")

                    hck = hc4[g % 2]

                    def ve_ym(e, ib=ib, g=g, hck=hck, ti=ti):
                        W(e, "pe", c.v.get(f"b_idmm_{ti}_{ib}_{g - 2}",
                                           c.v.get(f"b_idmm_{ti}_{ib - 1}_{g + 2}",
                                                   c.v.get(f"b_idmm_{ti - 1}_{NI - 1}_{g + 2}", 0))))
                        SW(e, "ve")
                        TK(e.tensor_tensor(hck, h4,
                                           crep[:, (g * 4) * TT:(g * 4 + 4) * TT],
                                           ALU.mult), "ve")
                    emit("vector", ve_ym)
                    c.v[f"b_ym_{ti}_{ib}_{g}"] = c.val("ve")

                    pyk = pf[4 + (ib % 2)]

                    def pe_id(e, ib=ib, g=g, hck=hck, pyk=pyk, ti=ti):
                        W(e, "ve", c.v[f"b_ym_{ti}_{ib}_{g}"])
                        if g == 0:
                            W(e, "ve", c.v.get(f"b_y2_{ti}_{ib - 2}",
                                               c.v.get(f"b_y2_{ti - 1}_{ib - 2 + NI}", 0)))
                        for s in range(4):
                            mm = e.matmul(pyk, idbf, hck[:, s * 512:(s + 1) * 512],
                                          start=(g == 0 and s == 0),
                                          stop=(g == 3 and s == 3))
                        TK(mm, "pe")
                    emit("tensor", pe_id)
                    c.v[f"b_idmm_{ti}_{ib}_{g}"] = c.val("pe")

                def ve_y2(e, ib=ib, ti=ti):
                    pyk = pf[4 + (ib % 2)]
                    W(e, "pe", c.v[f"b_idmm_{ti}_{ib}_{3}"])
                    W(e, "ys", c.v.get(f"b_ydma_{ti}_{ib // 4 - 1}",
                                       c.v.get(f"b_ydma_{ti - 1}_{3}", 0)))
                    W(e, c.v[f"b_usem_{ti}"], c.v[f"b_u_{ti}"])
                    SW(e, "ve")
                    TK(e.scalar_tensor_tensor(y2st[:, (ib % 4) * 512:(ib % 4 + 1) * 512],
                                              pyk, 1.0, ub[:, ib * TT:(ib + 1) * TT],
                                              ALU.mult, ALU.add), "ve")
                emit("vector", ve_y2)
                c.v[f"b_y2_{ti}_{ib}"] = c.val("ve")

                if ib % 4 == 3:
                    def st_y2(e, ib=ib, ti=ti):
                        W(e, "ve", c.v[f"b_y2_{ti}_{ib}"])
                        e.dma_start(
                            out=y2_hbm.ap()[ti * 128:(ti + 1) * 128,
                                            (ib - 3) * TT:(ib + 1) * TT],
                            in_=y2st).then_inc(sem["ys"], 16)
                    c.v[f"b_ydma_{ti}_{ib // 4}"] = c.inc("ys", 16)
                    emit("sync", st_y2)

    emit("sync", lambda e, v=c.val("ys"): e.wait_ge(sem["ys"], v))
    barrier()

    ar.reset()
    w_in_z = ar.take("w_in_z", ND * I, BF16)
    w_o = ar.take("w_o", NI * D, BF16)
    wq = ar.take("wq", ND * D, BF16)
    xb2 = [ar.take(f"x2_{i}", D, F32) for i in range(5)]
    xnbf2 = [ar.take(f"xnbf2_{i}", D, BF16) for i in range(2)]
    xnf2 = ar.take("xnf2", ND * TT, BF16)
    xqf = ar.take("xqf", ND * TT, BF16)
    y2b_off = (ar.off + 63) & ~63
    y2b = ar.take("y2b", NI * TT, BF16)
    yop = ar.take("yop", NI * TT, BF16)
    szp = [ar.take(f"szp{i}", 512, BF16) for i in range(2)]
    xq_t = [ar.take(f"xqt{i}", D, BF16) for i in range(2)]
    q1s = ar.take("q1s", D, F32)
    attn_sb = ar.take("attn_sb", 512, F32)
    o_f = ar.take("o_f", 512, F32)
    outsb = [ar.take(f"outsb{i}", D, F32) for i in range(4)]
    wstage2 = ar.at(y2b_off, "wstage2", 2 * 2048, F32)
    fwstage = ar.at(y2b_off + 16 * KB, "fwstage", D, F32)
    awstage = ar.at(y2b_off + 20 * KB, "awstage", D, F32)
    bsum = ar.take("bsum", 16, F32)

    weight_load_cast(w_in_z, lambda kb: w_in_h.ap()[kb * 128:(kb + 1) * 128, I:2 * I],
                     wstage2, 2048, ND, I)
    weight_load_cast(w_o, lambda kb: w_out_h.ap()[kb * 128:(kb + 1) * 128, :],
                     wstage2, 2048, NI, D)
    ve_w2 = c.val("ve")
    ac_w2 = c.val("ac")

    def g_dma(e):
        e.dma_start(out=smll[0:1, 0:128], in_=grun[:, 0:1]).then_inc(sem["md"], 16)
    md_g = c.inc("md", 16)
    emit("sync", g_dma)

    def g_fin(e, md_g=md_g):
        W(e, "md", md_g)
        SW(e, "ve")
        TK(e.tensor_reduce(tvec[0:1, 0:1], smll[0:1, 0:128], AX.X, ALU.max), "ve")
        SW(e, "ve")
        TK(e.reciprocal(tvec[0:1, 1:2], tvec[0:1, 0:1]), "ve")
        SW(e, "ve")
        TK(e.tensor_scalar(tvec[0:1, 1:2], tvec[0:1, 1:2], QB, None, ALU.mult), "ve")
    emit("vector", g_fin)

    for kb in range(ND):
        def fw_ld(e, kb=kb):
            W(e, "ve", c.v.get(f"c_abs_{kb - 1}", 0))
            e.dma_start(out=fwstage, in_=fw_h.ap()[kb * 128:(kb + 1) * 128, :]
                        ).then_inc(sem["fs"], 16)
        wd_v = c.inc("fs", 16)
        emit("sync", fw_ld)

        def ve_abs(e, kb=kb, wd_v=wd_v):
            W(e, "fs", wd_v)
            SW(e, "ve")
            TK(e.scalar_tensor_tensor(awstage, fwstage, -1.0, fwstage,
                                     ALU.mult, ALU.max), "ve")
            SW(e, "ve")
            TK(e.tensor_reduce(bsum[:, kb:kb + 1], awstage, AX.X, ALU.add), "ve")
        emit("vector", ve_abs)
        c.v[f"c_abs_{kb}"] = c.val("ve")

    def ve_bsum(e):
        SW(e, "ve")
        TK(e.reduce_sum(bsum[:, 8:9], bsum[:, 0:8], AX.X), "ve")
    emit("vector", ve_bsum)
    ve_bs = c.val("ve")

    def pe_bsum(e, ve_bs=ve_bs):
        W(e, "ve", ve_bs)
        TK(e.matmul(pf[2][0:1, 0:1], onesc, bsum[:, 8:9], start=True, stop=True),
          "pe")
    emit("tensor", pe_bsum)
    pe_bs = c.val("pe")

    def ve_beta(e, pe_bs=pe_bs):
        W(e, "pe", pe_bs)
        SW(e, "ve")
        TK(e.tensor_scalar(tvec[0:1, 2:3], pf[2][0:1, 0:1], 1.0 / (D * D), None,
                          ALU.mult), "ve")
        SW(e, "ve")
        TK(e.tensor_scalar(tvec[0:1, 3:4], tvec[0:1, 2:3], W_EPS, 0.5,
                          ALU.add, ALU.mult), "ve")
        TK(e.tensor_tensor(tvec[0:1, 4:5], tvec[0:1, 2:3], tvec[0:1, 0:1],
                          ALU.mult), "ve")
        SW(e, "ve")
        TK(e.tensor_scalar(tvec[0:1, 4:5], tvec[0:1, 4:5], 1.0 / QB, None,
                          ALU.mult), "ve")
    emit("vector", ve_beta)
    ve_beta_v = c.val("ve")

    def pe_bcastv(e, ve_beta_v=ve_beta_v):
        W(e, "ve", ve_beta_v)
        TK(e.matmul(pf[2][:, 0:5], ones1, tvec[0:1, 0:5], start=True, stop=True),
          "pe")
    emit("tensor", pe_bcastv)
    pe_bv = c.val("pe")

    def ve_vecs(e, pe_bv=pe_bv):
        W(e, "pe", pe_bv)
        SW(e, "ve")
        TK(e.tensor_copy(vec4[:, 0:1], pf[2][:, 1:2]), "ve")
        TK(e.tensor_copy(vec4[:, 1:2], pf[2][:, 3:4]), "ve")
        TK(e.tensor_copy(vec4[:, 2:3], pf[2][:, 4:5]), "ve")
        TK(e.tensor_copy(vec4[:, 3:4], pf[2][:, 2:3]), "ve")
        SW(e, "ve")
        TK(e.tensor_scalar(g_rep, g_rep, vec4[:, 0:1], None, ALU.mult), "ve")
        TK(e.tensor_scalar(b_rep, b_rep, vec4[:, 0:1], None, ALU.mult), "ve")
    emit("vector", ve_vecs)
    ve_vecs_v = c.val("ve")

    for kb in range(ND):
        def fw_ld2(e, kb=kb):
            W(e, "ve", c.v.get(f"c_wq_{kb - 1}", ve_vecs_v))
            W(e, "ac", c.v.get(f"c_sgn_{kb - 1}", 0))
            e.dma_start(out=fwstage, in_=fw_h.ap()[kb * 128:(kb + 1) * 128, :]
                        ).then_inc(sem["fs"], 16)
        wd_v = c.inc("fs", 16)
        emit("sync", fw_ld2)

        def ac_sgn(e, kb=kb, wd_v=wd_v):
            W(e, "fs", wd_v)
            SW(e, "ac")
            TK(e.activation(awstage, fwstage, AF.Sign), "ac")
        emit("scalar", ac_sgn)
        c.v[f"c_sgn_{kb}"] = c.val("ac")

        def ve_wq(e, kb=kb, wd_v=wd_v):
            W(e, "fs", wd_v)
            W(e, "ve", ve_vecs_v)
            W(e, "ac", c.v[f"c_sgn_{kb}"])
            SW(e, "ve")
            TK(e.scalar_tensor_tensor(q1s, fwstage, -1.0, fwstage, ALU.mult,
                                     ALU.max), "ve")
            SW(e, "ve")
            TK(e.tensor_scalar(q1s, q1s, vec4[:, 1:2], None, ALU.is_gt), "ve")
            SW(e, "ve")
            TK(e.tensor_tensor(wq[:, kb * D:(kb + 1) * D], q1s, awstage, ALU.mult),
              "ve")
        emit("vector", ve_wq)
        c.v[f"c_wq_{kb}"] = c.val("ve")

    for ti in range(NT):
        b = ti // (NT // BL)
        t0 = (ti % (NT // BL)) * TT

        for tb in range(NTB):
            xb = xb2[(ti * NTB + tb) % 5]
            xs = f"xd{(ti * NTB + tb) % 5}"

            def ld(e, b=b, t0=t0, tb=tb, xb=xb, ti=ti, xs=xs):
                W(e, "ve", c.v.get(f"c_oadd_{ti - 1}", 0))
                e.dma_start(out=xb, in_=x_h.ap()[b, t0 + tb * 128:t0 + (tb + 1) * 128, :]
                            ).then_inc(sem[xs], 16)
            c.v[f"c_x_{ti}_{tb}"] = c.inc(xs, 16)
            c.v[f"c_xsem_{ti}_{tb}"] = xs
            emit("sync", ld)

        def ld_y2(e, ti=ti):
            W(e, "pe", c.v.get(f"c_attn_{ti - 1}_{ND - 1}", 0))
            W(e, "ve", c.v[f"c_wq_{ND - 1}"])
            e.dma_start(out=y2b, in_=y2_hbm.ap()[ti * 128:(ti + 1) * 128, :]
                        ).then_inc(sem["yl"], 16)
        c.v[f"c_y2_{ti}"] = c.inc("yl", 16)
        emit("sync", ld_y2)

        for tb in range(NTB):
            sti = ti * NTB + tb
            xb = xb2[sti % 5]
            xn = xnbf2[tb % 2]

            def ve_xn2(e, xb=xb, xn=xn, sti=sti, ti=ti, tb=tb):
                W(e, c.v[f"c_xsem_{ti}_{tb}"], c.v[f"c_x_{ti}_{tb}"])
                W(e, "pe", c.v.get(f"c_tp_{ti}_{tb - 2}", 0))
                SW(e, "ve")
                TK(e.tensor_scalar(xn, xb, stash[:, sti:sti + 1],
                                  stash[:, NST + sti:NST + sti + 1],
                                  ALU.add, ALU.mult), "ve")
            emit("vector", ve_xn2)
            c.v[f"c_xn_{ti}_{tb}"] = c.val("ve")

            pbk = pb[tb % 2]

            def pe_tp(e, xn=xn, pbk=pbk, ti=ti, tb=tb):
                W(e, "ve", c.v[f"c_xn_{ti}_{tb}"])
                W(e, "ve", c.v.get(f"c_tpev_{ti}_{tb - 2}", 0))
                for db in range(ND):
                    TK(e.transpose(pbk[:, db * 128:(db + 1) * 128],
                                  xn[:, db * 128:(db + 1) * 128], idbf), "pe")
            emit("tensor", pe_tp)
            c.v[f"c_tp_{ti}_{tb}"] = c.val("pe")

            def ve_tpev(e, pbk=pbk, tb=tb, ti=ti):
                W(e, "pe", c.v[f"c_tp_{ti}_{tb}"])
                SW(e, "ve")
                src = pbk.rearrange("p (d t) -> p d t", d=ND)
                dst = xnf2.rearrange("p (d t) -> p d t", d=ND)[:, :, tb * 128:(tb + 1) * 128]
                TK(e.tensor_copy(dst, src), "ve")
            emit("vector", ve_tpev)
            c.v[f"c_tpev_{ti}_{tb}"] = c.val("ve")

        for ib in range(NI):
            pwk = pf[ib % 2]

            def pe_z(e, ib=ib, pwk=pwk, ti=ti):
                W(e, "ac", ac_w2)
                W(e, "ve", c.v[f"c_tpev_{ti}_{NTB - 1}"])
                W(e, "ac", c.v.get(f"c_sz_{ti}_{ib - 2}",
                                   c.v.get(f"c_attnev_{ti - 1}_{ND - 1}", 0)))
                for kb in range(ND):
                    mm = e.matmul(pwk, w_in_z[:, (kb * NI + ib) * 128:(kb * NI + ib + 1) * 128],
                                  xnf2[:, kb * TT:(kb + 1) * TT],
                                  start=(kb == 0), stop=(kb == ND - 1))
                TK(mm, "pe")
            emit("tensor", pe_z)
            c.v[f"c_z_{ti}_{ib}"] = c.val("pe")

            def ac_sz(e, ib=ib, pwk=pwk, ti=ti):
                W(e, "pe", c.v[f"c_z_{ti}_{ib}"])
                W(e, "ve", c.v.get(f"c_yopk_{ti}_{ib - 2}", 0))
                SW(e, "ac")
                TK(e.activation(szp[ib % 2], pwk, AF.Silu), "ac")
            emit("scalar", ac_sz)
            c.v[f"c_sz_{ti}_{ib}"] = c.val("ac")

            def ve_yop(e, ib=ib, ti=ti):
                W(e, "ac", c.v[f"c_sz_{ti}_{ib}"])
                W(e, "yl", c.v[f"c_y2_{ti}"])
                SW(e, "ve")
                TK(e.tensor_tensor(yop[:, ib * TT:(ib + 1) * TT],
                                  y2b[:, ib * TT:(ib + 1) * TT], szp[ib % 2],
                                  ALU.mult), "ve")
            emit("vector", ve_yop)
            c.v[f"c_yopk_{ti}_{ib}"] = c.val("ve")
        c.v[f"c_yop_{ti}"] = c.val("ve")

        for tb in range(NTB):
            sti = ti * NTB + tb
            xn = xnbf2[tb % 2]
            xqt = xq_t[tb % 2]

            def ve_xq(e, xn=xn, sti=sti, xqt=xqt, ti=ti, tb=tb):
                xb = xb2[sti % 5]
                W(e, "pe", c.v.get(f"c_tpq_{ti}_{tb - 2}", 0))
                SW(e, "ve")
                TK(e.tensor_scalar(xn, xb, stash[:, sti:sti + 1],
                                  stash[:, NST + sti:NST + sti + 1],
                                  ALU.add, ALU.mult), "ve")
                SW(e, "ve")
                TK(e.scalar_tensor_tensor(q1s, xn,
                                         stash[:, 2 * NST + sti:2 * NST + sti + 1],
                                         g_rep, ALU.mult, ALU.mult), "ve")
                SW(e, "ve")
                TK(e.tensor_add(q1s, q1s, b_rep), "ve")
                SW(e, "ve")
                TK(e.tensor_scalar(xqt, q1s, CLIPV, -CLIPV, ALU.min, ALU.max), "ve")
            emit("vector", ve_xq)
            c.v[f"c_xq_{ti}_{tb}"] = c.val("ve")

            pbk = pb[tb % 2]

            def pe_tpq(e, xqt=xqt, pbk=pbk, ti=ti, tb=tb):
                W(e, "ve", c.v[f"c_xq_{ti}_{tb}"])
                W(e, "ve", c.v.get(f"c_tpqev_{ti}_{tb - 2}", 0))
                for db in range(ND):
                    TK(e.transpose(pbk[:, db * 128:(db + 1) * 128],
                                  xqt[:, db * 128:(db + 1) * 128], idbf), "pe")
            emit("tensor", pe_tpq)
            c.v[f"c_tpq_{ti}_{tb}"] = c.val("pe")

            def ve_tpqev(e, pbk=pbk, tb=tb, ti=ti):
                W(e, "pe", c.v[f"c_tpq_{ti}_{tb}"])
                SW(e, "ve")
                src = pbk.rearrange("p (d t) -> p d t", d=ND)
                dst = xqf.rearrange("p (d t) -> p d t", d=ND)[:, :, tb * 128:(tb + 1) * 128]
                TK(e.tensor_copy(dst, src), "ve")
            emit("vector", ve_tpqev)
            c.v[f"c_tpqev_{ti}_{tb}"] = c.val("ve")

        for db in range(ND):
            pa = pf[db % 2]
            pq = pf[2 + db % 2]
            ptr = pf[4 + db % 2]

            def pe_attn(e, db=db, pa=pa, ti=ti):
                W(e, "ve", c.v[f"c_yop_{ti}"])
                W(e, "ac", c.v.get(f"c_attnev_{ti}_{db - 2}",
                                   c.v.get(f"c_sz_{ti}_{NI - 1}", 0)))
                for kb in range(NI):
                    mm = e.matmul(pa, w_o[:, kb * D + db * 128:kb * D + (db + 1) * 128],
                                  yop[:, kb * TT:(kb + 1) * TT],
                                  start=(kb == 0), stop=(kb == NI - 1))
                TK(mm, "pe")
            emit("tensor", pe_attn)
            c.v[f"c_attn_{ti}_{db}"] = c.val("pe")

            def pe_ffn(e, db=db, pq=pq, ti=ti):
                W(e, "ve", c.v[f"c_tpqev_{ti}_{NTB - 1}"])
                W(e, "ve", c.v.get(f"c_of_{ti}_{db - 2}",
                                   c.v.get(f"c_of_{ti - 1}_{ND - 1}", 0)))
                for kb in range(ND):
                    mm = e.matmul(pq, wq[:, kb * D + db * 128:kb * D + (db + 1) * 128],
                                  xqf[:, kb * TT:(kb + 1) * TT],
                                  start=(kb == 0), stop=(kb == ND - 1))
                TK(mm, "pe")
            emit("tensor", pe_ffn)
            c.v[f"c_ffn_{ti}_{db}"] = c.val("pe")

            def ac_attnev(e, pa=pa, ti=ti, db=db):
                W(e, "pe", c.v[f"c_attn_{ti}_{db}"])
                W(e, "ve", c.v.get(f"c_of_{ti}_{db - 1}", 0))
                SW(e, "ac")
                TK(e.copy(attn_sb, pa), "ac")
            emit("scalar", ac_attnev)
            c.v[f"c_attnev_{ti}_{db}"] = c.val("ac")

            def ve_of(e, pq=pq, ti=ti, db=db):
                W(e, "pe", c.v[f"c_ffn_{ti}_{db}"])
                W(e, "ac", c.v[f"c_attnev_{ti}_{db}"])
                W(e, "pe", c.v.get(f"c_tro_{ti}_{db - 1}", 0))
                SW(e, "ve")
                TK(e.scalar_tensor_tensor(o_f, pq, vec4[:, 2:3], attn_sb,
                                         ALU.mult, ALU.add), "ve")
            emit("vector", ve_of)
            c.v[f"c_of_{ti}_{db}"] = c.val("ve")

            def pe_tro(e, db=db, ptr=ptr, ti=ti):
                W(e, "ve", c.v[f"c_of_{ti}_{db}"])
                W(e, "ve", c.v.get(f"c_oaddp_{ti}_{db - 2}",
                                   c.v.get(f"c_oaddp_{ti - 1}_{ND - 1}", 0)))
                for tb in range(NTB):
                    TK(e.transpose(ptr[:, tb * 128:(tb + 1) * 128],
                                  o_f[:, tb * 128:(tb + 1) * 128], idf), "pe")
            emit("tensor", pe_tro)
            c.v[f"c_tro_{ti}_{db}"] = c.val("pe")

            def ve_outadd(e, db=db, ptr=ptr, ti=ti):
                W(e, "pe", c.v[f"c_tro_{ti}_{db}"])
                if db == 0:
                    for q in range(NTB):
                        W(e, f"od{q}", c.v.get(f"c_odma_{ti - 1}_{q}", 0))
                SW(e, "ve")
                for tb in range(NTB):
                    xb = xb2[(ti * NTB + tb) % 5]
                    TK(e.tensor_tensor(outsb[tb][:, db * 128:(db + 1) * 128],
                                      xb[:, db * 128:(db + 1) * 128],
                                      ptr[:, tb * 128:(tb + 1) * 128], ALU.add),
                      "ve")
            emit("vector", ve_outadd)
            c.v[f"c_oaddp_{ti}_{db}"] = c.val("ve")
        c.v[f"c_oadd_{ti}"] = c.val("ve")

        for tb in range(NTB):
            def st_out(e, tb=tb, b=b, t0=t0, ti=ti):
                W(e, "ve", c.v[f"c_oadd_{ti}"])
                e.dma_start(out=out_h.ap()[b, t0 + tb * 128:t0 + (tb + 1) * 128, :],
                            in_=outsb[tb]).then_inc(sem[f"od{tb}"], 16)
            c.v[f"c_odma_{ti}_{tb}"] = c.inc(f"od{tb}", 16)
            emit("sync", st_out)

    def fin(e):
        for q in range(NTB):
            W(e, f"od{q}", c.val(f"od{q}"))
    emit("sync", fin)

    es.close()
    return nc


def kernel(**inputs):
    x = np.ascontiguousarray(np.asarray(inputs["L0_attn_inputs"], np.float32))
    nc = build_program()
    common = dict(
        w_in=np.ascontiguousarray(inputs["L0_attn_W_in"], np.float32),
        conv_w=np.ascontiguousarray(inputs["L0_attn_conv_w"], np.float32),
        conv_b=np.ascontiguousarray(inputs["L0_attn_conv_b"], np.float32),
        w_x=np.ascontiguousarray(inputs["L0_attn_W_x"], np.float32),
        w_dt=np.ascontiguousarray(inputs["L0_attn_W_dt"], np.float32),
        b_dt=np.ascontiguousarray(inputs["L0_attn_b_dt"], np.float32),
        a_log=np.ascontiguousarray(inputs["L0_attn_A_log"], np.float32),
        w_out=np.ascontiguousarray(inputs["L0_attn_W_out"], np.float32),
        fw=np.ascontiguousarray(inputs["L0_ffn_weight"], np.float32),
        lng=np.ascontiguousarray(inputs["L0_ffn_ln_gamma"], np.float32),
        lnb=np.ascontiguousarray(inputs["L0_ffn_ln_beta"], np.float32),
    )
    in_maps = [dict(common, x=np.ascontiguousarray(x[2 * cc:2 * cc + 2]))
               for cc in range(NCORES)]
    res = run_bass_kernel_spmd(nc, in_maps, core_ids=list(range(NCORES)))
    return np.concatenate([r["out"] for r in res.results], axis=0)


if __name__ == "__main__":
    build_program()
    print("build OK")


# version 16
# speedup vs baseline: 1.1055x; 1.0759x over previous
import contextlib
import numpy as np
import concourse.bass as bass
import concourse.mybir as mybir
from concourse.bass_utils import run_bass_kernel_spmd

F32 = mybir.dt.float32
BF16 = mybir.dt.bfloat16
U8 = mybir.dt.uint8
AF = mybir.ActivationFunctionType
ALU = mybir.AluOpType
AX = mybir.AxisListType

NCORES = 8
B, T, D = 16, 2048, 1024
BL = B // NCORES
I = 2048
S = 16
TT = 512
NT = BL * T // TT
NTB = TT // 128
ND = D // 128
NI = I // 128
NST = NT * NTB
QB = 128.0
CLIPV = float(np.float32(QB - 1e-5))
LN1_EPS = 1e-5
LN2_EPS = 1e-3
W_EPS = 1e-5
KB = 1024


class Ctr:
    def __init__(self):
        self.v = {}

    def inc(self, name, n):
        self.v[name] = self.v.get(name, 0) + n
        return self.v[name]

    def val(self, name):
        return self.v.get(name, 0)


class Carver:
    def __init__(self, arena_ap, nbytes):
        self.ap = arena_ap
        self.n = nbytes
        self.off = 0

    def reset(self):
        self.off = 0

    def take(self, name, cols, dtype):
        esz = 2 if dtype == BF16 else 4
        nb = cols * esz
        off = (self.off + 63) & ~63
        assert off + nb <= self.n, f"arena overflow at {name}: {off}+{nb}>{self.n}"
        self.off = off + nb
        v = self.ap[:, off:off + nb].bitcast(dtype)
        assert v.shape == (128, cols), (name, v.shape)
        return v

    def at(self, off, name, cols, dtype):
        esz = 2 if dtype == BF16 else 4
        nb = cols * esz
        assert off % 64 == 0 and off + nb <= self.n, name
        v = self.ap[:, off:off + nb].bitcast(dtype)
        assert v.shape == (128, cols), (name, v.shape)
        return v


def build_program():
    nc = bass.Bass("TRN2", target_bir_lowering=False, debug=False)

    x_h = nc.dram_tensor("x", [BL, T, D], F32, kind="ExternalInput")
    w_in_h = nc.dram_tensor("w_in", [D, 2 * I], F32, kind="ExternalInput")
    conv_w_h = nc.dram_tensor("conv_w", [4, 1, I], F32, kind="ExternalInput")
    conv_b_h = nc.dram_tensor("conv_b", [I], F32, kind="ExternalInput")
    w_x_h = nc.dram_tensor("w_x", [I, D + 2 * S], F32, kind="ExternalInput")
    w_dt_h = nc.dram_tensor("w_dt", [D, I], F32, kind="ExternalInput")
    b_dt_h = nc.dram_tensor("b_dt", [I], F32, kind="ExternalInput")
    a_log_h = nc.dram_tensor("a_log", [I, S], F32, kind="ExternalInput")
    w_out_h = nc.dram_tensor("w_out", [I, D], F32, kind="ExternalInput")
    fw_h = nc.dram_tensor("fw", [D, D], F32, kind="ExternalInput")
    lng_h = nc.dram_tensor("lng", [D], F32, kind="ExternalInput")
    lnb_h = nc.dram_tensor("lnb", [D], F32, kind="ExternalInput")
    out_h = nc.dram_tensor("out", [BL, T, D], F32, kind="ExternalOutput")

    u_hbm = nc.dram_tensor("u_scr", [NT * 128, NI * TT], BF16)
    y2_hbm = nc.dram_tensor("y2_scr", [NT * 128, NI * TT], BF16)

    idf = nc.alloc_sbuf_tensor("idf", [128, 128], F32).ap()
    idbf = nc.alloc_sbuf_tensor("idbf", [128, 128], BF16).ap()
    ones1 = nc.alloc_sbuf_tensor("ones1", [1, 128], F32).ap()
    onesc = nc.alloc_sbuf_tensor("onesc", [128, 1], F32).ap()
    g_rep = nc.alloc_sbuf_tensor("g_rep", [128, D], F32).ap()
    b_rep = nc.alloc_sbuf_tensor("b_rep", [128, D], F32).ap()
    stash = nc.alloc_sbuf_tensor("stash", [128, 3 * NST], F32).ap()
    hc = nc.alloc_sbuf_tensor("hc", [128, NI * S], F32).ap()
    a_sb = nc.alloc_sbuf_tensor("a_sb", [128, NI * S], F32).ap()
    cw_sb = nc.alloc_sbuf_tensor("cw_sb", [128, 4 * NI], F32).ap()
    cb_sb = nc.alloc_sbuf_tensor("cb_sb", [128, NI], F32).ap()
    bdt_sb = nc.alloc_sbuf_tensor("bdt_sb", [128, NI], F32).ap()
    tails = nc.alloc_sbuf_tensor("tails", [128, 3 * NI], F32).ap()
    grun = nc.alloc_sbuf_tensor("grun", [128, 2], F32).ap()
    smll = nc.alloc_sbuf_tensor("smll", [1, 128], F32).ap()
    vec4 = nc.alloc_sbuf_tensor("vec4", [128, 4], F32).ap()
    tvec = nc.alloc_sbuf_tensor("tvec", [128, 48], F32).ap()
    cbank = nc.alloc_sbuf_tensor("cbank", [128, 4], F32).ap()
    fbv = nc.alloc_sbuf_tensor("fbv", [1, 4], F32).ap()
    fba = nc.alloc_sbuf_tensor("fba", [1, 4], F32).ap()

    arena_bytes = (nc.sbuf_bytes_remaining - 192) & ~63
    arena_t = nc.alloc_sbuf_tensor("arena", [128, arena_bytes], U8)
    ar = Carver(arena_t.ap(), arena_bytes)

    pf = [nc.alloc_psum_tensor(f"pf{i}", [128, 512], F32).ap() for i in range(6)]
    pb = [nc.alloc_psum_tensor(f"pb{i}", [128, 1024], BF16).ap() for i in range(2)]

    es = contextlib.ExitStack()
    sem = {}
    for name in (["md", "ve", "ac", "pe", "gp", "ph", "us", "ys", "yl", "fs"]
                 + [f"xd{i}" for i in range(5)] + [f"ws{i}" for i in range(2)]
                 + [f"ud{i}" for i in range(2)] + [f"od{i}" for i in range(4)]):
        sem[name] = es.enter_context(nc.semaphore(name))
    block = es.enter_context(nc.Block())
    c = Ctr()

    class InstrRec:
        __slots__ = ("name", "args", "kwargs", "incs")

        def __init__(self, name, args, kwargs):
            self.name = name
            self.args = args
            self.kwargs = kwargs
            self.incs = []

        def then_inc(self, s, n):
            self.incs.append((s, n))
            return self

    class RecEngine:
        def __init__(self, calls):
            self._calls = calls

        def __getattr__(self, name):
            def call(*a, **k):
                r = InstrRec(name, a, k)
                self._calls.append(r)
                return r
            return call

    PEND = {n: [] for n in ["sync", "vector", "scalar", "tensor", "gpsimd"]}

    def emit(engname, fn):
        if engname == "vector":
            def wrapped(e, fn=fn):
                fn(e)
                e.wait_ge(sem["ve"], c.val("ve"))
                TK(e.memset(fbv[0:1, 0:1], 0.0), "ve")
            fn = wrapped
        fn(RecEngine(PEND[engname]))

    def flush_sections():
        def replay(calls, e):
            for r in calls:
                instr = getattr(e, r.name)(*r.args, **r.kwargs)
                for s, n in r.incs:
                    instr.then_inc(s, n)
        for name in ["sync", "vector", "scalar", "tensor", "gpsimd"]:
            getattr(block, name)(
                lambda e, cs=PEND[name]: replay(cs, e))

    def W(e, name, val):
        assert val <= c.val(name), f"wait on future {name}: {val} > {c.val(name)}"
        if val > 0:
            e.wait_ge(sem[name], val)

    def TK(instr, s):
        instr.then_inc(sem[s], 1)
        return c.inc(s, 1)

    def SW(e, s):
        e.wait_ge(sem[s], c.val(s))



    def weight_load_cast(dst_bf, src_ap_fn, stage_f32, half_cols, kblks, row_cols):
        assert row_cols <= half_cols
        for kb in range(kblks):
            h0 = (kb % 2) * half_cols
            ws = f"ws{kb % 2}"

            def ld(e, kb=kb, h0=h0, ws=ws):
                W(e, "ac", c.val("ac"))
                e.dma_start(out=stage_f32[:, h0:h0 + row_cols],
                            in_=src_ap_fn(kb)).then_inc(sem[ws], 16)
            wd_after = c.inc(ws, 16)
            emit("sync", ld)

            def cast(e, kb=kb, h0=h0, wd_after=wd_after, ws=ws):
                W(e, ws, wd_after)
                SW(e, "ac")
                if kb == 0:
                    W(e, "ve", c.val("ve"))
                    W(e, "pe", c.val("pe"))
                TK(e.copy(dst_bf[:, kb * row_cols:(kb + 1) * row_cols],
                          stage_f32[:, h0:h0 + row_cols]), "ac")
            emit("scalar", cast)

    def barrier():
        target = c.val("ph") + 5
        for eng in ["sync", "vector", "scalar", "tensor", "gpsimd"]:
            emit(eng, lambda e: e.drain().then_inc(sem["ph"], 1))
        c.inc("ph", 5)
        for eng in ["sync", "vector", "scalar", "tensor", "gpsimd"]:
            emit(eng, lambda e, t=target: e.wait_ge(sem["ph"], t))

    iotat = ar.take("iotat", 128, F32)
    lngr_c = ar.take("lngr", D, F32)
    lnbr_c = ar.take("lnbr", D, F32)
    lngr = lngr_c[0:1, :]
    lnbr = lnbr_c[0:1, :]

    def p_dma(e):
        for ib in range(NI):
            e.dma_start(out=a_sb[:, ib * S:(ib + 1) * S],
                        in_=a_log_h.ap()[ib * 128:(ib + 1) * 128, :]).then_inc(sem["md"], 16)
            e.dma_start(out=cb_sb[:, ib:ib + 1],
                        in_=conv_b_h.ap()[ib * 128:(ib + 1) * 128].unsqueeze(1)).then_inc(sem["md"], 16)
            e.dma_start(out=bdt_sb[:, ib:ib + 1],
                        in_=b_dt_h.ap()[ib * 128:(ib + 1) * 128].unsqueeze(1)).then_inc(sem["md"], 16)
            for k in range(4):
                e.dma_start(out=cw_sb[:, k * NI + ib:k * NI + ib + 1],
                            in_=conv_w_h.ap()[k, 0, ib * 128:(ib + 1) * 128].unsqueeze(1)).then_inc(sem["md"], 16)
        e.dma_start(out=lngr, in_=lng_h.ap().unsqueeze(0)).then_inc(sem["md"], 16)
        e.dma_start(out=lnbr, in_=lnb_h.ap().unsqueeze(0)).then_inc(sem["md"], 16)
    n_pre = NI * 7 + 2
    md_pre = c.inc("md", 16 * n_pre)
    emit("sync", p_dma)

    def p_iota(e):
        e.iota(iotat, [[1, 128]], channel_multiplier=-1,
               allow_small_or_imprecise_dtypes=True).then_inc(sem["gp"], 1)
    gp_iota = c.inc("gp", 1)
    emit("gpsimd", p_iota)

    def p_ids(e):
        W(e, "gp", gp_iota)
        TK(e.tensor_scalar(idf, iotat, 0.0, None, ALU.is_equal), "ve")
        SW(e, "ve")
        TK(e.tensor_copy(idbf, idf), "ve")
        TK(e.memset(ones1, 1.0), "ve")
        TK(e.memset(onesc, 1.0), "ve")
        TK(e.memset(grun, 0.0), "ve")
        TK(e.memset(cbank[:, 0:1], LN1_EPS), "ve")
        TK(e.memset(cbank[:, 1:2], LN2_EPS), "ve")
        TK(e.memset(cbank[:, 2:3], -0.5), "ve")
        TK(e.memset(cbank[:, 3:4], 1.0 / D), "ve")
    emit("vector", p_ids)
    ve_ids = c.val("ve")

    def p_aexp(e):
        W(e, "md", md_pre)
        TK(e.activation(a_sb, a_sb, AF.Exp), "ac")
    emit("scalar", p_aexp)
    ac_aexp = c.val("ac")

    def p_aneg(e):
        W(e, "ac", ac_aexp)
        SW(e, "ve")
        TK(e.tensor_scalar(a_sb, a_sb, -1.0, None, ALU.mult), "ve")
    emit("vector", p_aneg)

    def p_grep_mm(e):
        W(e, "md", md_pre)
        W(e, "ve", ve_ids)
        e.matmul(pf[2], ones1, lngr[0:1, 0:512], start=True, stop=True)
        e.matmul(pf[3], ones1, lngr[0:1, 512:1024], start=True, stop=True)
        e.matmul(pf[4], ones1, lnbr[0:1, 0:512], start=True, stop=True)
        TK(e.matmul(pf[5], ones1, lnbr[0:1, 512:1024], start=True, stop=True), "pe")
    emit("tensor", p_grep_mm)
    pe_grep = c.val("pe")

    def p_grep_ev(e):
        W(e, "pe", pe_grep)
        SW(e, "ve")
        TK(e.tensor_copy(g_rep[:, 0:512], pf[2]), "ve")
        TK(e.tensor_copy(g_rep[:, 512:1024], pf[3]), "ve")
        TK(e.tensor_copy(b_rep[:, 0:512], pf[4]), "ve")
        TK(e.tensor_copy(b_rep[:, 512:1024], pf[5]), "ve")
    emit("vector", p_grep_ev)

    ar.reset()
    w_in_x = ar.take("w_in_x", ND * I, BF16)
    xbuf = [ar.take(f"x{i}", D, F32) for i in range(5)]
    xnbf = [ar.take(f"xnbf{i}", D, BF16) for i in range(4)]
    xnsq = ar.take("xnsq", D, BF16)
    g1s = ar.take("g1s", D, F32)
    g2s = ar.take("g2s", D, BF16)
    xnf = ar.take("xnf", ND * TT, BF16)
    dgw = ar.take("dgw", 4 * NI * 128, BF16)
    xpe = [ar.take(f"xpe{i}", 516, BF16) for i in range(2)]
    tlb = nc.alloc_sbuf_tensor("tlb", [128, 3 * NI], BF16).ap()
    ubf_off = (ar.off + 63) & ~63
    ubf = ar.take("ubf", NI * TT, BF16)
    cacc = [ar.take(f"cacc{i}", 512, F32) for i in range(2)]
    wstage = ar.at(ubf_off, "wstage", 2 * 2048, F32)

    weight_load_cast(w_in_x, lambda kb: w_in_h.ap()[kb * 128:(kb + 1) * 128, 0:I],
                     wstage, 2048, ND, I)
    ac_wa = c.val("ac")

    def p1a_dgw(e):
        W(e, "md", md_pre)
        SW(e, "ve")
        for k in range(4):
            for ib in range(NI):
                TK(e.tensor_scalar(dgw[:, (k * NI + ib) * 128:(k * NI + ib + 1) * 128],
                                   idbf, cw_sb[:, k * NI + ib:k * NI + ib + 1],
                                   None, ALU.mult), "ve")
    emit("vector", p1a_dgw)
    ve_dgw = c.val("ve")

    def p1a_s1(ti):
        b = ti // (NT // BL)
        t0 = (ti % (NT // BL)) * TT

        for tb in range(NTB):
            sti = ti * NTB + tb
            xb = xbuf[sti % 5]
            xs = f"xd{sti % 5}"

            def ld(e, b=b, t0=t0, tb=tb, xb=xb, sti=sti, xs=xs):
                W(e, "ve", c.v.get(f"a_xn_{sti - 5}", 0))
                e.dma_start(out=xb, in_=x_h.ap()[b, t0 + tb * 128:t0 + (tb + 1) * 128, :]
                            ).then_inc(sem[xs], 16)
            c.v[f"a_x_{sti}"] = c.inc(xs, 16)
            c.v[f"a_xsem_{sti}"] = xs
            emit("sync", ld)

        for tb in range(NTB):
            sti = ti * NTB + tb
            xb = xbuf[sti % 5]
            tvb = (ti % 2) * 24 + tb * 6

            def ve_ln(e, xb=xb, sti=sti, tvb=tvb, tb=tb, ti=ti):
                W(e, c.v[f"a_xsem_{sti}"], c.v[f"a_x_{sti}"])
                if tb == 0:
                    W(e, "ac", c.v.get(f"a_r2t_{ti - 2}", 0))
                SW(e, "ve")
                T = tvec
                TK(e.scalar_tensor_tensor(xnsq, xb, 1.0, xb, ALU.mult, ALU.mult,
                                          accum_out=T[:, tvb:tvb + 1]), "ve")
                TK(e.reduce_sum(T[:, tvb + 1:tvb + 2], xb, AX.X), "ve")
                SW(e, "ve")
                TK(e.tensor_scalar(stash[:, sti:sti + 1], T[:, tvb + 1:tvb + 2],
                                   -1.0 / D, None, ALU.mult), "ve")
                TK(e.tensor_scalar(T[:, tvb + 2:tvb + 3], T[:, tvb:tvb + 1],
                                   1.0 / D, None, ALU.mult), "ve")
                SW(e, "ve")
                TK(e.scalar_tensor_tensor(T[:, tvb + 3:tvb + 4], stash[:, sti:sti + 1],
                                          1.0, stash[:, sti:sti + 1], ALU.mult,
                                          ALU.mult), "ve")
                SW(e, "ve")
                TK(e.tensor_sub(T[:, tvb + 2:tvb + 3], T[:, tvb + 2:tvb + 3],
                                T[:, tvb + 3:tvb + 4]), "ve")
            emit("vector", ve_ln)
            c.v[f"a_var_{sti}"] = c.val("ve")

        def ac_ln1(e, ti=ti):
            for tb in range(NTB):
                sti = ti * NTB + tb
                tvb = (ti % 2) * 24 + tb * 6
                W(e, "ve", c.v[f"a_var_{sti}"])
                SW(e, "ac")
                TK(e.activation(tvec[:, tvb + 3:tvb + 4], tvec[:, tvb + 2:tvb + 3],
                                AF.Ln, bias=cbank[:, 0:1]), "ac")
        emit("scalar", ac_ln1)

        def ac_exp1(e, ti=ti):
            for tb in range(NTB):
                sti = ti * NTB + tb
                tvb = (ti % 2) * 24 + tb * 6
                SW(e, "ac")
                TK(e.activation(stash[:, NST + sti:NST + sti + 1],
                                tvec[:, tvb + 3:tvb + 4], AF.Exp,
                                scale=cbank[:, 2:3]), "ac")
                c.v[f"a_rstd_{sti}"] = c.val("ac")
        emit("scalar", ac_exp1)

    def p1a_s2(ti):
        for tb in range(NTB):
            sti = ti * NTB + tb
            xb = xbuf[sti % 5]
            xn = xnbf[tb]
            tvb = (ti % 2) * 24 + tb * 6

            def ve_xn(e, xb=xb, xn=xn, sti=sti, tvb=tvb):
                W(e, "ac", c.v[f"a_rstd_{sti}"])
                W(e, "pe", c.v.get(f"a_tp_{sti - 4}", 0))
                SW(e, "ve")
                TK(e.tensor_scalar(xn, xb, stash[:, sti:sti + 1],
                                   stash[:, NST + sti:NST + sti + 1],
                                   ALU.add, ALU.mult), "ve")
                SW(e, "ve")
                TK(e.scalar_tensor_tensor(xnsq, xn, 1.0, xn, ALU.mult, ALU.mult,
                                          accum_out=tvec[:, tvb + 4:tvb + 5]), "ve")
            emit("vector", ve_xn)
            c.v[f"a_xn_{sti}"] = c.val("ve")
            c.v[f"a_v2_{sti}"] = c.val("ve")

        def ac_ln2(e, ti=ti):
            for tb in range(NTB):
                sti = ti * NTB + tb
                tvb = (ti % 2) * 24 + tb * 6
                W(e, "ve", c.v[f"a_v2_{sti}"])
                SW(e, "ac")
                TK(e.activation(tvec[:, tvb + 5:tvb + 6], tvec[:, tvb + 4:tvb + 5],
                                AF.Ln, bias=cbank[:, 1:2], scale=cbank[:, 3:4]),
                   "ac")
        emit("scalar", ac_ln2)

        def ac_exp2(e, ti=ti):
            for tb in range(NTB):
                sti = ti * NTB + tb
                tvb = (ti % 2) * 24 + tb * 6
                SW(e, "ac")
                TK(e.activation(stash[:, 2 * NST + sti:2 * NST + sti + 1],
                                tvec[:, tvb + 5:tvb + 6], AF.Exp,
                                scale=cbank[:, 2:3]), "ac")
                c.v[f"a_r2_{sti}"] = c.val("ac")
        emit("scalar", ac_exp2)
        c.v[f"a_r2t_{ti}"] = c.val("ac")

        for tb in range(NTB):
            sti = ti * NTB + tb
            xn = xnbf[tb]
            pbk = pb[tb % 2]

            def pe_tp(e, xn=xn, pbk=pbk, sti=sti, tb=tb, ti=ti):
                W(e, "ve", c.v[f"a_xn_{sti}"])
                W(e, "ac", c.v.get(f"a_tpev_{ti}_{tb - 2}",
                                   c.v.get(f"a_tpev_{ti - 1}_{tb + 2}", 0)))
                for db in range(ND):
                    TK(e.transpose(pbk[:, db * 128:(db + 1) * 128],
                                   xn[:, db * 128:(db + 1) * 128], idbf), "pe")
            emit("tensor", pe_tp)
            c.v[f"a_tp_{sti}"] = c.val("pe")

            def ac_tpev(e, pbk=pbk, tb=tb, sti=sti, ti=ti):
                W(e, "pe", c.v[f"a_tp_{sti}"])
                W(e, "pe", c.v.get(f"a_mm_{ti - 1}_{NI - 1}", 0))
                SW(e, "ac")
                src_ = pbk.rearrange("p (d t) -> p d t", d=ND)
                dst = xnf.rearrange("p (d t) -> p d t", d=ND)[:, :, tb * 128:(tb + 1) * 128]
                TK(e.copy(dst, src_), "ac")
            emit("scalar", ac_tpev)
            c.v[f"a_tpev_{ti}_{tb}"] = c.val("ac")
        c.v[f"a_xnfd_{ti}"] = c.val("ac")

    def p1a_s3(ti):
        first = (ti % (NT // BL)) == 0
        for tb in range(NTB):
            sti = ti * NTB + tb
            xn = xnbf[tb]

            def ve_g(e, xn=xn, sti=sti):
                W(e, "ac", c.v[f"a_r2_{sti}"])
                SW(e, "ve")
                TK(e.scalar_tensor_tensor(g1s, xn,
                                          stash[:, 2 * NST + sti:2 * NST + sti + 1],
                                          g_rep, ALU.mult, ALU.mult), "ve")
                SW(e, "ve")
                TK(e.tensor_add(g2s, g1s, b_rep), "ve")
                SW(e, "ve")
                TK(e.tensor_reduce(grun[:, 1:2], g2s, AX.X, ALU.max,
                                   apply_absolute_value=True), "ve")
                SW(e, "ve")
                TK(e.tensor_tensor(grun[:, 0:1], grun[:, 0:1], grun[:, 1:2],
                                   ALU.max), "ve")
            emit("vector", ve_g)
        ve_xnf_done = c.v[f"a_xnfd_{ti}"]

        for ib in range(NI):
            pwk = pf[ib % 2]

            def pe_mm(e, ib=ib, pwk=pwk, ve_xnf_done=ve_xnf_done, ti=ti):
                W(e, "ac", ac_wa)
                W(e, "ac", ve_xnf_done)
                W(e, "ac", c.v.get(f"a_evx_{ti}_{ib - 2}",
                                   c.v.get(f"a_evx_{ti - 1}_{ib - 2 + NI}", 0)))
                for kb in range(ND):
                    mm = e.matmul(pwk, w_in_x[:, (kb * NI + ib) * 128:(kb * NI + ib + 1) * 128],
                                  xnf[:, kb * TT:(kb + 1) * TT],
                                  start=(kb == 0), stop=(kb == ND - 1))
                TK(mm, "pe")
            emit("tensor", pe_mm)
            c.v[f"a_mm_{ti}_{ib}"] = c.val("pe")

            def ac_evx(e, ib=ib, pwk=pwk, first=first, ti=ti):
                W(e, "pe", c.v[f"a_mm_{ti}_{ib}"])
                W(e, "pe", c.v.get(f"a_cv_{ti}_{ib - 2}", 0))
                SW(e, "ac")
                xp = xpe[ib % 2]
                TK(e.activation(xp[:, 3:515], pwk, AF.Copy), "ac")
                SW(e, "ac")
                if first:
                    TK(e.activation(xp[:, 0:3], xp[:, 0:3], AF.Copy, scale=0.0),
                       "ac")
                else:
                    TK(e.activation(xp[:, 0:3], tlb[:, ib * 3:(ib + 1) * 3],
                                    AF.Copy), "ac")
                SW(e, "ac")
                TK(e.activation(tlb[:, ib * 3:(ib + 1) * 3], xp[:, 512:515],
                                AF.Copy), "ac")
            emit("scalar", ac_evx)
            c.v[f"a_evx_{ti}_{ib}"] = c.val("ac")

            def pe_cv(e, ib=ib, ti=ti):
                W(e, "ac", c.v[f"a_evx_{ti}_{ib}"])
                W(e, "ve", ve_dgw)
                W(e, "ac", c.v.get(f"a_u_{ti}_{ib - 2}", 0))
                xp = xpe[ib % 2]
                pcv = pf[2 + ib % 2]
                for k in range(4):
                    mm = e.matmul(pcv, dgw[:, (k * NI + ib) * 128:(k * NI + ib + 1) * 128],
                                  xp[:, k:k + 512], start=(k == 0), stop=(k == 3))
                TK(mm, "pe")
            emit("tensor", pe_cv)
            c.v[f"a_cv_{ti}_{ib}"] = c.val("pe")

            def ac_silu(e, ib=ib, ti=ti):
                W(e, "pe", c.v[f"a_cv_{ti}_{ib}"])
                W(e, "us", c.v.get(f"a_udma_{ti - 1}", 0))
                SW(e, "ac")
                TK(e.activation(ubf[:, ib * TT:(ib + 1) * TT], pf[2 + ib % 2],
                                AF.Silu, bias=cb_sb[:, ib:ib + 1]), "ac")
            emit("scalar", ac_silu)
            c.v[f"a_u_{ti}_{ib}"] = c.val("ac")

        def st_u(e, ti=ti):
            W(e, "ac", c.v[f"a_u_{ti}_{NI - 1}"])
            e.dma_start(out=u_hbm.ap()[ti * 128:(ti + 1) * 128, :], in_=ubf
                        ).then_inc(sem["us"], 16)
        c.v[f"a_udma_{ti}"] = c.inc("us", 16)
        emit("sync", st_u)

    for k in range(NT + 2):
        if 0 <= k - 2 < NT:
            p1a_s3(k - 2)
        if 0 <= k - 1 < NT:
            p1a_s2(k - 1)
        if k < NT:
            p1a_s1(k)

    emit("sync", lambda e, v=c.val("us"): e.wait_ge(sem["us"], v))
    barrier()

    ar.reset()
    w_x = ar.take("w_x", NI * 1056, BF16)
    w_dt = ar.take("w_dt", ND * I, BF16)
    uin = [ar.take(f"uin{i}", NI * TT, BF16) for i in range(2)]
    dtin = [ar.take(f"dtin{i}", ND * TT, BF16) for i in range(2)]
    bcb = ar.take("bcb", 512, BF16)
    iotae_c = ar.take("iotae", 128, F32)
    eb_c = ar.take("eb", 2 * S * 128, BF16)
    iotae = iotae_c[0:32, :]
    eb = eb_c[0:32, :]
    brep_off = (ar.off + 63) & ~63
    brep = ar.take("brep", S * TT, BF16)
    crep = ar.take("crep", S * TT, BF16)
    wda = ar.take("wda", 4 * 512, F32)
    dub = ar.take("dub", 512, BF16)
    da2 = [ar.take(f"da2{i}", 2 * 512, F32) for i in range(2)]
    dbu2 = ar.take("dbu2", 2 * 512, BF16)
    h2 = [ar.take(f"h2{i}", 2 * 512, BF16) for i in range(2)]
    hk2 = [ar.take(f"hk2{i}", 2 * 512, BF16) for i in range(2)]
    y2st = ar.take("y2st", 4 * 512, BF16)
    wstage_b = ar.at(brep_off, "wstage_b", 2 * 2048, F32)

    def b_iota(e):
        e.iota(iotae, [[0, 128]], channel_multiplier=1,
               allow_small_or_imprecise_dtypes=True).then_inc(sem["gp"], 1)
    gp_eb = c.inc("gp", 1)
    emit("gpsimd", b_iota)

    def b_eb(e):
        W(e, "gp", gp_eb)
        SW(e, "ve")
        for s in range(2 * S):
            TK(e.tensor_scalar(eb[:, s * 128:(s + 1) * 128], iotae, float(s), None,
                              ALU.is_equal), "ve")
    emit("vector", b_eb)

    weight_load_cast(w_x, lambda kb: w_x_h.ap()[kb * 128:(kb + 1) * 128, :],
                     wstage_b, 2048, NI, 1056)
    weight_load_cast(w_dt, lambda kb: w_dt_h.ap()[kb * 128:(kb + 1) * 128, :],
                     wstage_b, 2048, ND, I)
    ve_wb = c.val("ve")
    ac_wb = c.val("ac")

    def p1b_init(e):
        SW(e, "ve")
        TK(e.memset(hc, 0.0), "ve")
    emit("vector", p1b_init)

    def emit_ld_u(ti):
        ub_ = uin[ti % 2]
        usem = f"ud{ti % 2}"

        def ld_u(e, ti=ti, ub_=ub_, usem=usem):
            W(e, "ac", c.v.get(f"b_y2_{ti - 2}_{NI - 1}", 0))
            W(e, "ve", c.v.get(f"b_du_{ti - 2}_{NI - 1}", 0))
            e.dma_start(out=ub_, in_=u_hbm.ap()[ti * 128:(ti + 1) * 128, :]
                        ).then_inc(sem[usem], 16)
        c.v[f"b_u_{ti}"] = c.inc(usem, 16)
        c.v[f"b_usem_{ti}"] = usem
        emit("sync", ld_u)

    def emit_wx_one(ti, nb):
        ub_ = uin[ti % 2]
        dtb = dtin[ti % 2]
        pwk = pf[nb % 2]

        def pe_wx(e, nb=nb, pwk=pwk, ti=ti, ub_=ub_):
            W(e, "ac", ac_wb)
            W(e, c.v[f"b_usem_{ti}"], c.v[f"b_u_{ti}"])
            W(e, "ac", c.v.get(f"b_dtin_{ti}_{nb - 2}",
                               c.v.get(f"b_dtin_{ti - 1}_{ND - 1}", 0)))
            for kb in range(NI):
                mm = e.matmul(pwk, w_x[:, kb * 1056 + nb * 128:kb * 1056 + (nb + 1) * 128],
                              ub_[:, kb * TT:(kb + 1) * TT],
                              start=(kb == 0), stop=(kb == NI - 1))
            TK(mm, "pe")
        emit("tensor", pe_wx)
        c.v[f"b_wx_{ti}_{nb}"] = c.val("pe")

        def ac_dtin(e, nb=nb, pwk=pwk, ti=ti, dtb=dtb):
            W(e, "pe", c.v[f"b_wx_{ti}_{nb}"])
            if nb == 0:
                W(e, "pe", c.v.get(f"b_wdt_{ti - 2}_{NI - 1}", 0))
            SW(e, "ac")
            TK(e.copy(dtb[:, nb * TT:(nb + 1) * TT], pwk), "ac")
        emit("scalar", ac_dtin)
        c.v[f"b_dtin_{ti}_{nb}"] = c.val("ac")

    emit_ld_u(0)
    for nb0 in range(ND):
        emit_wx_one(0, nb0)

    for ti in range(NT):
        ub = uin[ti % 2]
        first = (ti % (NT // BL)) == 0
        dtb = dtin[ti % 2]

        if first and ti > 0:
            def rhc(e):
                SW(e, "ve")
                TK(e.memset(hc, 0.0), "ve")
            emit("vector", rhc)

        def pe_bc(e, ti=ti):
            W(e, "ve", c.v.get(f"b_bcev_{ti - 1}", 0))
            W(e, "ac", c.v.get(f"b_y2_{ti - 1}_{NI - 2}", 0))
            for kb in range(NI):
                mm = e.matmul(pf[4][0:32, :], w_x[:, kb * 1056 + 1024:kb * 1056 + 1056],
                              ub[:, kb * TT:(kb + 1) * TT],
                              start=(kb == 0), stop=(kb == NI - 1))
            TK(mm, "pe")
        emit("tensor", pe_bc)
        c.v[f"b_bc_{ti}"] = c.val("pe")

        def ve_bcev(e, ti=ti):
            W(e, "pe", c.v[f"b_bc_{ti}"])
            SW(e, "ve")
            TK(e.tensor_copy(bcb[0:32, 0:512], pf[4][0:32, :]), "ve")
        emit("vector", ve_bcev)
        c.v[f"b_bcev_{ti}"] = c.val("ve")

        for s in range(S):
            def pe_br(e, s=s, ti=ti):
                W(e, "ve", c.v[f"b_bcev_{ti}"])
                W(e, "ve", c.v.get(f"b_brev_{ti}_{s - 1}", 0))
                if s == 0:
                    W(e, "ac", c.v.get(f"b_y2_{ti - 1}_{NI - 1}", 0))
                e.matmul(pf[4], eb[:, s * 128:(s + 1) * 128], bcb[0:32, 0:512],
                         start=True, stop=True)
                TK(e.matmul(pf[5], eb[:, (S + s) * 128:(S + s + 1) * 128],
                           bcb[0:32, 0:512], start=True, stop=True), "pe")
            emit("tensor", pe_br)
            c.v[f"b_br_{ti}_{s}"] = c.val("pe")

            def ve_brev(e, s=s, ti=ti):
                W(e, "pe", c.v[f"b_br_{ti}_{s}"])
                SW(e, "ve")
                TK(e.tensor_copy(brep[:, s * TT:(s + 1) * TT], pf[4]), "ve")
                TK(e.tensor_copy(crep[:, s * TT:(s + 1) * TT], pf[5]), "ve")
            emit("vector", ve_brev)
            c.v[f"b_brev_{ti}_{s}"] = c.val("ve")
        ve_brep_done = c.val("ve")

        def SCNv(tj, ib, g):
            while g < 0:
                g += 8
                ib -= 1
            while ib < 0:
                ib += NI
                tj -= 1
            if tj < 0:
                return 0
            return c.v.get(f"b_scan_{tj}_{ib}_{g}", 0)

        def YMv(tj, ib, g):
            while g < 0:
                g += 8
                ib -= 1
            while ib < 0:
                ib += NI
                tj -= 1
            if tj < 0:
                return 0
            return c.v.get(f"b_ym_{tj}_{ib}_{g}", 0)

        def IDv(tj, ib, g):
            while g < 0:
                g += 8
                ib -= 1
            while ib < 0:
                ib += NI
                tj -= 1
            if tj < 0:
                return 0
            return c.v.get(f"b_idmm_{tj}_{ib}_{g}", 0)

        for q in range(4):
            for ib in range(4 * q, 4 * q + 4):
                pwk = pf[2 + ib % 2]

                def pe_wdt(e, ib=ib, pwk=pwk, ti=ti, dtb=dtb):
                    W(e, "ac", c.v[f"b_dtin_{ti}_{ND - 1}"])
                    W(e, "ac", c.v.get(f"b_sp_{ti}_{ib - 2}",
                                       c.v.get(f"b_sp_{ti - 1}_{ib + 14}", 0)))
                    for kb in range(ND):
                        mm = e.matmul(pwk, w_dt[:, kb * I + ib * 128:kb * I + (ib + 1) * 128],
                                      dtb[:, kb * TT:(kb + 1) * TT],
                                      start=(kb == 0), stop=(kb == ND - 1))
                    TK(mm, "pe")
                emit("tensor", pe_wdt)
                c.v[f"b_wdt_{ti}_{ib}"] = c.val("pe")

                def ac_spexp(e, ib=ib, pwk=pwk, ti=ti):
                    W(e, "pe", c.v[f"b_wdt_{ti}_{ib}"])
                    W(e, "ve", c.v.get(f"b_du_{ti}_{ib - 4}",
                                       c.v.get(f"b_du_{ti - 1}_{ib + 12}", 0)))
                    SW(e, "ac")
                    TK(e.activation(wda[:, (ib % 4) * 512:(ib % 4 + 1) * 512], pwk,
                                    AF.Exp, bias=bdt_sb[:, ib:ib + 1]), "ac")
                emit("scalar", ac_spexp)
                c.v[f"b_sp_{ti}_{ib}"] = c.val("ac")

            def ac_lnq(e, q=q, ti=ti):
                SW(e, "ac")
                TK(e.activation(wda, wda, AF.Ln, bias=1.0), "ac")
            emit("scalar", ac_lnq)
            c.v[f"b_lnq_{ti}_{q}"] = c.val("ac")

            for ib in range(4 * q, 4 * q + 4):
                dsl = wda[:, (ib % 4) * 512:(ib % 4 + 1) * 512]

                def ve_du(e, ib=ib, ti=ti, q=q, dsl=dsl):
                    W(e, "ac", c.v[f"b_lnq_{ti}_{q}"])
                    W(e, c.v[f"b_usem_{ti}"], c.v[f"b_u_{ti}"])
                    SW(e, "ve")
                    TK(e.tensor_tensor(dub, dsl, ub[:, ib * TT:(ib + 1) * TT],
                                       ALU.mult), "ve")
                emit("vector", ve_du)
                c.v[f"b_du_{ti}_{ib}"] = c.val("ve")

                for g in range(8):
                    dab = da2[g % 2]

                    def ac_da(e, ib=ib, g=g, dab=dab, ti=ti, dsl=dsl):
                        W(e, "ve", SCNv(ti, ib, g - 2))
                        SW(e, "ac")
                        for s in range(2):
                            TK(e.activation(dab[:, s * 512:(s + 1) * 512], dsl, AF.Exp,
                                            scale=a_sb[:, ib * S + g * 2 + s:ib * S + g * 2 + s + 1]),
                               "ac")
                    emit("scalar", ac_da)
                    c.v[f"b_da_{ti}_{ib}_{g}"] = c.val("ac")

                    hb = h2[g % 2]
                    hck = hk2[g % 2]

                    def ve_sc(e, ib=ib, g=g, dab=dab, hb=hb, hck=hck, ti=ti):
                        W(e, "ac", c.v[f"b_da_{ti}_{ib}_{g}"])
                        W(e, "pe", IDv(ti, ib, g - 2))
                        SW(e, "ve")
                        for s in range(2):
                            TK(e.tensor_tensor(dbu2[:, s * 512:(s + 1) * 512], dub,
                                               brep[:, (g * 2 + s) * TT:(g * 2 + s + 1) * TT],
                                               ALU.mult), "ve")
                        SW(e, "ve")
                        for s in range(2):
                            TK(e.tensor_tensor_scan(
                                hb[:, s * 512:(s + 1) * 512],
                                dab[:, s * 512:(s + 1) * 512],
                                dbu2[:, s * 512:(s + 1) * 512],
                                hc[:, ib * S + g * 2 + s:ib * S + g * 2 + s + 1],
                                ALU.mult, ALU.add), "ve")
                        SW(e, "ve")
                        TK(e.tensor_copy(hc[:, ib * S + g * 2:ib * S + g * 2 + 2],
                                         hb.rearrange("p (s t) -> p s t", s=2)[:, :, 511]),
                           "ve")
                        TK(e.tensor_tensor(hck, hb,
                                           crep[:, (g * 2) * TT:(g * 2 + 2) * TT],
                                           ALU.mult), "ve")
                    emit("vector", ve_sc)
                    c.v[f"b_scan_{ti}_{ib}_{g}"] = c.val("ve")
                    c.v[f"b_ym_{ti}_{ib}_{g}"] = c.val("ve")

                    pyk = pf[4 + (ib % 2)]

                    def pe_id(e, ib=ib, g=g, hck=hck, pyk=pyk, ti=ti):
                        W(e, "ve", c.v[f"b_ym_{ti}_{ib}_{g}"])
                        if g == 0 and ib >= 2:
                            W(e, "ac", c.v[f"b_y2_{ti}_{ib - 2}"])
                        for s in range(2):
                            mm = e.matmul(pyk, idbf, hck[:, s * 512:(s + 1) * 512],
                                          start=(g == 0 and s == 0), stop=False)
                        if g == 7:
                            mm = e.matmul(pyk, idbf, ub[:, ib * TT:(ib + 1) * TT],
                                          start=False, stop=True)
                        TK(mm, "pe")
                    emit("tensor", pe_id)
                    c.v[f"b_idmm_{ti}_{ib}_{g}"] = c.val("pe")

                def ac_y2(e, ib=ib, ti=ti):
                    pyk = pf[4 + (ib % 2)]
                    W(e, "pe", c.v[f"b_idmm_{ti}_{ib}_{7}"])
                    W(e, "ys", c.v.get(f"b_ydma_{ti}_{ib // 4 - 1}",
                                       c.v.get(f"b_ydma_{ti - 1}_{3}", 0)))
                    SW(e, "ac")
                    TK(e.copy(y2st[:, (ib % 4) * 512:(ib % 4 + 1) * 512], pyk), "ac")
                emit("scalar", ac_y2)
                c.v[f"b_y2_{ti}_{ib}"] = c.val("ac")

                if ib % 4 == 3:
                    def st_y2(e, ib=ib, ti=ti):
                        W(e, "ac", c.v[f"b_y2_{ti}_{ib}"])
                        e.dma_start(
                            out=y2_hbm.ap()[ti * 128:(ti + 1) * 128,
                                            (ib - 3) * TT:(ib + 1) * TT],
                            in_=y2st).then_inc(sem["ys"], 16)
                    c.v[f"b_ydma_{ti}_{ib // 4}"] = c.inc("ys", 16)
                    emit("sync", st_y2)

                if ti + 1 < NT and 4 <= ib < 12:
                    emit_wx_one(ti + 1, ib - 4)

            if q == 0 and ti + 1 < NT:
                emit_ld_u(ti + 1)

    emit("sync", lambda e, v=c.val("ys"): e.wait_ge(sem["ys"], v))
    barrier()

    ar.reset()
    w_in_z = ar.take("w_in_z", ND * I, BF16)
    w_o = ar.take("w_o", NI * D, BF16)
    wq = ar.take("wq", ND * D, BF16)
    xb2 = [ar.take(f"x2_{i}", D, F32) for i in range(5)]
    xnbf2 = [ar.take(f"xnbf2_{i}", D, BF16) for i in range(2)]
    xnf2 = ar.take("xnf2", ND * TT, BF16)
    xqf = ar.take("xqf", ND * TT, BF16)
    y2b_off = (ar.off + 63) & ~63
    y2b = ar.take("y2b", NI * TT, BF16)
    yop = ar.take("yop", NI * TT, BF16)
    szp = [ar.take(f"szp{i}", 512, BF16) for i in range(2)]
    xq_t = [ar.take(f"xqt{i}", D, BF16) for i in range(2)]
    q1s = ar.take("q1s", D, F32)
    attn_sb = ar.take("attn_sb", 512, F32)
    o_f = ar.take("o_f", 512, F32)
    outsb = [ar.take(f"outsb{i}", D, F32) for i in range(4)]
    wstage2 = ar.at(y2b_off, "wstage2", 2 * 2048, F32)
    fwstage = ar.at(y2b_off + 16 * KB, "fwstage", D, F32)
    awstage = ar.at(y2b_off + 20 * KB, "awstage", D, F32)
    bsum = ar.take("bsum", 16, F32)

    weight_load_cast(w_in_z, lambda kb: w_in_h.ap()[kb * 128:(kb + 1) * 128, I:2 * I],
                     wstage2, 2048, ND, I)
    weight_load_cast(w_o, lambda kb: w_out_h.ap()[kb * 128:(kb + 1) * 128, :],
                     wstage2, 2048, NI, D)
    ve_w2 = c.val("ve")
    ac_w2 = c.val("ac")

    def g_dma(e):
        e.dma_start(out=smll[0:1, 0:128], in_=grun[:, 0:1]).then_inc(sem["md"], 16)
    md_g = c.inc("md", 16)
    emit("sync", g_dma)

    def g_fin(e, md_g=md_g):
        W(e, "md", md_g)
        SW(e, "ve")
        TK(e.tensor_reduce(tvec[0:1, 0:1], smll[0:1, 0:128], AX.X, ALU.max), "ve")
        SW(e, "ve")
        TK(e.reciprocal(tvec[0:1, 1:2], tvec[0:1, 0:1]), "ve")
        SW(e, "ve")
        TK(e.tensor_scalar(tvec[0:1, 1:2], tvec[0:1, 1:2], QB, None, ALU.mult), "ve")
    emit("vector", g_fin)

    for kb in range(ND):
        def fw_ld(e, kb=kb):
            W(e, "ve", c.v.get(f"c_abs_{kb - 1}", 0))
            e.dma_start(out=fwstage, in_=fw_h.ap()[kb * 128:(kb + 1) * 128, :]
                        ).then_inc(sem["fs"], 16)
        wd_v = c.inc("fs", 16)
        emit("sync", fw_ld)

        def ve_abs(e, kb=kb, wd_v=wd_v):
            W(e, "fs", wd_v)
            SW(e, "ve")
            TK(e.scalar_tensor_tensor(awstage, fwstage, -1.0, fwstage,
                                     ALU.mult, ALU.max), "ve")
            SW(e, "ve")
            TK(e.tensor_reduce(bsum[:, kb:kb + 1], awstage, AX.X, ALU.add), "ve")
        emit("vector", ve_abs)
        c.v[f"c_abs_{kb}"] = c.val("ve")

    def ve_bsum(e):
        SW(e, "ve")
        TK(e.reduce_sum(bsum[:, 8:9], bsum[:, 0:8], AX.X), "ve")
    emit("vector", ve_bsum)
    ve_bs = c.val("ve")

    def pe_bsum(e, ve_bs=ve_bs):
        W(e, "ve", ve_bs)
        TK(e.matmul(pf[2][0:1, 0:1], onesc, bsum[:, 8:9], start=True, stop=True),
          "pe")
    emit("tensor", pe_bsum)
    pe_bs = c.val("pe")

    def ve_beta(e, pe_bs=pe_bs):
        W(e, "pe", pe_bs)
        SW(e, "ve")
        TK(e.tensor_scalar(tvec[0:1, 2:3], pf[2][0:1, 0:1], 1.0 / (D * D), None,
                          ALU.mult), "ve")
        SW(e, "ve")
        TK(e.tensor_scalar(tvec[0:1, 3:4], tvec[0:1, 2:3], W_EPS, 0.5,
                          ALU.add, ALU.mult), "ve")
        TK(e.tensor_tensor(tvec[0:1, 4:5], tvec[0:1, 2:3], tvec[0:1, 0:1],
                          ALU.mult), "ve")
        SW(e, "ve")
        TK(e.tensor_scalar(tvec[0:1, 4:5], tvec[0:1, 4:5], 1.0 / QB, None,
                          ALU.mult), "ve")
    emit("vector", ve_beta)
    ve_beta_v = c.val("ve")

    def pe_bcastv(e, ve_beta_v=ve_beta_v):
        W(e, "ve", ve_beta_v)
        TK(e.matmul(pf[2][:, 0:5], ones1, tvec[0:1, 0:5], start=True, stop=True),
          "pe")
    emit("tensor", pe_bcastv)
    pe_bv = c.val("pe")

    def ve_vecs(e, pe_bv=pe_bv):
        W(e, "pe", pe_bv)
        SW(e, "ve")
        TK(e.tensor_copy(vec4[:, 0:1], pf[2][:, 1:2]), "ve")
        TK(e.tensor_copy(vec4[:, 1:2], pf[2][:, 3:4]), "ve")
        TK(e.tensor_copy(vec4[:, 2:3], pf[2][:, 4:5]), "ve")
        TK(e.tensor_copy(vec4[:, 3:4], pf[2][:, 2:3]), "ve")
        SW(e, "ve")
        TK(e.tensor_scalar(g_rep, g_rep, vec4[:, 0:1], None, ALU.mult), "ve")
        TK(e.tensor_scalar(b_rep, b_rep, vec4[:, 0:1], None, ALU.mult), "ve")
    emit("vector", ve_vecs)
    ve_vecs_v = c.val("ve")

    for kb in range(ND):
        def fw_ld2(e, kb=kb):
            W(e, "ve", c.v.get(f"c_wq_{kb - 1}", ve_vecs_v))
            W(e, "ac", c.v.get(f"c_sgn_{kb - 1}", 0))
            e.dma_start(out=fwstage, in_=fw_h.ap()[kb * 128:(kb + 1) * 128, :]
                        ).then_inc(sem["fs"], 16)
        wd_v = c.inc("fs", 16)
        emit("sync", fw_ld2)

        def ac_sgn(e, kb=kb, wd_v=wd_v):
            W(e, "fs", wd_v)
            SW(e, "ac")
            TK(e.activation(awstage, fwstage, AF.Sign), "ac")
        emit("scalar", ac_sgn)
        c.v[f"c_sgn_{kb}"] = c.val("ac")

        def ve_wq(e, kb=kb, wd_v=wd_v):
            W(e, "fs", wd_v)
            W(e, "ve", ve_vecs_v)
            W(e, "ac", c.v[f"c_sgn_{kb}"])
            SW(e, "ve")
            TK(e.scalar_tensor_tensor(q1s, fwstage, -1.0, fwstage, ALU.mult,
                                     ALU.max), "ve")
            SW(e, "ve")
            TK(e.tensor_scalar(q1s, q1s, vec4[:, 1:2], None, ALU.is_gt), "ve")
            SW(e, "ve")
            TK(e.tensor_tensor(wq[:, kb * D:(kb + 1) * D], q1s, awstage, ALU.mult),
              "ve")
        emit("vector", ve_wq)
        c.v[f"c_wq_{kb}"] = c.val("ve")

    for ti in range(NT):
        b = ti // (NT // BL)
        t0 = (ti % (NT // BL)) * TT

        for tb in range(NTB):
            xb = xb2[(ti * NTB + tb) % 5]
            xs = f"xd{(ti * NTB + tb) % 5}"

            def ld(e, b=b, t0=t0, tb=tb, xb=xb, ti=ti, xs=xs):
                W(e, "ve", c.v.get(f"c_oadd_{ti - 1}", 0))
                e.dma_start(out=xb, in_=x_h.ap()[b, t0 + tb * 128:t0 + (tb + 1) * 128, :]
                            ).then_inc(sem[xs], 16)
            c.v[f"c_x_{ti}_{tb}"] = c.inc(xs, 16)
            c.v[f"c_xsem_{ti}_{tb}"] = xs
            emit("sync", ld)

        def ld_y2(e, ti=ti):
            W(e, "pe", c.v.get(f"c_attn_{ti - 1}_{ND - 1}", 0))
            W(e, "ve", c.v[f"c_wq_{ND - 1}"])
            e.dma_start(out=y2b, in_=y2_hbm.ap()[ti * 128:(ti + 1) * 128, :]
                        ).then_inc(sem["yl"], 16)
        c.v[f"c_y2_{ti}"] = c.inc("yl", 16)
        emit("sync", ld_y2)

        for tb in range(NTB):
            sti = ti * NTB + tb
            xb = xb2[sti % 5]
            xn = xnbf2[tb % 2]

            def ve_xn2(e, xb=xb, xn=xn, sti=sti, ti=ti, tb=tb):
                W(e, c.v[f"c_xsem_{ti}_{tb}"], c.v[f"c_x_{ti}_{tb}"])
                W(e, "pe", c.v.get(f"c_tp_{ti}_{tb - 2}", 0))
                SW(e, "ve")
                TK(e.tensor_scalar(xn, xb, stash[:, sti:sti + 1],
                                  stash[:, NST + sti:NST + sti + 1],
                                  ALU.add, ALU.mult), "ve")
            emit("vector", ve_xn2)
            c.v[f"c_xn_{ti}_{tb}"] = c.val("ve")

            pbk = pb[tb % 2]

            def pe_tp(e, xn=xn, pbk=pbk, ti=ti, tb=tb):
                W(e, "ve", c.v[f"c_xn_{ti}_{tb}"])
                W(e, "ve", c.v.get(f"c_tpev_{ti}_{tb - 2}", 0))
                for db in range(ND):
                    TK(e.transpose(pbk[:, db * 128:(db + 1) * 128],
                                  xn[:, db * 128:(db + 1) * 128], idbf), "pe")
            emit("tensor", pe_tp)
            c.v[f"c_tp_{ti}_{tb}"] = c.val("pe")

            def ve_tpev(e, pbk=pbk, tb=tb, ti=ti):
                W(e, "pe", c.v[f"c_tp_{ti}_{tb}"])
                SW(e, "ve")
                src = pbk.rearrange("p (d t) -> p d t", d=ND)
                dst = xnf2.rearrange("p (d t) -> p d t", d=ND)[:, :, tb * 128:(tb + 1) * 128]
                TK(e.tensor_copy(dst, src), "ve")
            emit("vector", ve_tpev)
            c.v[f"c_tpev_{ti}_{tb}"] = c.val("ve")

        for ib in range(NI):
            pwk = pf[ib % 2]

            def pe_z(e, ib=ib, pwk=pwk, ti=ti):
                W(e, "ac", ac_w2)
                W(e, "ve", c.v[f"c_tpev_{ti}_{NTB - 1}"])
                W(e, "ac", c.v.get(f"c_sz_{ti}_{ib - 2}",
                                   c.v.get(f"c_attnev_{ti - 1}_{ND - 1}", 0)))
                for kb in range(ND):
                    mm = e.matmul(pwk, w_in_z[:, (kb * NI + ib) * 128:(kb * NI + ib + 1) * 128],
                                  xnf2[:, kb * TT:(kb + 1) * TT],
                                  start=(kb == 0), stop=(kb == ND - 1))
                TK(mm, "pe")
            emit("tensor", pe_z)
            c.v[f"c_z_{ti}_{ib}"] = c.val("pe")

            def ac_sz(e, ib=ib, pwk=pwk, ti=ti):
                W(e, "pe", c.v[f"c_z_{ti}_{ib}"])
                W(e, "ve", c.v.get(f"c_yopk_{ti}_{ib - 2}", 0))
                SW(e, "ac")
                TK(e.activation(szp[ib % 2], pwk, AF.Silu), "ac")
            emit("scalar", ac_sz)
            c.v[f"c_sz_{ti}_{ib}"] = c.val("ac")

            def ve_yop(e, ib=ib, ti=ti):
                W(e, "ac", c.v[f"c_sz_{ti}_{ib}"])
                W(e, "yl", c.v[f"c_y2_{ti}"])
                SW(e, "ve")
                TK(e.tensor_tensor(yop[:, ib * TT:(ib + 1) * TT],
                                  y2b[:, ib * TT:(ib + 1) * TT], szp[ib % 2],
                                  ALU.mult), "ve")
            emit("vector", ve_yop)
            c.v[f"c_yopk_{ti}_{ib}"] = c.val("ve")
        c.v[f"c_yop_{ti}"] = c.val("ve")

        for tb in range(NTB):
            sti = ti * NTB + tb
            xn = xnbf2[tb % 2]
            xqt = xq_t[tb % 2]

            def ve_xq(e, xn=xn, sti=sti, xqt=xqt, ti=ti, tb=tb):
                xb = xb2[sti % 5]
                W(e, "pe", c.v.get(f"c_tpq_{ti}_{tb - 2}", 0))
                SW(e, "ve")
                TK(e.tensor_scalar(xn, xb, stash[:, sti:sti + 1],
                                  stash[:, NST + sti:NST + sti + 1],
                                  ALU.add, ALU.mult), "ve")
                SW(e, "ve")
                TK(e.scalar_tensor_tensor(q1s, xn,
                                         stash[:, 2 * NST + sti:2 * NST + sti + 1],
                                         g_rep, ALU.mult, ALU.mult), "ve")
                SW(e, "ve")
                TK(e.tensor_add(q1s, q1s, b_rep), "ve")
                SW(e, "ve")
                TK(e.tensor_scalar(xqt, q1s, CLIPV, -CLIPV, ALU.min, ALU.max), "ve")
            emit("vector", ve_xq)
            c.v[f"c_xq_{ti}_{tb}"] = c.val("ve")

            pbk = pb[tb % 2]

            def pe_tpq(e, xqt=xqt, pbk=pbk, ti=ti, tb=tb):
                W(e, "ve", c.v[f"c_xq_{ti}_{tb}"])
                W(e, "ve", c.v.get(f"c_tpqev_{ti}_{tb - 2}", 0))
                for db in range(ND):
                    TK(e.transpose(pbk[:, db * 128:(db + 1) * 128],
                                  xqt[:, db * 128:(db + 1) * 128], idbf), "pe")
            emit("tensor", pe_tpq)
            c.v[f"c_tpq_{ti}_{tb}"] = c.val("pe")

            def ve_tpqev(e, pbk=pbk, tb=tb, ti=ti):
                W(e, "pe", c.v[f"c_tpq_{ti}_{tb}"])
                SW(e, "ve")
                src = pbk.rearrange("p (d t) -> p d t", d=ND)
                dst = xqf.rearrange("p (d t) -> p d t", d=ND)[:, :, tb * 128:(tb + 1) * 128]
                TK(e.tensor_copy(dst, src), "ve")
            emit("vector", ve_tpqev)
            c.v[f"c_tpqev_{ti}_{tb}"] = c.val("ve")

        for db in range(ND):
            pa = pf[db % 2]
            pq = pf[2 + db % 2]
            ptr = pf[4 + db % 2]

            def pe_attn(e, db=db, pa=pa, ti=ti):
                W(e, "ve", c.v[f"c_yop_{ti}"])
                W(e, "ac", c.v.get(f"c_attnev_{ti}_{db - 2}",
                                   c.v.get(f"c_sz_{ti}_{NI - 1}", 0)))
                for kb in range(NI):
                    mm = e.matmul(pa, w_o[:, kb * D + db * 128:kb * D + (db + 1) * 128],
                                  yop[:, kb * TT:(kb + 1) * TT],
                                  start=(kb == 0), stop=(kb == NI - 1))
                TK(mm, "pe")
            emit("tensor", pe_attn)
            c.v[f"c_attn_{ti}_{db}"] = c.val("pe")

            def pe_ffn(e, db=db, pq=pq, ti=ti):
                W(e, "ve", c.v[f"c_tpqev_{ti}_{NTB - 1}"])
                W(e, "ve", c.v.get(f"c_of_{ti}_{db - 2}",
                                   c.v.get(f"c_of_{ti - 1}_{ND - 1}", 0)))
                for kb in range(ND):
                    mm = e.matmul(pq, wq[:, kb * D + db * 128:kb * D + (db + 1) * 128],
                                  xqf[:, kb * TT:(kb + 1) * TT],
                                  start=(kb == 0), stop=(kb == ND - 1))
                TK(mm, "pe")
            emit("tensor", pe_ffn)
            c.v[f"c_ffn_{ti}_{db}"] = c.val("pe")

            def ac_attnev(e, pa=pa, ti=ti, db=db):
                W(e, "pe", c.v[f"c_attn_{ti}_{db}"])
                W(e, "ve", c.v.get(f"c_of_{ti}_{db - 1}", 0))
                SW(e, "ac")
                TK(e.copy(attn_sb, pa), "ac")
            emit("scalar", ac_attnev)
            c.v[f"c_attnev_{ti}_{db}"] = c.val("ac")

            def ve_of(e, pq=pq, ti=ti, db=db):
                W(e, "pe", c.v[f"c_ffn_{ti}_{db}"])
                W(e, "ac", c.v[f"c_attnev_{ti}_{db}"])
                W(e, "pe", c.v.get(f"c_tro_{ti}_{db - 1}", 0))
                SW(e, "ve")
                TK(e.scalar_tensor_tensor(o_f, pq, vec4[:, 2:3], attn_sb,
                                         ALU.mult, ALU.add), "ve")
            emit("vector", ve_of)
            c.v[f"c_of_{ti}_{db}"] = c.val("ve")

            def pe_tro(e, db=db, ptr=ptr, ti=ti):
                W(e, "ve", c.v[f"c_of_{ti}_{db}"])
                W(e, "ve", c.v.get(f"c_oaddp_{ti}_{db - 2}",
                                   c.v.get(f"c_oaddp_{ti - 1}_{ND - 1}", 0)))
                for tb in range(NTB):
                    TK(e.transpose(ptr[:, tb * 128:(tb + 1) * 128],
                                  o_f[:, tb * 128:(tb + 1) * 128], idf), "pe")
            emit("tensor", pe_tro)
            c.v[f"c_tro_{ti}_{db}"] = c.val("pe")

            def ve_outadd(e, db=db, ptr=ptr, ti=ti):
                W(e, "pe", c.v[f"c_tro_{ti}_{db}"])
                if db == 0:
                    for q in range(NTB):
                        W(e, f"od{q}", c.v.get(f"c_odma_{ti - 1}_{q}", 0))
                SW(e, "ve")
                for tb in range(NTB):
                    xb = xb2[(ti * NTB + tb) % 5]
                    TK(e.tensor_tensor(outsb[tb][:, db * 128:(db + 1) * 128],
                                      xb[:, db * 128:(db + 1) * 128],
                                      ptr[:, tb * 128:(tb + 1) * 128], ALU.add),
                      "ve")
            emit("vector", ve_outadd)
            c.v[f"c_oaddp_{ti}_{db}"] = c.val("ve")
        c.v[f"c_oadd_{ti}"] = c.val("ve")

        for tb in range(NTB):
            def st_out(e, tb=tb, b=b, t0=t0, ti=ti):
                W(e, "ve", c.v[f"c_oadd_{ti}"])
                e.dma_start(out=out_h.ap()[b, t0 + tb * 128:t0 + (tb + 1) * 128, :],
                            in_=outsb[tb]).then_inc(sem[f"od{tb}"], 16)
            c.v[f"c_odma_{ti}_{tb}"] = c.inc(f"od{tb}", 16)
            emit("sync", st_out)

    def fin(e):
        for q in range(NTB):
            W(e, f"od{q}", c.val(f"od{q}"))
    emit("sync", fin)

    flush_sections()
    es.close()
    return nc


def kernel(**inputs):
    x = np.ascontiguousarray(np.asarray(inputs["L0_attn_inputs"], np.float32))
    nc = build_program()
    common = dict(
        w_in=np.ascontiguousarray(inputs["L0_attn_W_in"], np.float32),
        conv_w=np.ascontiguousarray(inputs["L0_attn_conv_w"], np.float32),
        conv_b=np.ascontiguousarray(inputs["L0_attn_conv_b"], np.float32),
        w_x=np.ascontiguousarray(inputs["L0_attn_W_x"], np.float32),
        w_dt=np.ascontiguousarray(inputs["L0_attn_W_dt"], np.float32),
        b_dt=np.ascontiguousarray(inputs["L0_attn_b_dt"], np.float32),
        a_log=np.ascontiguousarray(inputs["L0_attn_A_log"], np.float32),
        w_out=np.ascontiguousarray(inputs["L0_attn_W_out"], np.float32),
        fw=np.ascontiguousarray(inputs["L0_ffn_weight"], np.float32),
        lng=np.ascontiguousarray(inputs["L0_ffn_ln_gamma"], np.float32),
        lnb=np.ascontiguousarray(inputs["L0_ffn_ln_beta"], np.float32),
    )
    in_maps = [dict(common, x=np.ascontiguousarray(x[2 * cc:2 * cc + 2]))
               for cc in range(NCORES)]
    res = run_bass_kernel_spmd(nc, in_maps, core_ids=list(range(NCORES)))
    return np.concatenate([r["out"] for r in res.results], axis=0)


if __name__ == "__main__":
    build_program()
    print("build OK")


# version 17
# speedup vs baseline: 1.1388x; 1.0301x over previous
import contextlib
import numpy as np
import concourse.bass as bass
import concourse.mybir as mybir
from concourse.bass_utils import run_bass_kernel_spmd

F32 = mybir.dt.float32
BF16 = mybir.dt.bfloat16
U8 = mybir.dt.uint8
AF = mybir.ActivationFunctionType
ALU = mybir.AluOpType
AX = mybir.AxisListType

NCORES = 8
B, T, D = 16, 2048, 1024
BL = B // NCORES
I = 2048
S = 16
TT = 512
NT = BL * T // TT
NTB = TT // 128
ND = D // 128
NI = I // 128
NST = NT * NTB
QB = 128.0
CLIPV = float(np.float32(QB - 1e-5))
LN1_EPS = 1e-5
LN2_EPS = 1e-3
W_EPS = 1e-5
KB = 1024


class Ctr:
    def __init__(self):
        self.v = {}

    def inc(self, name, n):
        self.v[name] = self.v.get(name, 0) + n
        return self.v[name]

    def val(self, name):
        return self.v.get(name, 0)


class Carver:
    def __init__(self, arena_ap, nbytes):
        self.ap = arena_ap
        self.n = nbytes
        self.off = 0

    def reset(self):
        self.off = 0

    def take(self, name, cols, dtype):
        esz = 2 if dtype == BF16 else 4
        nb = cols * esz
        off = (self.off + 63) & ~63
        assert off + nb <= self.n, f"arena overflow at {name}: {off}+{nb}>{self.n}"
        self.off = off + nb
        v = self.ap[:, off:off + nb].bitcast(dtype)
        assert v.shape == (128, cols), (name, v.shape)
        return v

    def at(self, off, name, cols, dtype):
        esz = 2 if dtype == BF16 else 4
        nb = cols * esz
        assert off % 64 == 0 and off + nb <= self.n, name
        v = self.ap[:, off:off + nb].bitcast(dtype)
        assert v.shape == (128, cols), (name, v.shape)
        return v


def build_program():
    nc = bass.Bass("TRN2", target_bir_lowering=False, debug=False)

    x_h = nc.dram_tensor("x", [BL, T, D], F32, kind="ExternalInput")
    w_in_h = nc.dram_tensor("w_in", [D, 2 * I], F32, kind="ExternalInput")
    conv_w_h = nc.dram_tensor("conv_w", [4, 1, I], F32, kind="ExternalInput")
    conv_b_h = nc.dram_tensor("conv_b", [I], F32, kind="ExternalInput")
    w_x_h = nc.dram_tensor("w_x", [I, D + 2 * S], F32, kind="ExternalInput")
    w_dt_h = nc.dram_tensor("w_dt", [D, I], F32, kind="ExternalInput")
    b_dt_h = nc.dram_tensor("b_dt", [I], F32, kind="ExternalInput")
    a_log_h = nc.dram_tensor("a_log", [I, S], F32, kind="ExternalInput")
    w_out_h = nc.dram_tensor("w_out", [I, D], F32, kind="ExternalInput")
    fw_h = nc.dram_tensor("fw", [D, D], F32, kind="ExternalInput")
    lng_h = nc.dram_tensor("lng", [D], F32, kind="ExternalInput")
    lnb_h = nc.dram_tensor("lnb", [D], F32, kind="ExternalInput")
    out_h = nc.dram_tensor("out", [BL, T, D], F32, kind="ExternalOutput")

    u_hbm = nc.dram_tensor("u_scr", [NT * 128, NI * TT], BF16)
    y2_hbm = nc.dram_tensor("y2_scr", [NT * 128, NI * TT], BF16)

    idf = nc.alloc_sbuf_tensor("idf", [128, 128], F32).ap()
    idbf = nc.alloc_sbuf_tensor("idbf", [128, 128], BF16).ap()
    ones1 = nc.alloc_sbuf_tensor("ones1", [1, 128], F32).ap()
    onesc = nc.alloc_sbuf_tensor("onesc", [128, 1], F32).ap()
    g_rep = nc.alloc_sbuf_tensor("g_rep", [128, D], F32).ap()
    b_rep = nc.alloc_sbuf_tensor("b_rep", [128, D], F32).ap()
    stash = nc.alloc_sbuf_tensor("stash", [128, 3 * NST], F32).ap()
    hc = nc.alloc_sbuf_tensor("hc", [128, NI * S], F32).ap()
    a_sb = nc.alloc_sbuf_tensor("a_sb", [128, NI * S], F32).ap()
    cw_sb = nc.alloc_sbuf_tensor("cw_sb", [128, 4 * NI], F32).ap()
    cb_sb = nc.alloc_sbuf_tensor("cb_sb", [128, NI], F32).ap()
    bdt_sb = nc.alloc_sbuf_tensor("bdt_sb", [128, NI], F32).ap()
    tails = nc.alloc_sbuf_tensor("tails", [128, 3 * NI], F32).ap()
    grun = nc.alloc_sbuf_tensor("grun", [128, 2], F32).ap()
    smll = nc.alloc_sbuf_tensor("smll", [1, 128], F32).ap()
    vec4 = nc.alloc_sbuf_tensor("vec4", [128, 4], F32).ap()
    tvec = nc.alloc_sbuf_tensor("tvec", [128, 48], F32).ap()
    cbank = nc.alloc_sbuf_tensor("cbank", [128, 4], F32).ap()
    fbv = nc.alloc_sbuf_tensor("fbv", [1, 4], F32).ap()
    fba = nc.alloc_sbuf_tensor("fba", [1, 4], F32).ap()

    arena_bytes = (nc.sbuf_bytes_remaining - 192) & ~63
    arena_t = nc.alloc_sbuf_tensor("arena", [128, arena_bytes], U8)
    ar = Carver(arena_t.ap(), arena_bytes)

    pf = [nc.alloc_psum_tensor(f"pf{i}", [128, 512], F32).ap() for i in range(6)]
    pb = [nc.alloc_psum_tensor(f"pb{i}", [128, 1024], BF16).ap() for i in range(2)]

    es = contextlib.ExitStack()
    sem = {}
    for name in (["md", "ve", "ac", "pe", "gp", "ph", "us", "ys", "yl", "fs"]
                 + [f"xd{i}" for i in range(5)] + [f"ws{i}" for i in range(2)]
                 + [f"ud{i}" for i in range(2)] + [f"od{i}" for i in range(4)]):
        sem[name] = es.enter_context(nc.semaphore(name))
    block = es.enter_context(nc.Block())
    c = Ctr()

    class InstrRec:
        __slots__ = ("name", "args", "kwargs", "incs")

        def __init__(self, name, args, kwargs):
            self.name = name
            self.args = args
            self.kwargs = kwargs
            self.incs = []

        def then_inc(self, s, n):
            self.incs.append((s, n))
            return self

    class RecEngine:
        def __init__(self, calls):
            self._calls = calls

        def __getattr__(self, name):
            def call(*a, **k):
                r = InstrRec(name, a, k)
                self._calls.append(r)
                return r
            return call

    PEND = {n: [] for n in ["sync", "vector", "scalar", "tensor", "gpsimd"]}

    def emit(engname, fn):
        if engname == "vector":
            def wrapped(e, fn=fn):
                fn(e)
                e.wait_ge(sem["ve"], c.val("ve"))
                TK(e.memset(fbv[0:1, 0:1], 0.0), "ve")
            fn = wrapped
        fn(RecEngine(PEND[engname]))

    def flush_sections():
        def replay(calls, e):
            for r in calls:
                instr = getattr(e, r.name)(*r.args, **r.kwargs)
                for s, n in r.incs:
                    instr.then_inc(s, n)
        for name in ["sync", "vector", "scalar", "tensor", "gpsimd"]:
            getattr(block, name)(
                lambda e, cs=PEND[name]: replay(cs, e))

    def W(e, name, val):
        assert val <= c.val(name), f"wait on future {name}: {val} > {c.val(name)}"
        if val > 0:
            e.wait_ge(sem[name], val)

    def TK(instr, s):
        instr.then_inc(sem[s], 1)
        return c.inc(s, 1)

    def SW(e, s):
        e.wait_ge(sem[s], c.val(s))



    def weight_load_cast(dst_bf, src_ap_fn, stage_f32, half_cols, kblks, row_cols):
        assert row_cols <= half_cols
        for kb in range(kblks):
            h0 = (kb % 2) * half_cols
            ws = f"ws{kb % 2}"

            def ld(e, kb=kb, h0=h0, ws=ws):
                W(e, "ac", c.val("ac"))
                e.dma_start(out=stage_f32[:, h0:h0 + row_cols],
                            in_=src_ap_fn(kb)).then_inc(sem[ws], 16)
            wd_after = c.inc(ws, 16)
            emit("sync", ld)

            def cast(e, kb=kb, h0=h0, wd_after=wd_after, ws=ws):
                W(e, ws, wd_after)
                SW(e, "ac")
                if kb == 0:
                    W(e, "ve", c.val("ve"))
                    W(e, "pe", c.val("pe"))
                TK(e.copy(dst_bf[:, kb * row_cols:(kb + 1) * row_cols],
                          stage_f32[:, h0:h0 + row_cols]), "ac")
            emit("scalar", cast)

    def barrier():
        target = c.val("ph") + 5
        for eng in ["sync", "vector", "scalar", "tensor", "gpsimd"]:
            emit(eng, lambda e: e.drain().then_inc(sem["ph"], 1))
        c.inc("ph", 5)
        for eng in ["sync", "vector", "scalar", "tensor", "gpsimd"]:
            emit(eng, lambda e, t=target: e.wait_ge(sem["ph"], t))

    iotat = ar.take("iotat", 128, F32)
    lngr_c = ar.take("lngr", D, F32)
    lnbr_c = ar.take("lnbr", D, F32)
    lngr = lngr_c[0:1, :]
    lnbr = lnbr_c[0:1, :]

    def p_dma(e):
        for ib in range(NI):
            e.dma_start(out=a_sb[:, ib * S:(ib + 1) * S],
                        in_=a_log_h.ap()[ib * 128:(ib + 1) * 128, :]).then_inc(sem["md"], 16)
            e.dma_start(out=cb_sb[:, ib:ib + 1],
                        in_=conv_b_h.ap()[ib * 128:(ib + 1) * 128].unsqueeze(1)).then_inc(sem["md"], 16)
            e.dma_start(out=bdt_sb[:, ib:ib + 1],
                        in_=b_dt_h.ap()[ib * 128:(ib + 1) * 128].unsqueeze(1)).then_inc(sem["md"], 16)
            for k in range(4):
                e.dma_start(out=cw_sb[:, k * NI + ib:k * NI + ib + 1],
                            in_=conv_w_h.ap()[k, 0, ib * 128:(ib + 1) * 128].unsqueeze(1)).then_inc(sem["md"], 16)
        e.dma_start(out=lngr, in_=lng_h.ap().unsqueeze(0)).then_inc(sem["md"], 16)
        e.dma_start(out=lnbr, in_=lnb_h.ap().unsqueeze(0)).then_inc(sem["md"], 16)
    n_pre = NI * 7 + 2
    md_pre = c.inc("md", 16 * n_pre)
    emit("sync", p_dma)

    def p_iota(e):
        e.iota(iotat, [[1, 128]], channel_multiplier=-1,
               allow_small_or_imprecise_dtypes=True).then_inc(sem["gp"], 1)
    gp_iota = c.inc("gp", 1)
    emit("gpsimd", p_iota)

    def p_ids(e):
        W(e, "gp", gp_iota)
        TK(e.tensor_scalar(idf, iotat, 0.0, None, ALU.is_equal), "ve")
        SW(e, "ve")
        TK(e.tensor_copy(idbf, idf), "ve")
        TK(e.memset(ones1, 1.0), "ve")
        TK(e.memset(onesc, 1.0), "ve")
        TK(e.memset(grun, 0.0), "ve")
        TK(e.memset(cbank[:, 0:1], LN1_EPS), "ve")
        TK(e.memset(cbank[:, 1:2], LN2_EPS), "ve")
        TK(e.memset(cbank[:, 2:3], -0.5), "ve")
        TK(e.memset(cbank[:, 3:4], 1.0 / D), "ve")
    emit("vector", p_ids)
    ve_ids = c.val("ve")

    def p_aexp(e):
        W(e, "md", md_pre)
        TK(e.activation(a_sb, a_sb, AF.Exp), "ac")
    emit("scalar", p_aexp)
    ac_aexp = c.val("ac")

    def p_aneg(e):
        W(e, "ac", ac_aexp)
        SW(e, "ve")
        TK(e.tensor_scalar(a_sb, a_sb, -1.0, None, ALU.mult), "ve")
    emit("vector", p_aneg)

    def p_grep_mm(e):
        W(e, "md", md_pre)
        W(e, "ve", ve_ids)
        e.matmul(pf[2], ones1, lngr[0:1, 0:512], start=True, stop=True)
        e.matmul(pf[3], ones1, lngr[0:1, 512:1024], start=True, stop=True)
        e.matmul(pf[4], ones1, lnbr[0:1, 0:512], start=True, stop=True)
        TK(e.matmul(pf[5], ones1, lnbr[0:1, 512:1024], start=True, stop=True), "pe")
    emit("tensor", p_grep_mm)
    pe_grep = c.val("pe")

    def p_grep_ev(e):
        W(e, "pe", pe_grep)
        SW(e, "ve")
        TK(e.tensor_copy(g_rep[:, 0:512], pf[2]), "ve")
        TK(e.tensor_copy(g_rep[:, 512:1024], pf[3]), "ve")
        TK(e.tensor_copy(b_rep[:, 0:512], pf[4]), "ve")
        TK(e.tensor_copy(b_rep[:, 512:1024], pf[5]), "ve")
    emit("vector", p_grep_ev)

    ar.reset()
    w_in_x = ar.take("w_in_x", ND * I, BF16)
    xbuf = [ar.take(f"x{i}", D, F32) for i in range(5)]
    xnbf = [ar.take(f"xnbf{i}", D, BF16) for i in range(4)]
    xnsq = ar.take("xnsq", D, BF16)
    g1s = ar.take("g1s", D, F32)
    g2s = ar.take("g2s", D, BF16)
    xnf = ar.take("xnf", ND * TT, BF16)
    dgw = ar.take("dgw", 4 * NI * 128, BF16)
    xpe = [ar.take(f"xpe{i}", 516, BF16) for i in range(2)]
    tlb = nc.alloc_sbuf_tensor("tlb", [128, 3 * NI], BF16).ap()
    ubf_off = (ar.off + 63) & ~63
    ubf = ar.take("ubf", NI * TT, BF16)
    cacc = [ar.take(f"cacc{i}", 512, F32) for i in range(2)]
    wstage = ar.at(ubf_off, "wstage", 2 * 2048, F32)

    weight_load_cast(w_in_x, lambda kb: w_in_h.ap()[kb * 128:(kb + 1) * 128, 0:I],
                     wstage, 2048, ND, I)
    ac_wa = c.val("ac")

    def p1a_dgw(e):
        W(e, "md", md_pre)
        SW(e, "ve")
        for k in range(4):
            for ib in range(NI):
                TK(e.tensor_scalar(dgw[:, (k * NI + ib) * 128:(k * NI + ib + 1) * 128],
                                   idbf, cw_sb[:, k * NI + ib:k * NI + ib + 1],
                                   None, ALU.mult), "ve")
    emit("vector", p1a_dgw)
    ve_dgw = c.val("ve")

    def p1a_s1(ti):
        b = ti // (NT // BL)
        t0 = (ti % (NT // BL)) * TT

        for tb in range(NTB):
            sti = ti * NTB + tb
            xb = xbuf[sti % 5]
            xs = f"xd{sti % 5}"

            def ld(e, b=b, t0=t0, tb=tb, xb=xb, sti=sti, xs=xs):
                W(e, "ve", c.v.get(f"a_xn_{sti - 5}", 0))
                e.dma_start(out=xb, in_=x_h.ap()[b, t0 + tb * 128:t0 + (tb + 1) * 128, :]
                            ).then_inc(sem[xs], 16)
            c.v[f"a_x_{sti}"] = c.inc(xs, 16)
            c.v[f"a_xsem_{sti}"] = xs
            emit("sync", ld)

        for tb in range(NTB):
            sti = ti * NTB + tb
            xb = xbuf[sti % 5]
            tvb = (ti % 2) * 24 + tb * 6

            def ve_ln(e, xb=xb, sti=sti, tvb=tvb, tb=tb, ti=ti):
                W(e, c.v[f"a_xsem_{sti}"], c.v[f"a_x_{sti}"])
                if tb == 0:
                    W(e, "ac", c.v.get(f"a_r2t_{ti - 2}", 0))
                SW(e, "ve")
                T = tvec
                TK(e.scalar_tensor_tensor(xnsq, xb, 1.0, xb, ALU.mult, ALU.mult,
                                          accum_out=T[:, tvb:tvb + 1]), "ve")
                TK(e.reduce_sum(T[:, tvb + 1:tvb + 2], xb, AX.X), "ve")
                SW(e, "ve")
                TK(e.tensor_scalar(stash[:, sti:sti + 1], T[:, tvb + 1:tvb + 2],
                                   -1.0 / D, None, ALU.mult), "ve")
                TK(e.tensor_scalar(T[:, tvb + 2:tvb + 3], T[:, tvb:tvb + 1],
                                   1.0 / D, None, ALU.mult), "ve")
                SW(e, "ve")
                TK(e.scalar_tensor_tensor(T[:, tvb + 3:tvb + 4], stash[:, sti:sti + 1],
                                          1.0, stash[:, sti:sti + 1], ALU.mult,
                                          ALU.mult), "ve")
                SW(e, "ve")
                TK(e.tensor_sub(T[:, tvb + 2:tvb + 3], T[:, tvb + 2:tvb + 3],
                                T[:, tvb + 3:tvb + 4]), "ve")
            emit("vector", ve_ln)
            c.v[f"a_var_{sti}"] = c.val("ve")

        def ac_ln1(e, ti=ti):
            for tb in range(NTB):
                sti = ti * NTB + tb
                tvb = (ti % 2) * 24 + tb * 6
                W(e, "ve", c.v[f"a_var_{sti}"])
                SW(e, "ac")
                TK(e.activation(tvec[:, tvb + 3:tvb + 4], tvec[:, tvb + 2:tvb + 3],
                                AF.Ln, bias=cbank[:, 0:1]), "ac")
        emit("scalar", ac_ln1)

        def ac_exp1(e, ti=ti):
            for tb in range(NTB):
                sti = ti * NTB + tb
                tvb = (ti % 2) * 24 + tb * 6
                SW(e, "ac")
                TK(e.activation(stash[:, NST + sti:NST + sti + 1],
                                tvec[:, tvb + 3:tvb + 4], AF.Exp,
                                scale=cbank[:, 2:3]), "ac")
                c.v[f"a_rstd_{sti}"] = c.val("ac")
        emit("scalar", ac_exp1)

    def p1a_s2(ti):
        for tb in range(NTB):
            sti = ti * NTB + tb
            xb = xbuf[sti % 5]
            xn = xnbf[tb]
            tvb = (ti % 2) * 24 + tb * 6

            def ve_xn(e, xb=xb, xn=xn, sti=sti, tvb=tvb):
                W(e, "ac", c.v[f"a_rstd_{sti}"])
                W(e, "pe", c.v.get(f"a_tp_{sti - 4}", 0))
                SW(e, "ve")
                TK(e.tensor_scalar(xn, xb, stash[:, sti:sti + 1],
                                   stash[:, NST + sti:NST + sti + 1],
                                   ALU.add, ALU.mult), "ve")
                SW(e, "ve")
                TK(e.scalar_tensor_tensor(xnsq, xn, 1.0, xn, ALU.mult, ALU.mult,
                                          accum_out=tvec[:, tvb + 4:tvb + 5]), "ve")
            emit("vector", ve_xn)
            c.v[f"a_xn_{sti}"] = c.val("ve")
            c.v[f"a_v2_{sti}"] = c.val("ve")

        def ac_ln2(e, ti=ti):
            for tb in range(NTB):
                sti = ti * NTB + tb
                tvb = (ti % 2) * 24 + tb * 6
                W(e, "ve", c.v[f"a_v2_{sti}"])
                SW(e, "ac")
                TK(e.activation(tvec[:, tvb + 5:tvb + 6], tvec[:, tvb + 4:tvb + 5],
                                AF.Ln, bias=cbank[:, 1:2], scale=cbank[:, 3:4]),
                   "ac")
        emit("scalar", ac_ln2)

        def ac_exp2(e, ti=ti):
            for tb in range(NTB):
                sti = ti * NTB + tb
                tvb = (ti % 2) * 24 + tb * 6
                SW(e, "ac")
                TK(e.activation(stash[:, 2 * NST + sti:2 * NST + sti + 1],
                                tvec[:, tvb + 5:tvb + 6], AF.Exp,
                                scale=cbank[:, 2:3]), "ac")
                c.v[f"a_r2_{sti}"] = c.val("ac")
        emit("scalar", ac_exp2)
        c.v[f"a_r2t_{ti}"] = c.val("ac")

        for tb in range(NTB):
            sti = ti * NTB + tb
            xn = xnbf[tb]
            pbk = pb[tb % 2]

            def pe_tp(e, xn=xn, pbk=pbk, sti=sti, tb=tb, ti=ti):
                W(e, "ve", c.v[f"a_xn_{sti}"])
                W(e, "ac", c.v.get(f"a_tpev_{ti}_{tb - 2}",
                                   c.v.get(f"a_tpev_{ti - 1}_{tb + 2}", 0)))
                for db in range(ND):
                    TK(e.transpose(pbk[:, db * 128:(db + 1) * 128],
                                   xn[:, db * 128:(db + 1) * 128], idbf), "pe")
            emit("tensor", pe_tp)
            c.v[f"a_tp_{sti}"] = c.val("pe")

            def ac_tpev(e, pbk=pbk, tb=tb, sti=sti, ti=ti):
                W(e, "pe", c.v[f"a_tp_{sti}"])
                W(e, "pe", c.v.get(f"a_mm_{ti - 1}_{NI - 1}", 0))
                SW(e, "ac")
                src_ = pbk.rearrange("p (d t) -> p d t", d=ND)
                dst = xnf.rearrange("p (d t) -> p d t", d=ND)[:, :, tb * 128:(tb + 1) * 128]
                TK(e.copy(dst, src_), "ac")
            emit("scalar", ac_tpev)
            c.v[f"a_tpev_{ti}_{tb}"] = c.val("ac")
        c.v[f"a_xnfd_{ti}"] = c.val("ac")

    def p1a_s3(ti):
        first = (ti % (NT // BL)) == 0
        for tb in range(NTB):
            sti = ti * NTB + tb
            xn = xnbf[tb]

            def ve_g(e, xn=xn, sti=sti):
                W(e, "ac", c.v[f"a_r2_{sti}"])
                SW(e, "ve")
                TK(e.scalar_tensor_tensor(g1s, xn,
                                          stash[:, 2 * NST + sti:2 * NST + sti + 1],
                                          g_rep, ALU.mult, ALU.mult), "ve")
                SW(e, "ve")
                TK(e.tensor_add(g2s, g1s, b_rep), "ve")
                SW(e, "ve")
                TK(e.tensor_reduce(grun[:, 1:2], g2s, AX.X, ALU.max,
                                   apply_absolute_value=True), "ve")
                SW(e, "ve")
                TK(e.tensor_tensor(grun[:, 0:1], grun[:, 0:1], grun[:, 1:2],
                                   ALU.max), "ve")
            emit("vector", ve_g)
        ve_xnf_done = c.v[f"a_xnfd_{ti}"]

        for ib in range(NI):
            pwk = pf[ib % 2]

            def pe_mm(e, ib=ib, pwk=pwk, ve_xnf_done=ve_xnf_done, ti=ti):
                W(e, "ac", ac_wa)
                W(e, "ac", ve_xnf_done)
                W(e, "ac", c.v.get(f"a_evx_{ti}_{ib - 2}",
                                   c.v.get(f"a_evx_{ti - 1}_{ib - 2 + NI}", 0)))
                for kb in range(ND):
                    mm = e.matmul(pwk, w_in_x[:, (kb * NI + ib) * 128:(kb * NI + ib + 1) * 128],
                                  xnf[:, kb * TT:(kb + 1) * TT],
                                  start=(kb == 0), stop=(kb == ND - 1))
                TK(mm, "pe")
            emit("tensor", pe_mm)
            c.v[f"a_mm_{ti}_{ib}"] = c.val("pe")

            def ac_evx(e, ib=ib, pwk=pwk, first=first, ti=ti):
                W(e, "pe", c.v[f"a_mm_{ti}_{ib}"])
                W(e, "pe", c.v.get(f"a_cv_{ti}_{ib - 2}", 0))
                SW(e, "ac")
                xp = xpe[ib % 2]
                TK(e.activation(xp[:, 3:515], pwk, AF.Copy), "ac")
                SW(e, "ac")
                if first:
                    TK(e.activation(xp[:, 0:3], xp[:, 0:3], AF.Copy, scale=0.0),
                       "ac")
                else:
                    TK(e.activation(xp[:, 0:3], tlb[:, ib * 3:(ib + 1) * 3],
                                    AF.Copy), "ac")
                SW(e, "ac")
                TK(e.activation(tlb[:, ib * 3:(ib + 1) * 3], xp[:, 512:515],
                                AF.Copy), "ac")
            emit("scalar", ac_evx)
            c.v[f"a_evx_{ti}_{ib}"] = c.val("ac")

            def pe_cv(e, ib=ib, ti=ti):
                W(e, "ac", c.v[f"a_evx_{ti}_{ib}"])
                W(e, "ve", ve_dgw)
                W(e, "ac", c.v.get(f"a_u_{ti}_{ib - 2}", 0))
                xp = xpe[ib % 2]
                pcv = pf[2 + ib % 2]
                for k in range(4):
                    mm = e.matmul(pcv, dgw[:, (k * NI + ib) * 128:(k * NI + ib + 1) * 128],
                                  xp[:, k:k + 512], start=(k == 0), stop=(k == 3))
                TK(mm, "pe")
            emit("tensor", pe_cv)
            c.v[f"a_cv_{ti}_{ib}"] = c.val("pe")

            def ac_silu(e, ib=ib, ti=ti):
                W(e, "pe", c.v[f"a_cv_{ti}_{ib}"])
                W(e, "us", c.v.get(f"a_udma_{ti - 1}", 0))
                SW(e, "ac")
                TK(e.activation(ubf[:, ib * TT:(ib + 1) * TT], pf[2 + ib % 2],
                                AF.Silu, bias=cb_sb[:, ib:ib + 1]), "ac")
            emit("scalar", ac_silu)
            c.v[f"a_u_{ti}_{ib}"] = c.val("ac")

        def st_u(e, ti=ti):
            W(e, "ac", c.v[f"a_u_{ti}_{NI - 1}"])
            e.dma_start(out=u_hbm.ap()[ti * 128:(ti + 1) * 128, :], in_=ubf
                        ).then_inc(sem["us"], 16)
        c.v[f"a_udma_{ti}"] = c.inc("us", 16)
        emit("sync", st_u)

    for k in range(NT + 2):
        if 0 <= k - 2 < NT:
            p1a_s3(k - 2)
        if 0 <= k - 1 < NT:
            p1a_s2(k - 1)
        if k < NT:
            p1a_s1(k)

    emit("sync", lambda e, v=c.val("us"): e.wait_ge(sem["us"], v))
    barrier()

    ar.reset()
    w_x = ar.take("w_x", NI * 1056, BF16)
    w_dt = ar.take("w_dt", ND * I, BF16)
    uin = [ar.take(f"uin{i}", NI * TT, BF16) for i in range(2)]
    dtin = [ar.take(f"dtin{i}", ND * TT, BF16) for i in range(2)]
    bcb = ar.take("bcb", 512, BF16)
    iotae_c = ar.take("iotae", 128, F32)
    eb_c = ar.take("eb", 2 * S * 128, BF16)
    iotae = iotae_c[0:32, :]
    eb = eb_c[0:32, :]
    brep_off = (ar.off + 63) & ~63
    brep = ar.take("brep", S * TT, BF16)
    crep = ar.take("crep", S * TT, BF16)
    wda = ar.take("wda", 4 * 512, F32)
    dub = ar.take("dub", 512, BF16)
    da2 = [ar.take(f"da2{i}", 2 * 512, F32) for i in range(3)]
    dbu2 = ar.take("dbu2", 2 * 512, BF16)
    h2 = [ar.take(f"h2{i}", 2 * 512, BF16) for i in range(2)]
    hk2 = [ar.take(f"hk2{i}", 2 * 512, BF16) for i in range(3)]
    y2st = ar.take("y2st", 4 * 512, BF16)
    wstage_b = ar.at(brep_off, "wstage_b", 2 * 2048, F32)

    def b_iota(e):
        e.iota(iotae, [[0, 128]], channel_multiplier=1,
               allow_small_or_imprecise_dtypes=True).then_inc(sem["gp"], 1)
    gp_eb = c.inc("gp", 1)
    emit("gpsimd", b_iota)

    def b_eb(e):
        W(e, "gp", gp_eb)
        SW(e, "ve")
        for s in range(2 * S):
            TK(e.tensor_scalar(eb[:, s * 128:(s + 1) * 128], iotae, float(s), None,
                              ALU.is_equal), "ve")
    emit("vector", b_eb)

    weight_load_cast(w_x, lambda kb: w_x_h.ap()[kb * 128:(kb + 1) * 128, :],
                     wstage_b, 2048, NI, 1056)
    weight_load_cast(w_dt, lambda kb: w_dt_h.ap()[kb * 128:(kb + 1) * 128, :],
                     wstage_b, 2048, ND, I)
    ve_wb = c.val("ve")
    ac_wb = c.val("ac")

    def p1b_init(e):
        SW(e, "ve")
        TK(e.memset(hc, 0.0), "ve")
    emit("vector", p1b_init)

    def emit_ld_u(ti):
        ub_ = uin[ti % 2]
        usem = f"ud{ti % 2}"

        def ld_u(e, ti=ti, ub_=ub_, usem=usem):
            W(e, "ac", c.v.get(f"b_y2_{ti - 2}_{NI - 1}", 0))
            W(e, "ve", c.v.get(f"b_du_{ti - 2}_{NI - 1}", 0))
            e.dma_start(out=ub_, in_=u_hbm.ap()[ti * 128:(ti + 1) * 128, :]
                        ).then_inc(sem[usem], 16)
        c.v[f"b_u_{ti}"] = c.inc(usem, 16)
        c.v[f"b_usem_{ti}"] = usem
        emit("sync", ld_u)

    def emit_wx_one(ti, nb):
        ub_ = uin[ti % 2]
        dtb = dtin[ti % 2]
        pwk = pf[nb % 2]

        def pe_wx(e, nb=nb, pwk=pwk, ti=ti, ub_=ub_):
            W(e, "ac", ac_wb)
            W(e, c.v[f"b_usem_{ti}"], c.v[f"b_u_{ti}"])
            W(e, "ac", c.v.get(f"b_dtin_{ti}_{nb - 2}",
                               c.v.get(f"b_dtin_{ti - 1}_{ND - 1}", 0)))
            for kb in range(NI):
                mm = e.matmul(pwk, w_x[:, kb * 1056 + nb * 128:kb * 1056 + (nb + 1) * 128],
                              ub_[:, kb * TT:(kb + 1) * TT],
                              start=(kb == 0), stop=(kb == NI - 1))
            TK(mm, "pe")
        emit("tensor", pe_wx)
        c.v[f"b_wx_{ti}_{nb}"] = c.val("pe")

        def ac_dtin(e, nb=nb, pwk=pwk, ti=ti, dtb=dtb):
            W(e, "pe", c.v[f"b_wx_{ti}_{nb}"])
            if nb == 0:
                W(e, "pe", c.v.get(f"b_wdt_{ti - 2}_{NI - 1}", 0))
            SW(e, "ac")
            TK(e.copy(dtb[:, nb * TT:(nb + 1) * TT], pwk), "ac")
        emit("scalar", ac_dtin)
        c.v[f"b_dtin_{ti}_{nb}"] = c.val("ac")

    def emit_y2(ti, ib):
        def ac_y2(e, ib=ib, ti=ti):
            pyk = pf[4 + (ib % 2)]
            W(e, "pe", c.v[f"b_idmm_{ti}_{ib}_{7}"])
            W(e, "ys", c.v.get(f"b_ydma_{ti}_{ib // 4 - 1}",
                               c.v.get(f"b_ydma_{ti - 1}_{3}", 0)))
            SW(e, "ac")
            TK(e.copy(y2st[:, (ib % 4) * 512:(ib % 4 + 1) * 512], pyk), "ac")
        emit("scalar", ac_y2)
        c.v[f"b_y2_{ti}_{ib}"] = c.val("ac")

        if ib % 4 == 3:
            def st_y2(e, ib=ib, ti=ti):
                W(e, "ac", c.v[f"b_y2_{ti}_{ib}"])
                e.dma_start(
                    out=y2_hbm.ap()[ti * 128:(ti + 1) * 128,
                                    (ib - 3) * TT:(ib + 1) * TT],
                    in_=y2st).then_inc(sem["ys"], 16)
            c.v[f"b_ydma_{ti}_{ib // 4}"] = c.inc("ys", 16)
            emit("sync", st_y2)

    emit_ld_u(0)
    for nb0 in range(ND):
        emit_wx_one(0, nb0)

    for ti in range(NT):
        ub = uin[ti % 2]
        first = (ti % (NT // BL)) == 0
        dtb = dtin[ti % 2]

        if first and ti > 0:
            def rhc(e):
                SW(e, "ve")
                TK(e.memset(hc, 0.0), "ve")
            emit("vector", rhc)

        def pe_bc(e, ti=ti):
            W(e, "ve", c.v.get(f"b_bcev_{ti - 1}", 0))
            W(e, "ac", c.v.get(f"b_y2_{ti - 1}_{NI - 2}", 0))
            for kb in range(NI):
                mm = e.matmul(pf[4][0:32, :], w_x[:, kb * 1056 + 1024:kb * 1056 + 1056],
                              ub[:, kb * TT:(kb + 1) * TT],
                              start=(kb == 0), stop=(kb == NI - 1))
            TK(mm, "pe")
        emit("tensor", pe_bc)
        c.v[f"b_bc_{ti}"] = c.val("pe")

        def ve_bcev(e, ti=ti):
            W(e, "pe", c.v[f"b_bc_{ti}"])
            SW(e, "ve")
            TK(e.tensor_copy(bcb[0:32, 0:512], pf[4][0:32, :]), "ve")
        emit("vector", ve_bcev)
        c.v[f"b_bcev_{ti}"] = c.val("ve")

        for s in range(S):
            def pe_br(e, s=s, ti=ti):
                W(e, "ve", c.v[f"b_bcev_{ti}"])
                W(e, "ve", c.v.get(f"b_brev_{ti}_{s - 1}", 0))
                if s == 0:
                    W(e, "ac", c.v.get(f"b_y2_{ti - 1}_{NI - 1}", 0))
                e.matmul(pf[4], eb[:, s * 128:(s + 1) * 128], bcb[0:32, 0:512],
                         start=True, stop=True)
                TK(e.matmul(pf[5], eb[:, (S + s) * 128:(S + s + 1) * 128],
                           bcb[0:32, 0:512], start=True, stop=True), "pe")
            emit("tensor", pe_br)
            c.v[f"b_br_{ti}_{s}"] = c.val("pe")

            def ve_brev(e, s=s, ti=ti):
                W(e, "pe", c.v[f"b_br_{ti}_{s}"])
                SW(e, "ve")
                TK(e.tensor_copy(brep[:, s * TT:(s + 1) * TT], pf[4]), "ve")
                TK(e.tensor_copy(crep[:, s * TT:(s + 1) * TT], pf[5]), "ve")
            emit("vector", ve_brev)
            c.v[f"b_brev_{ti}_{s}"] = c.val("ve")
        ve_brep_done = c.val("ve")

        def SCNv(tj, ib, g):
            while g < 0:
                g += 8
                ib -= 1
            while ib < 0:
                ib += NI
                tj -= 1
            if tj < 0:
                return 0
            return c.v.get(f"b_scan_{tj}_{ib}_{g}", 0)

        def YMv(tj, ib, g):
            while g < 0:
                g += 8
                ib -= 1
            while ib < 0:
                ib += NI
                tj -= 1
            if tj < 0:
                return 0
            return c.v.get(f"b_ym_{tj}_{ib}_{g}", 0)

        def IDv(tj, ib, g):
            while g < 0:
                g += 8
                ib -= 1
            while ib < 0:
                ib += NI
                tj -= 1
            if tj < 0:
                return 0
            return c.v.get(f"b_idmm_{tj}_{ib}_{g}", 0)

        for q in range(4):
            for ib in range(4 * q, 4 * q + 4):
                pwk = pf[2 + ib % 2]

                def pe_wdt(e, ib=ib, pwk=pwk, ti=ti, dtb=dtb):
                    W(e, "ac", c.v[f"b_dtin_{ti}_{ND - 1}"])
                    W(e, "ac", c.v.get(f"b_sp_{ti}_{ib - 2}",
                                       c.v.get(f"b_sp_{ti - 1}_{ib + 14}", 0)))
                    for kb in range(ND):
                        mm = e.matmul(pwk, w_dt[:, kb * I + ib * 128:kb * I + (ib + 1) * 128],
                                      dtb[:, kb * TT:(kb + 1) * TT],
                                      start=(kb == 0), stop=(kb == ND - 1))
                    TK(mm, "pe")
                emit("tensor", pe_wdt)
                c.v[f"b_wdt_{ti}_{ib}"] = c.val("pe")

                def ac_spexp(e, ib=ib, pwk=pwk, ti=ti):
                    W(e, "pe", c.v[f"b_wdt_{ti}_{ib}"])
                    W(e, "ve", c.v.get(f"b_du_{ti}_{ib - 4}",
                                       c.v.get(f"b_du_{ti - 1}_{ib + 12}", 0)))
                    SW(e, "ac")
                    TK(e.activation(wda[:, (ib % 4) * 512:(ib % 4 + 1) * 512], pwk,
                                    AF.Exp, bias=bdt_sb[:, ib:ib + 1]), "ac")
                emit("scalar", ac_spexp)
                c.v[f"b_sp_{ti}_{ib}"] = c.val("ac")

            def ac_lnq(e, q=q, ti=ti):
                SW(e, "ac")
                TK(e.activation(wda, wda, AF.Ln, bias=1.0), "ac")
            emit("scalar", ac_lnq)
            c.v[f"b_lnq_{ti}_{q}"] = c.val("ac")

            for ib in range(4 * q, 4 * q + 4):
                dsl = wda[:, (ib % 4) * 512:(ib % 4 + 1) * 512]

                def ve_du(e, ib=ib, ti=ti, q=q, dsl=dsl):
                    W(e, "ac", c.v[f"b_lnq_{ti}_{q}"])
                    W(e, c.v[f"b_usem_{ti}"], c.v[f"b_u_{ti}"])
                    SW(e, "ve")
                    TK(e.tensor_tensor(dub, dsl, ub[:, ib * TT:(ib + 1) * TT],
                                       ALU.mult), "ve")
                emit("vector", ve_du)
                c.v[f"b_du_{ti}_{ib}"] = c.val("ve")

                for g in range(8):
                    dab = da2[g % 3]

                    def ac_da(e, ib=ib, g=g, dab=dab, ti=ti, dsl=dsl):
                        W(e, "ve", SCNv(ti, ib, g - 3))
                        SW(e, "ac")
                        for s in range(2):
                            TK(e.activation(dab[:, s * 512:(s + 1) * 512], dsl, AF.Exp,
                                            scale=a_sb[:, ib * S + g * 2 + s:ib * S + g * 2 + s + 1]),
                               "ac")
                    emit("scalar", ac_da)
                    c.v[f"b_da_{ti}_{ib}_{g}"] = c.val("ac")

                    if g == 2 and c.v.get("y2_pending", None) is not None:
                        emit_y2(ti, c.v["y2_pending"])
                        c.v["y2_pending"] = None

                    hb = h2[g % 2]
                    hck = hk2[g % 3]

                    def ve_sc(e, ib=ib, g=g, dab=dab, hb=hb, hck=hck, ti=ti):
                        W(e, "ac", c.v[f"b_da_{ti}_{ib}_{g}"])
                        W(e, "pe", IDv(ti, ib, g - 3))
                        SW(e, "ve")
                        bs2 = brep[:, (g * 2) * TT:(g * 2 + 2) * TT].rearrange(
                            "p (s t) -> p s t", s=2)
                        TK(e.tensor_tensor(dbu2.rearrange("p (s t) -> p s t", s=2),
                                           dub.unsqueeze(1).to_broadcast([128, 2, TT]),
                                           bs2, ALU.mult), "ve")
                        SW(e, "ve")
                        for s in range(2):
                            TK(e.tensor_tensor_scan(
                                hb[:, s * 512:(s + 1) * 512],
                                dab[:, s * 512:(s + 1) * 512],
                                dbu2[:, s * 512:(s + 1) * 512],
                                hc[:, ib * S + g * 2 + s:ib * S + g * 2 + s + 1],
                                ALU.mult, ALU.add), "ve")
                        SW(e, "ve")
                        TK(e.tensor_copy(hc[:, ib * S + g * 2:ib * S + g * 2 + 2],
                                         hb.rearrange("p (s t) -> p s t", s=2)[:, :, 511]),
                           "ve")
                        TK(e.tensor_tensor(hck, hb,
                                           crep[:, (g * 2) * TT:(g * 2 + 2) * TT],
                                           ALU.mult), "ve")
                    emit("vector", ve_sc)
                    c.v[f"b_scan_{ti}_{ib}_{g}"] = c.val("ve")
                    c.v[f"b_ym_{ti}_{ib}_{g}"] = c.val("ve")

                    pyk = pf[4 + (ib % 2)]

                    def pe_id(e, ib=ib, g=g, hck=hck, pyk=pyk, ti=ti):
                        W(e, "ve", c.v[f"b_ym_{ti}_{ib}_{g}"])
                        if g == 0 and ib >= 2:
                            W(e, "ac", c.v[f"b_y2_{ti}_{ib - 2}"])
                        for s in range(2):
                            mm = e.matmul(pyk, idbf, hck[:, s * 512:(s + 1) * 512],
                                          start=(g == 0 and s == 0), stop=False)
                        if g == 7:
                            mm = e.matmul(pyk, idbf, ub[:, ib * TT:(ib + 1) * TT],
                                          start=False, stop=True)
                        TK(mm, "pe")
                    emit("tensor", pe_id)
                    c.v[f"b_idmm_{ti}_{ib}_{g}"] = c.val("pe")

                c.v["y2_pending"] = ib

                if ti + 1 < NT and 4 <= ib < 12:
                    emit_wx_one(ti + 1, ib - 4)

            if q == 0 and ti + 1 < NT:
                emit_ld_u(ti + 1)

        emit_y2(ti, c.v["y2_pending"])
        c.v["y2_pending"] = None

    emit("sync", lambda e, v=c.val("ys"): e.wait_ge(sem["ys"], v))
    barrier()

    ar.reset()
    w_in_z = ar.take("w_in_z", ND * I, BF16)
    w_o = ar.take("w_o", NI * D, BF16)
    wq = ar.take("wq", ND * D, BF16)
    xb2 = [ar.take(f"x2_{i}", D, F32) for i in range(5)]
    xnbf2 = [ar.take(f"xnbf2_{i}", D, BF16) for i in range(2)]
    xnf2 = ar.take("xnf2", ND * TT, BF16)
    xqf = ar.take("xqf", ND * TT, BF16)
    y2b_off = (ar.off + 63) & ~63
    y2b = ar.take("y2b", NI * TT, BF16)
    yop = ar.take("yop", NI * TT, BF16)
    szp = [ar.take(f"szp{i}", 512, BF16) for i in range(2)]
    xq_t = [ar.take(f"xqt{i}", D, BF16) for i in range(2)]
    q1s = ar.take("q1s", D, F32)
    attn_sb = ar.take("attn_sb", 512, F32)
    o_f = ar.take("o_f", 512, F32)
    outsb = [ar.take(f"outsb{i}", D, F32) for i in range(4)]
    wstage2 = ar.at(y2b_off, "wstage2", 2 * 2048, F32)
    fwstage = ar.at(y2b_off + 16 * KB, "fwstage", D, F32)
    awstage = ar.at(y2b_off + 20 * KB, "awstage", D, F32)
    bsum = ar.take("bsum", 16, F32)

    weight_load_cast(w_in_z, lambda kb: w_in_h.ap()[kb * 128:(kb + 1) * 128, I:2 * I],
                     wstage2, 2048, ND, I)
    weight_load_cast(w_o, lambda kb: w_out_h.ap()[kb * 128:(kb + 1) * 128, :],
                     wstage2, 2048, NI, D)
    ve_w2 = c.val("ve")
    ac_w2 = c.val("ac")

    def g_dma(e):
        e.dma_start(out=smll[0:1, 0:128], in_=grun[:, 0:1]).then_inc(sem["md"], 16)
    md_g = c.inc("md", 16)
    emit("sync", g_dma)

    def g_fin(e, md_g=md_g):
        W(e, "md", md_g)
        SW(e, "ve")
        TK(e.tensor_reduce(tvec[0:1, 0:1], smll[0:1, 0:128], AX.X, ALU.max), "ve")
        SW(e, "ve")
        TK(e.reciprocal(tvec[0:1, 1:2], tvec[0:1, 0:1]), "ve")
        SW(e, "ve")
        TK(e.tensor_scalar(tvec[0:1, 1:2], tvec[0:1, 1:2], QB, None, ALU.mult), "ve")
    emit("vector", g_fin)

    for kb in range(ND):
        def fw_ld(e, kb=kb):
            W(e, "ve", c.v.get(f"c_abs_{kb - 1}", 0))
            e.dma_start(out=fwstage, in_=fw_h.ap()[kb * 128:(kb + 1) * 128, :]
                        ).then_inc(sem["fs"], 16)
        wd_v = c.inc("fs", 16)
        emit("sync", fw_ld)

        def ve_abs(e, kb=kb, wd_v=wd_v):
            W(e, "fs", wd_v)
            SW(e, "ve")
            TK(e.scalar_tensor_tensor(awstage, fwstage, -1.0, fwstage,
                                     ALU.mult, ALU.max), "ve")
            SW(e, "ve")
            TK(e.tensor_reduce(bsum[:, kb:kb + 1], awstage, AX.X, ALU.add), "ve")
        emit("vector", ve_abs)
        c.v[f"c_abs_{kb}"] = c.val("ve")

    def ve_bsum(e):
        SW(e, "ve")
        TK(e.reduce_sum(bsum[:, 8:9], bsum[:, 0:8], AX.X), "ve")
    emit("vector", ve_bsum)
    ve_bs = c.val("ve")

    def pe_bsum(e, ve_bs=ve_bs):
        W(e, "ve", ve_bs)
        TK(e.matmul(pf[2][0:1, 0:1], onesc, bsum[:, 8:9], start=True, stop=True),
          "pe")
    emit("tensor", pe_bsum)
    pe_bs = c.val("pe")

    def ve_beta(e, pe_bs=pe_bs):
        W(e, "pe", pe_bs)
        SW(e, "ve")
        TK(e.tensor_scalar(tvec[0:1, 2:3], pf[2][0:1, 0:1], 1.0 / (D * D), None,
                          ALU.mult), "ve")
        SW(e, "ve")
        TK(e.tensor_scalar(tvec[0:1, 3:4], tvec[0:1, 2:3], W_EPS, 0.5,
                          ALU.add, ALU.mult), "ve")
        TK(e.tensor_tensor(tvec[0:1, 4:5], tvec[0:1, 2:3], tvec[0:1, 0:1],
                          ALU.mult), "ve")
        SW(e, "ve")
        TK(e.tensor_scalar(tvec[0:1, 4:5], tvec[0:1, 4:5], 1.0 / QB, None,
                          ALU.mult), "ve")
    emit("vector", ve_beta)
    ve_beta_v = c.val("ve")

    def pe_bcastv(e, ve_beta_v=ve_beta_v):
        W(e, "ve", ve_beta_v)
        TK(e.matmul(pf[2][:, 0:5], ones1, tvec[0:1, 0:5], start=True, stop=True),
          "pe")
    emit("tensor", pe_bcastv)
    pe_bv = c.val("pe")

    def ve_vecs(e, pe_bv=pe_bv):
        W(e, "pe", pe_bv)
        SW(e, "ve")
        TK(e.tensor_copy(vec4[:, 0:1], pf[2][:, 1:2]), "ve")
        TK(e.tensor_copy(vec4[:, 1:2], pf[2][:, 3:4]), "ve")
        TK(e.tensor_copy(vec4[:, 2:3], pf[2][:, 4:5]), "ve")
        TK(e.tensor_copy(vec4[:, 3:4], pf[2][:, 2:3]), "ve")
        SW(e, "ve")
        TK(e.tensor_scalar(g_rep, g_rep, vec4[:, 0:1], None, ALU.mult), "ve")
        TK(e.tensor_scalar(b_rep, b_rep, vec4[:, 0:1], None, ALU.mult), "ve")
    emit("vector", ve_vecs)
    ve_vecs_v = c.val("ve")

    for kb in range(ND):
        def fw_ld2(e, kb=kb):
            W(e, "ve", c.v.get(f"c_wq_{kb - 1}", ve_vecs_v))
            W(e, "ac", c.v.get(f"c_sgn_{kb - 1}", 0))
            e.dma_start(out=fwstage, in_=fw_h.ap()[kb * 128:(kb + 1) * 128, :]
                        ).then_inc(sem["fs"], 16)
        wd_v = c.inc("fs", 16)
        emit("sync", fw_ld2)

        def ac_sgn(e, kb=kb, wd_v=wd_v):
            W(e, "fs", wd_v)
            SW(e, "ac")
            TK(e.activation(awstage, fwstage, AF.Sign), "ac")
        emit("scalar", ac_sgn)
        c.v[f"c_sgn_{kb}"] = c.val("ac")

        def ve_wq(e, kb=kb, wd_v=wd_v):
            W(e, "fs", wd_v)
            W(e, "ve", ve_vecs_v)
            W(e, "ac", c.v[f"c_sgn_{kb}"])
            SW(e, "ve")
            TK(e.scalar_tensor_tensor(q1s, fwstage, -1.0, fwstage, ALU.mult,
                                     ALU.max), "ve")
            SW(e, "ve")
            TK(e.tensor_scalar(q1s, q1s, vec4[:, 1:2], None, ALU.is_gt), "ve")
            SW(e, "ve")
            TK(e.tensor_tensor(wq[:, kb * D:(kb + 1) * D], q1s, awstage, ALU.mult),
              "ve")
        emit("vector", ve_wq)
        c.v[f"c_wq_{kb}"] = c.val("ve")

    for ti in range(NT):
        b = ti // (NT // BL)
        t0 = (ti % (NT // BL)) * TT

        for tb in range(NTB):
            xb = xb2[(ti * NTB + tb) % 5]
            xs = f"xd{(ti * NTB + tb) % 5}"

            def ld(e, b=b, t0=t0, tb=tb, xb=xb, ti=ti, xs=xs):
                W(e, "ve", c.v.get(f"c_oadd_{ti - 1}", 0))
                e.dma_start(out=xb, in_=x_h.ap()[b, t0 + tb * 128:t0 + (tb + 1) * 128, :]
                            ).then_inc(sem[xs], 16)
            c.v[f"c_x_{ti}_{tb}"] = c.inc(xs, 16)
            c.v[f"c_xsem_{ti}_{tb}"] = xs
            emit("sync", ld)

        def ld_y2(e, ti=ti):
            W(e, "pe", c.v.get(f"c_attn_{ti - 1}_{ND - 1}", 0))
            W(e, "ve", c.v[f"c_wq_{ND - 1}"])
            e.dma_start(out=y2b, in_=y2_hbm.ap()[ti * 128:(ti + 1) * 128, :]
                        ).then_inc(sem["yl"], 16)
        c.v[f"c_y2_{ti}"] = c.inc("yl", 16)
        emit("sync", ld_y2)

        for tb in range(NTB):
            sti = ti * NTB + tb
            xb = xb2[sti % 5]
            xn = xnbf2[tb % 2]

            def ve_xn2(e, xb=xb, xn=xn, sti=sti, ti=ti, tb=tb):
                W(e, c.v[f"c_xsem_{ti}_{tb}"], c.v[f"c_x_{ti}_{tb}"])
                W(e, "pe", c.v.get(f"c_tp_{ti}_{tb - 2}", 0))
                SW(e, "ve")
                TK(e.tensor_scalar(xn, xb, stash[:, sti:sti + 1],
                                  stash[:, NST + sti:NST + sti + 1],
                                  ALU.add, ALU.mult), "ve")
            emit("vector", ve_xn2)
            c.v[f"c_xn_{ti}_{tb}"] = c.val("ve")

            pbk = pb[tb % 2]

            def pe_tp(e, xn=xn, pbk=pbk, ti=ti, tb=tb):
                W(e, "ve", c.v[f"c_xn_{ti}_{tb}"])
                W(e, "ve", c.v.get(f"c_tpev_{ti}_{tb - 2}", 0))
                for db in range(ND):
                    TK(e.transpose(pbk[:, db * 128:(db + 1) * 128],
                                  xn[:, db * 128:(db + 1) * 128], idbf), "pe")
            emit("tensor", pe_tp)
            c.v[f"c_tp_{ti}_{tb}"] = c.val("pe")

            def ve_tpev(e, pbk=pbk, tb=tb, ti=ti):
                W(e, "pe", c.v[f"c_tp_{ti}_{tb}"])
                SW(e, "ve")
                src = pbk.rearrange("p (d t) -> p d t", d=ND)
                dst = xnf2.rearrange("p (d t) -> p d t", d=ND)[:, :, tb * 128:(tb + 1) * 128]
                TK(e.tensor_copy(dst, src), "ve")
            emit("vector", ve_tpev)
            c.v[f"c_tpev_{ti}_{tb}"] = c.val("ve")

        for ib in range(NI):
            pwk = pf[ib % 2]

            def pe_z(e, ib=ib, pwk=pwk, ti=ti):
                W(e, "ac", ac_w2)
                W(e, "ve", c.v[f"c_tpev_{ti}_{NTB - 1}"])
                W(e, "ac", c.v.get(f"c_sz_{ti}_{ib - 2}",
                                   c.v.get(f"c_attnev_{ti - 1}_{ND - 1}", 0)))
                for kb in range(ND):
                    mm = e.matmul(pwk, w_in_z[:, (kb * NI + ib) * 128:(kb * NI + ib + 1) * 128],
                                  xnf2[:, kb * TT:(kb + 1) * TT],
                                  start=(kb == 0), stop=(kb == ND - 1))
                TK(mm, "pe")
            emit("tensor", pe_z)
            c.v[f"c_z_{ti}_{ib}"] = c.val("pe")

            def ac_sz(e, ib=ib, pwk=pwk, ti=ti):
                W(e, "pe", c.v[f"c_z_{ti}_{ib}"])
                W(e, "ve", c.v.get(f"c_yopk_{ti}_{ib - 2}", 0))
                SW(e, "ac")
                TK(e.activation(szp[ib % 2], pwk, AF.Silu), "ac")
            emit("scalar", ac_sz)
            c.v[f"c_sz_{ti}_{ib}"] = c.val("ac")

            def ve_yop(e, ib=ib, ti=ti):
                W(e, "ac", c.v[f"c_sz_{ti}_{ib}"])
                W(e, "yl", c.v[f"c_y2_{ti}"])
                SW(e, "ve")
                TK(e.tensor_tensor(yop[:, ib * TT:(ib + 1) * TT],
                                  y2b[:, ib * TT:(ib + 1) * TT], szp[ib % 2],
                                  ALU.mult), "ve")
            emit("vector", ve_yop)
            c.v[f"c_yopk_{ti}_{ib}"] = c.val("ve")
        c.v[f"c_yop_{ti}"] = c.val("ve")

        for tb in range(NTB):
            sti = ti * NTB + tb
            xn = xnbf2[tb % 2]
            xqt = xq_t[tb % 2]

            def ve_xq(e, xn=xn, sti=sti, xqt=xqt, ti=ti, tb=tb):
                xb = xb2[sti % 5]
                W(e, "pe", c.v.get(f"c_tpq_{ti}_{tb - 2}", 0))
                SW(e, "ve")
                TK(e.tensor_scalar(xn, xb, stash[:, sti:sti + 1],
                                  stash[:, NST + sti:NST + sti + 1],
                                  ALU.add, ALU.mult), "ve")
                SW(e, "ve")
                TK(e.scalar_tensor_tensor(q1s, xn,
                                         stash[:, 2 * NST + sti:2 * NST + sti + 1],
                                         g_rep, ALU.mult, ALU.mult), "ve")
                SW(e, "ve")
                TK(e.tensor_add(q1s, q1s, b_rep), "ve")
                SW(e, "ve")
                TK(e.tensor_scalar(xqt, q1s, CLIPV, -CLIPV, ALU.min, ALU.max), "ve")
            emit("vector", ve_xq)
            c.v[f"c_xq_{ti}_{tb}"] = c.val("ve")

            pbk = pb[tb % 2]

            def pe_tpq(e, xqt=xqt, pbk=pbk, ti=ti, tb=tb):
                W(e, "ve", c.v[f"c_xq_{ti}_{tb}"])
                W(e, "ve", c.v.get(f"c_tpqev_{ti}_{tb - 2}", 0))
                for db in range(ND):
                    TK(e.transpose(pbk[:, db * 128:(db + 1) * 128],
                                  xqt[:, db * 128:(db + 1) * 128], idbf), "pe")
            emit("tensor", pe_tpq)
            c.v[f"c_tpq_{ti}_{tb}"] = c.val("pe")

            def ve_tpqev(e, pbk=pbk, tb=tb, ti=ti):
                W(e, "pe", c.v[f"c_tpq_{ti}_{tb}"])
                SW(e, "ve")
                src = pbk.rearrange("p (d t) -> p d t", d=ND)
                dst = xqf.rearrange("p (d t) -> p d t", d=ND)[:, :, tb * 128:(tb + 1) * 128]
                TK(e.tensor_copy(dst, src), "ve")
            emit("vector", ve_tpqev)
            c.v[f"c_tpqev_{ti}_{tb}"] = c.val("ve")

        for db in range(ND):
            pa = pf[db % 2]
            pq = pf[2 + db % 2]
            ptr = pf[4 + db % 2]

            def pe_attn(e, db=db, pa=pa, ti=ti):
                W(e, "ve", c.v[f"c_yop_{ti}"])
                W(e, "ac", c.v.get(f"c_attnev_{ti}_{db - 2}",
                                   c.v.get(f"c_sz_{ti}_{NI - 1}", 0)))
                for kb in range(NI):
                    mm = e.matmul(pa, w_o[:, kb * D + db * 128:kb * D + (db + 1) * 128],
                                  yop[:, kb * TT:(kb + 1) * TT],
                                  start=(kb == 0), stop=(kb == NI - 1))
                TK(mm, "pe")
            emit("tensor", pe_attn)
            c.v[f"c_attn_{ti}_{db}"] = c.val("pe")

            def pe_ffn(e, db=db, pq=pq, ti=ti):
                W(e, "ve", c.v[f"c_tpqev_{ti}_{NTB - 1}"])
                W(e, "ve", c.v.get(f"c_of_{ti}_{db - 2}",
                                   c.v.get(f"c_of_{ti - 1}_{ND - 1}", 0)))
                for kb in range(ND):
                    mm = e.matmul(pq, wq[:, kb * D + db * 128:kb * D + (db + 1) * 128],
                                  xqf[:, kb * TT:(kb + 1) * TT],
                                  start=(kb == 0), stop=(kb == ND - 1))
                TK(mm, "pe")
            emit("tensor", pe_ffn)
            c.v[f"c_ffn_{ti}_{db}"] = c.val("pe")

            def ac_attnev(e, pa=pa, ti=ti, db=db):
                W(e, "pe", c.v[f"c_attn_{ti}_{db}"])
                W(e, "ve", c.v.get(f"c_of_{ti}_{db - 1}", 0))
                SW(e, "ac")
                TK(e.copy(attn_sb, pa), "ac")
            emit("scalar", ac_attnev)
            c.v[f"c_attnev_{ti}_{db}"] = c.val("ac")

            def ve_of(e, pq=pq, ti=ti, db=db):
                W(e, "pe", c.v[f"c_ffn_{ti}_{db}"])
                W(e, "ac", c.v[f"c_attnev_{ti}_{db}"])
                W(e, "pe", c.v.get(f"c_tro_{ti}_{db - 1}", 0))
                SW(e, "ve")
                TK(e.scalar_tensor_tensor(o_f, pq, vec4[:, 2:3], attn_sb,
                                         ALU.mult, ALU.add), "ve")
            emit("vector", ve_of)
            c.v[f"c_of_{ti}_{db}"] = c.val("ve")

            def pe_tro(e, db=db, ptr=ptr, ti=ti):
                W(e, "ve", c.v[f"c_of_{ti}_{db}"])
                W(e, "ve", c.v.get(f"c_oaddp_{ti}_{db - 2}",
                                   c.v.get(f"c_oaddp_{ti - 1}_{ND - 1}", 0)))
                for tb in range(NTB):
                    TK(e.transpose(ptr[:, tb * 128:(tb + 1) * 128],
                                  o_f[:, tb * 128:(tb + 1) * 128], idf), "pe")
            emit("tensor", pe_tro)
            c.v[f"c_tro_{ti}_{db}"] = c.val("pe")

            def ve_outadd(e, db=db, ptr=ptr, ti=ti):
                W(e, "pe", c.v[f"c_tro_{ti}_{db}"])
                if db == 0:
                    for q in range(NTB):
                        W(e, f"od{q}", c.v.get(f"c_odma_{ti - 1}_{q}", 0))
                SW(e, "ve")
                for tb in range(NTB):
                    xb = xb2[(ti * NTB + tb) % 5]
                    TK(e.tensor_tensor(outsb[tb][:, db * 128:(db + 1) * 128],
                                      xb[:, db * 128:(db + 1) * 128],
                                      ptr[:, tb * 128:(tb + 1) * 128], ALU.add),
                      "ve")
            emit("vector", ve_outadd)
            c.v[f"c_oaddp_{ti}_{db}"] = c.val("ve")
        c.v[f"c_oadd_{ti}"] = c.val("ve")

        for tb in range(NTB):
            def st_out(e, tb=tb, b=b, t0=t0, ti=ti):
                W(e, "ve", c.v[f"c_oadd_{ti}"])
                e.dma_start(out=out_h.ap()[b, t0 + tb * 128:t0 + (tb + 1) * 128, :],
                            in_=outsb[tb]).then_inc(sem[f"od{tb}"], 16)
            c.v[f"c_odma_{ti}_{tb}"] = c.inc(f"od{tb}", 16)
            emit("sync", st_out)

    def fin(e):
        for q in range(NTB):
            W(e, f"od{q}", c.val(f"od{q}"))
    emit("sync", fin)

    flush_sections()
    es.close()
    return nc


def kernel(**inputs):
    x = np.ascontiguousarray(np.asarray(inputs["L0_attn_inputs"], np.float32))
    nc = build_program()
    common = dict(
        w_in=np.ascontiguousarray(inputs["L0_attn_W_in"], np.float32),
        conv_w=np.ascontiguousarray(inputs["L0_attn_conv_w"], np.float32),
        conv_b=np.ascontiguousarray(inputs["L0_attn_conv_b"], np.float32),
        w_x=np.ascontiguousarray(inputs["L0_attn_W_x"], np.float32),
        w_dt=np.ascontiguousarray(inputs["L0_attn_W_dt"], np.float32),
        b_dt=np.ascontiguousarray(inputs["L0_attn_b_dt"], np.float32),
        a_log=np.ascontiguousarray(inputs["L0_attn_A_log"], np.float32),
        w_out=np.ascontiguousarray(inputs["L0_attn_W_out"], np.float32),
        fw=np.ascontiguousarray(inputs["L0_ffn_weight"], np.float32),
        lng=np.ascontiguousarray(inputs["L0_ffn_ln_gamma"], np.float32),
        lnb=np.ascontiguousarray(inputs["L0_ffn_ln_beta"], np.float32),
    )
    in_maps = [dict(common, x=np.ascontiguousarray(x[2 * cc:2 * cc + 2]))
               for cc in range(NCORES)]
    res = run_bass_kernel_spmd(nc, in_maps, core_ids=list(range(NCORES)))
    return np.concatenate([r["out"] for r in res.results], axis=0)


if __name__ == "__main__":
    build_program()
    print("build OK")


# version 20
# speedup vs baseline: 1.2440x; 1.0924x over previous
import contextlib
import numpy as np
import concourse.bass as bass
import concourse.mybir as mybir
from concourse.bass_utils import run_bass_kernel_spmd

F32 = mybir.dt.float32
BF16 = mybir.dt.bfloat16
U8 = mybir.dt.uint8
AF = mybir.ActivationFunctionType
ALU = mybir.AluOpType
AX = mybir.AxisListType

NCORES = 8
B, T, D = 16, 2048, 1024
BL = B // NCORES
I = 2048
S = 16
TT = 512
NT = BL * T // TT
NTB = TT // 128
ND = D // 128
NI = I // 128
NST = NT * NTB
QB = 128.0
CLIPV = float(np.float32(QB - 1e-5))
LN1_EPS = 1e-5
LN2_EPS = 1e-3
W_EPS = 1e-5
KB = 1024


class Ctr:
    def __init__(self):
        self.v = {}

    def inc(self, name, n):
        self.v[name] = self.v.get(name, 0) + n
        return self.v[name]

    def val(self, name):
        return self.v.get(name, 0)


class Carver:
    def __init__(self, arena_ap, nbytes):
        self.ap = arena_ap
        self.n = nbytes
        self.off = 0

    def reset(self):
        self.off = 0

    def take(self, name, cols, dtype):
        esz = 2 if dtype == BF16 else 4
        nb = cols * esz
        off = (self.off + 63) & ~63
        assert off + nb <= self.n, f"arena overflow at {name}: {off}+{nb}>{self.n}"
        self.off = off + nb
        v = self.ap[:, off:off + nb].bitcast(dtype)
        assert v.shape == (128, cols), (name, v.shape)
        return v

    def at(self, off, name, cols, dtype):
        esz = 2 if dtype == BF16 else 4
        nb = cols * esz
        assert off % 64 == 0 and off + nb <= self.n, name
        v = self.ap[:, off:off + nb].bitcast(dtype)
        assert v.shape == (128, cols), (name, v.shape)
        return v


def build_program():
    nc = bass.Bass("TRN2", target_bir_lowering=False, debug=False)

    x_h = nc.dram_tensor("x", [BL, T, D], F32, kind="ExternalInput")
    w_in_h = nc.dram_tensor("w_in", [D, 2 * I], F32, kind="ExternalInput")
    conv_w_h = nc.dram_tensor("conv_w", [4, 1, I], F32, kind="ExternalInput")
    conv_b_h = nc.dram_tensor("conv_b", [I], F32, kind="ExternalInput")
    w_x_h = nc.dram_tensor("w_x", [I, D + 2 * S], F32, kind="ExternalInput")
    w_dt_h = nc.dram_tensor("w_dt", [D, I], F32, kind="ExternalInput")
    b_dt_h = nc.dram_tensor("b_dt", [I], F32, kind="ExternalInput")
    a_log_h = nc.dram_tensor("a_log", [I, S], F32, kind="ExternalInput")
    w_out_h = nc.dram_tensor("w_out", [I, D], F32, kind="ExternalInput")
    fw_h = nc.dram_tensor("fw", [D, D], F32, kind="ExternalInput")
    lng_h = nc.dram_tensor("lng", [D], F32, kind="ExternalInput")
    lnb_h = nc.dram_tensor("lnb", [D], F32, kind="ExternalInput")
    out_h = nc.dram_tensor("out", [BL, T, D], F32, kind="ExternalOutput")

    u_hbm = nc.dram_tensor("u_scr", [NT * 128, NI * TT], BF16)
    y2_hbm = nc.dram_tensor("y2_scr", [NT * 128, NI * TT], BF16)

    idf = nc.alloc_sbuf_tensor("idf", [128, 128], F32).ap()
    idbf = nc.alloc_sbuf_tensor("idbf", [128, 128], BF16).ap()
    ones1 = nc.alloc_sbuf_tensor("ones1", [1, 128], F32).ap()
    onesc = nc.alloc_sbuf_tensor("onesc", [128, 1], F32).ap()
    g_rep = nc.alloc_sbuf_tensor("g_rep", [128, D], F32).ap()
    b_rep = nc.alloc_sbuf_tensor("b_rep", [128, D], F32).ap()
    stash = nc.alloc_sbuf_tensor("stash", [128, 3 * NST], F32).ap()
    hc = nc.alloc_sbuf_tensor("hc", [128, NI * S], F32).ap()
    a_sb = nc.alloc_sbuf_tensor("a_sb", [128, NI * S], F32).ap()
    cw_sb = nc.alloc_sbuf_tensor("cw_sb", [128, 4 * NI], F32).ap()
    cb_sb = nc.alloc_sbuf_tensor("cb_sb", [128, NI], F32).ap()
    bdt_sb = nc.alloc_sbuf_tensor("bdt_sb", [128, NI], F32).ap()
    tails = nc.alloc_sbuf_tensor("tails", [128, 3 * NI], F32).ap()
    grun = nc.alloc_sbuf_tensor("grun", [128, 2], F32).ap()
    smll = nc.alloc_sbuf_tensor("smll", [1, 128], F32).ap()
    vec4 = nc.alloc_sbuf_tensor("vec4", [128, 4], F32).ap()
    tvec = nc.alloc_sbuf_tensor("tvec", [128, 48], F32).ap()
    cbank = nc.alloc_sbuf_tensor("cbank", [128, 4], F32).ap()
    fbv = nc.alloc_sbuf_tensor("fbv", [1, 4], F32).ap()
    fba = nc.alloc_sbuf_tensor("fba", [1, 4], F32).ap()

    arena_bytes = (nc.sbuf_bytes_remaining - 192) & ~63
    arena_t = nc.alloc_sbuf_tensor("arena", [128, arena_bytes], U8)
    ar = Carver(arena_t.ap(), arena_bytes)

    pf = [nc.alloc_psum_tensor(f"pf{i}", [128, 512], F32).ap() for i in range(6)]
    pb = [nc.alloc_psum_tensor(f"pb{i}", [128, 1024], BF16).ap() for i in range(2)]

    es = contextlib.ExitStack()
    sem = {}
    for name in (["md", "ve", "ac", "pe", "gp", "ph", "us", "ys", "yl", "fs"]
                 + [f"xd{i}" for i in range(5)] + [f"ws{i}" for i in range(2)]
                 + [f"ud{i}" for i in range(2)] + [f"od{i}" for i in range(4)]):
        sem[name] = es.enter_context(nc.semaphore(name))
    block = es.enter_context(nc.Block())
    c = Ctr()

    class InstrRec:
        __slots__ = ("name", "args", "kwargs", "incs")

        def __init__(self, name, args, kwargs):
            self.name = name
            self.args = args
            self.kwargs = kwargs
            self.incs = []

        def then_inc(self, s, n):
            self.incs.append((s, n))
            return self

    class RecEngine:
        def __init__(self, calls):
            self._calls = calls

        def __getattr__(self, name):
            def call(*a, **k):
                r = InstrRec(name, a, k)
                self._calls.append(r)
                return r
            return call

    PEND = {n: [] for n in ["sync", "vector", "scalar", "tensor", "gpsimd"]}

    def emit(engname, fn):
        if engname == "vector":
            def wrapped(e, fn=fn):
                fn(e)
                e.wait_ge(sem["ve"], c.val("ve"))
                TK(e.memset(fbv[0:1, 0:1], 0.0), "ve")
            fn = wrapped
        fn(RecEngine(PEND[engname]))

    def flush_sections():
        def replay(calls, e):
            for r in calls:
                instr = getattr(e, r.name)(*r.args, **r.kwargs)
                for s, n in r.incs:
                    instr.then_inc(s, n)
        for name in ["sync", "vector", "scalar", "tensor", "gpsimd"]:
            getattr(block, name)(
                lambda e, cs=PEND[name]: replay(cs, e))

    def W(e, name, val):
        assert val <= c.val(name), f"wait on future {name}: {val} > {c.val(name)}"
        if val > 0:
            e.wait_ge(sem[name], val)

    def TK(instr, s):
        instr.then_inc(sem[s], 1)
        return c.inc(s, 1)

    def SW(e, s):
        e.wait_ge(sem[s], c.val(s))



    def weight_load_cast(dst_bf, src_ap_fn, stage_f32, half_cols, kblks, row_cols):
        assert row_cols <= half_cols
        for kb in range(kblks):
            h0 = (kb % 2) * half_cols
            ws = f"ws{kb % 2}"

            def ld(e, kb=kb, h0=h0, ws=ws):
                W(e, "ac", c.val("ac"))
                e.dma_start(out=stage_f32[:, h0:h0 + row_cols],
                            in_=src_ap_fn(kb)).then_inc(sem[ws], 16)
            wd_after = c.inc(ws, 16)
            emit("sync", ld)

            def cast(e, kb=kb, h0=h0, wd_after=wd_after, ws=ws):
                W(e, ws, wd_after)
                SW(e, "ac")
                if kb == 0:
                    W(e, "ve", c.val("ve"))
                    W(e, "pe", c.val("pe"))
                TK(e.copy(dst_bf[:, kb * row_cols:(kb + 1) * row_cols],
                          stage_f32[:, h0:h0 + row_cols]), "ac")
            emit("scalar", cast)

    def barrier():
        target = c.val("ph") + 5
        for eng in ["sync", "vector", "scalar", "tensor", "gpsimd"]:
            emit(eng, lambda e: e.drain().then_inc(sem["ph"], 1))
        c.inc("ph", 5)
        for eng in ["sync", "vector", "scalar", "tensor", "gpsimd"]:
            emit(eng, lambda e, t=target: e.wait_ge(sem["ph"], t))

    iotat = ar.take("iotat", 128, F32)
    lngr_c = ar.take("lngr", D, F32)
    lnbr_c = ar.take("lnbr", D, F32)
    lngr = lngr_c[0:1, :]
    lnbr = lnbr_c[0:1, :]

    def p_dma(e):
        for ib in range(NI):
            e.dma_start(out=a_sb[:, ib * S:(ib + 1) * S],
                        in_=a_log_h.ap()[ib * 128:(ib + 1) * 128, :]).then_inc(sem["md"], 16)
            e.dma_start(out=cb_sb[:, ib:ib + 1],
                        in_=conv_b_h.ap()[ib * 128:(ib + 1) * 128].unsqueeze(1)).then_inc(sem["md"], 16)
            e.dma_start(out=bdt_sb[:, ib:ib + 1],
                        in_=b_dt_h.ap()[ib * 128:(ib + 1) * 128].unsqueeze(1)).then_inc(sem["md"], 16)
            for k in range(4):
                e.dma_start(out=cw_sb[:, k * NI + ib:k * NI + ib + 1],
                            in_=conv_w_h.ap()[k, 0, ib * 128:(ib + 1) * 128].unsqueeze(1)).then_inc(sem["md"], 16)
        e.dma_start(out=lngr, in_=lng_h.ap().unsqueeze(0)).then_inc(sem["md"], 16)
        e.dma_start(out=lnbr, in_=lnb_h.ap().unsqueeze(0)).then_inc(sem["md"], 16)
    n_pre = NI * 7 + 2
    md_pre = c.inc("md", 16 * n_pre)
    emit("sync", p_dma)

    def p_iota(e):
        e.iota(iotat, [[1, 128]], channel_multiplier=-1,
               allow_small_or_imprecise_dtypes=True).then_inc(sem["gp"], 1)
    gp_iota = c.inc("gp", 1)
    emit("gpsimd", p_iota)

    def p_ids(e):
        W(e, "gp", gp_iota)
        TK(e.tensor_scalar(idf, iotat, 0.0, None, ALU.is_equal), "ve")
        SW(e, "ve")
        TK(e.tensor_copy(idbf, idf), "ve")
        TK(e.memset(ones1, 1.0), "ve")
        TK(e.memset(onesc, 1.0), "ve")
        TK(e.memset(grun, 0.0), "ve")
        TK(e.memset(cbank[:, 0:1], LN1_EPS), "ve")
        TK(e.memset(cbank[:, 1:2], LN2_EPS), "ve")
        TK(e.memset(cbank[:, 2:3], -0.5), "ve")
        TK(e.memset(cbank[:, 3:4], 1.0 / D), "ve")
    emit("vector", p_ids)
    ve_ids = c.val("ve")

    def p_aexp(e):
        W(e, "md", md_pre)
        TK(e.activation(a_sb, a_sb, AF.Exp), "ac")
    emit("scalar", p_aexp)
    ac_aexp = c.val("ac")

    def p_aneg(e):
        W(e, "ac", ac_aexp)
        SW(e, "ve")
        TK(e.tensor_scalar(a_sb, a_sb, -1.0, None, ALU.mult), "ve")
    emit("vector", p_aneg)

    def p_grep_mm(e):
        W(e, "md", md_pre)
        W(e, "ve", ve_ids)
        e.matmul(pf[2], ones1, lngr[0:1, 0:512], start=True, stop=True)
        e.matmul(pf[3], ones1, lngr[0:1, 512:1024], start=True, stop=True)
        e.matmul(pf[4], ones1, lnbr[0:1, 0:512], start=True, stop=True)
        TK(e.matmul(pf[5], ones1, lnbr[0:1, 512:1024], start=True, stop=True), "pe")
    emit("tensor", p_grep_mm)
    pe_grep = c.val("pe")

    def p_grep_ev(e):
        W(e, "pe", pe_grep)
        SW(e, "ve")
        TK(e.tensor_copy(g_rep[:, 0:512], pf[2]), "ve")
        TK(e.tensor_copy(g_rep[:, 512:1024], pf[3]), "ve")
        TK(e.tensor_copy(b_rep[:, 0:512], pf[4]), "ve")
        TK(e.tensor_copy(b_rep[:, 512:1024], pf[5]), "ve")
    emit("vector", p_grep_ev)

    ar.reset()
    w_in_x = ar.take("w_in_x", ND * I, BF16)
    xbuf = [ar.take(f"x{i}", D, F32) for i in range(5)]
    xnbf = [ar.take(f"xnbf{i}", D, BF16) for i in range(4)]
    xnsq = ar.take("xnsq", D, BF16)
    g1s = ar.take("g1s", D, F32)
    g2s = ar.take("g2s", D, BF16)
    xnf = ar.take("xnf", ND * TT, BF16)
    dgw = ar.take("dgw", 4 * NI * 128, BF16)
    xpe = [ar.take(f"xpe{i}", 516, BF16) for i in range(2)]
    tlb = nc.alloc_sbuf_tensor("tlb", [128, 3 * NI], BF16).ap()
    ubf_off = (ar.off + 63) & ~63
    ubf = ar.take("ubf", NI * TT, BF16)
    cacc = [ar.take(f"cacc{i}", 512, F32) for i in range(2)]
    wstage = ar.at(ubf_off, "wstage", 2 * 2048, F32)

    weight_load_cast(w_in_x, lambda kb: w_in_h.ap()[kb * 128:(kb + 1) * 128, 0:I],
                     wstage, 2048, ND, I)
    ac_wa = c.val("ac")

    def p1a_dgw(e):
        W(e, "md", md_pre)
        SW(e, "ve")
        for k in range(4):
            for ib in range(NI):
                TK(e.tensor_scalar(dgw[:, (k * NI + ib) * 128:(k * NI + ib + 1) * 128],
                                   idbf, cw_sb[:, k * NI + ib:k * NI + ib + 1],
                                   None, ALU.mult), "ve")
    emit("vector", p1a_dgw)
    ve_dgw = c.val("ve")

    def p1a_s1(ti):
        b = ti // (NT // BL)
        t0 = (ti % (NT // BL)) * TT

        for tb in range(NTB):
            sti = ti * NTB + tb
            xb = xbuf[sti % 5]
            xs = f"xd{sti % 5}"

            def ld(e, b=b, t0=t0, tb=tb, xb=xb, sti=sti, xs=xs):
                W(e, "ve", c.v.get(f"a_xn_{sti - 5}", 0))
                e.dma_start(out=xb, in_=x_h.ap()[b, t0 + tb * 128:t0 + (tb + 1) * 128, :]
                            ).then_inc(sem[xs], 16)
            c.v[f"a_x_{sti}"] = c.inc(xs, 16)
            c.v[f"a_xsem_{sti}"] = xs
            emit("sync", ld)

        for tb in range(NTB):
            sti = ti * NTB + tb
            xb = xbuf[sti % 5]
            tvb = (ti % 2) * 24 + tb * 6

            def ve_ln(e, xb=xb, sti=sti, tvb=tvb, tb=tb, ti=ti):
                W(e, c.v[f"a_xsem_{sti}"], c.v[f"a_x_{sti}"])
                if tb == 0:
                    W(e, "ac", c.v.get(f"a_r2t_{ti - 2}", 0))
                SW(e, "ve")
                T = tvec
                TK(e.scalar_tensor_tensor(xnsq, xb, 1.0, xb, ALU.mult, ALU.mult,
                                          accum_out=T[:, tvb:tvb + 1]), "ve")
                TK(e.reduce_sum(T[:, tvb + 1:tvb + 2], xb, AX.X), "ve")
                SW(e, "ve")
                TK(e.tensor_scalar(stash[:, sti:sti + 1], T[:, tvb + 1:tvb + 2],
                                   -1.0 / D, None, ALU.mult), "ve")
                TK(e.tensor_scalar(T[:, tvb + 2:tvb + 3], T[:, tvb:tvb + 1],
                                   1.0 / D, None, ALU.mult), "ve")
                SW(e, "ve")
                TK(e.scalar_tensor_tensor(T[:, tvb + 3:tvb + 4], stash[:, sti:sti + 1],
                                          1.0, stash[:, sti:sti + 1], ALU.mult,
                                          ALU.mult), "ve")
                SW(e, "ve")
                TK(e.tensor_sub(T[:, tvb + 2:tvb + 3], T[:, tvb + 2:tvb + 3],
                                T[:, tvb + 3:tvb + 4]), "ve")
            emit("vector", ve_ln)
            c.v[f"a_var_{sti}"] = c.val("ve")

        def ac_ln1(e, ti=ti):
            for tb in range(NTB):
                sti = ti * NTB + tb
                tvb = (ti % 2) * 24 + tb * 6
                W(e, "ve", c.v[f"a_var_{sti}"])
                SW(e, "ac")
                TK(e.activation(tvec[:, tvb + 3:tvb + 4], tvec[:, tvb + 2:tvb + 3],
                                AF.Ln, bias=cbank[:, 0:1]), "ac")
        emit("scalar", ac_ln1)

        def ac_exp1(e, ti=ti):
            for tb in range(NTB):
                sti = ti * NTB + tb
                tvb = (ti % 2) * 24 + tb * 6
                SW(e, "ac")
                TK(e.activation(stash[:, NST + sti:NST + sti + 1],
                                tvec[:, tvb + 3:tvb + 4], AF.Exp,
                                scale=cbank[:, 2:3]), "ac")
                c.v[f"a_rstd_{sti}"] = c.val("ac")
        emit("scalar", ac_exp1)

    def p1a_s2(ti):
        for tb in range(NTB):
            sti = ti * NTB + tb
            xb = xbuf[sti % 5]
            xn = xnbf[tb]
            tvb = (ti % 2) * 24 + tb * 6

            def ve_xn(e, xb=xb, xn=xn, sti=sti, tvb=tvb):
                W(e, "ac", c.v[f"a_rstd_{sti}"])
                W(e, "pe", c.v.get(f"a_tp_{sti - 4}", 0))
                SW(e, "ve")
                TK(e.tensor_scalar(xn, xb, stash[:, sti:sti + 1],
                                   stash[:, NST + sti:NST + sti + 1],
                                   ALU.add, ALU.mult), "ve")
                SW(e, "ve")
                TK(e.scalar_tensor_tensor(xnsq, xn, 1.0, xn, ALU.mult, ALU.mult,
                                          accum_out=tvec[:, tvb + 4:tvb + 5]), "ve")
            emit("vector", ve_xn)
            c.v[f"a_xn_{sti}"] = c.val("ve")
            c.v[f"a_v2_{sti}"] = c.val("ve")

        def ac_ln2(e, ti=ti):
            for tb in range(NTB):
                sti = ti * NTB + tb
                tvb = (ti % 2) * 24 + tb * 6
                W(e, "ve", c.v[f"a_v2_{sti}"])
                SW(e, "ac")
                TK(e.activation(tvec[:, tvb + 5:tvb + 6], tvec[:, tvb + 4:tvb + 5],
                                AF.Ln, bias=cbank[:, 1:2], scale=cbank[:, 3:4]),
                   "ac")
        emit("scalar", ac_ln2)

        def ac_exp2(e, ti=ti):
            for tb in range(NTB):
                sti = ti * NTB + tb
                tvb = (ti % 2) * 24 + tb * 6
                SW(e, "ac")
                TK(e.activation(stash[:, 2 * NST + sti:2 * NST + sti + 1],
                                tvec[:, tvb + 5:tvb + 6], AF.Exp,
                                scale=cbank[:, 2:3]), "ac")
                c.v[f"a_r2_{sti}"] = c.val("ac")
        emit("scalar", ac_exp2)
        c.v[f"a_r2t_{ti}"] = c.val("ac")

        for tb in range(NTB):
            sti = ti * NTB + tb
            xn = xnbf[tb]
            pbk = pb[tb % 2]

            def pe_tp(e, xn=xn, pbk=pbk, sti=sti, tb=tb, ti=ti):
                W(e, "ve", c.v[f"a_xn_{sti}"])
                W(e, "ac", c.v.get(f"a_tpev_{ti}_{tb - 2}",
                                   c.v.get(f"a_tpev_{ti - 1}_{tb + 2}", 0)))
                for db in range(ND):
                    TK(e.transpose(pbk[:, db * 128:(db + 1) * 128],
                                   xn[:, db * 128:(db + 1) * 128], idbf), "pe")
            emit("tensor", pe_tp)
            c.v[f"a_tp_{sti}"] = c.val("pe")

            def ac_tpev(e, pbk=pbk, tb=tb, sti=sti, ti=ti):
                W(e, "pe", c.v[f"a_tp_{sti}"])
                W(e, "pe", c.v.get(f"a_mm_{ti - 1}_{NI - 1}", 0))
                SW(e, "ac")
                src_ = pbk.rearrange("p (d t) -> p d t", d=ND)
                dst = xnf.rearrange("p (d t) -> p d t", d=ND)[:, :, tb * 128:(tb + 1) * 128]
                TK(e.copy(dst, src_), "ac")
            emit("scalar", ac_tpev)
            c.v[f"a_tpev_{ti}_{tb}"] = c.val("ac")
        c.v[f"a_xnfd_{ti}"] = c.val("ac")

    def p1a_s3(ti):
        first = (ti % (NT // BL)) == 0
        for tb in range(NTB):
            sti = ti * NTB + tb
            xn = xnbf[tb]

            def ve_g(e, xn=xn, sti=sti):
                W(e, "ac", c.v[f"a_r2_{sti}"])
                SW(e, "ve")
                TK(e.scalar_tensor_tensor(g1s, xn,
                                          stash[:, 2 * NST + sti:2 * NST + sti + 1],
                                          g_rep, ALU.mult, ALU.mult), "ve")
                SW(e, "ve")
                TK(e.tensor_add(g2s, g1s, b_rep), "ve")
                SW(e, "ve")
                TK(e.tensor_reduce(grun[:, 1:2], g2s, AX.X, ALU.max,
                                   apply_absolute_value=True), "ve")
                SW(e, "ve")
                TK(e.tensor_tensor(grun[:, 0:1], grun[:, 0:1], grun[:, 1:2],
                                   ALU.max), "ve")
            emit("vector", ve_g)
        ve_xnf_done = c.v[f"a_xnfd_{ti}"]

        for ib in range(NI):
            pwk = pf[ib % 2]

            def pe_mm(e, ib=ib, pwk=pwk, ve_xnf_done=ve_xnf_done, ti=ti):
                W(e, "ac", ac_wa)
                W(e, "ac", ve_xnf_done)
                W(e, "ac", c.v.get(f"a_evx_{ti}_{ib - 2}",
                                   c.v.get(f"a_evx_{ti - 1}_{ib - 2 + NI}", 0)))
                for kb in range(ND):
                    mm = e.matmul(pwk, w_in_x[:, (kb * NI + ib) * 128:(kb * NI + ib + 1) * 128],
                                  xnf[:, kb * TT:(kb + 1) * TT],
                                  start=(kb == 0), stop=(kb == ND - 1))
                TK(mm, "pe")
            emit("tensor", pe_mm)
            c.v[f"a_mm_{ti}_{ib}"] = c.val("pe")

            def ac_evx(e, ib=ib, pwk=pwk, first=first, ti=ti):
                W(e, "pe", c.v[f"a_mm_{ti}_{ib}"])
                W(e, "pe", c.v.get(f"a_cv_{ti}_{ib - 2}", 0))
                SW(e, "ac")
                xp = xpe[ib % 2]
                TK(e.activation(xp[:, 3:515], pwk, AF.Copy), "ac")
                SW(e, "ac")
                if first:
                    TK(e.activation(xp[:, 0:3], xp[:, 0:3], AF.Copy, scale=0.0),
                       "ac")
                else:
                    TK(e.activation(xp[:, 0:3], tlb[:, ib * 3:(ib + 1) * 3],
                                    AF.Copy), "ac")
                SW(e, "ac")
                TK(e.activation(tlb[:, ib * 3:(ib + 1) * 3], xp[:, 512:515],
                                AF.Copy), "ac")
            emit("scalar", ac_evx)
            c.v[f"a_evx_{ti}_{ib}"] = c.val("ac")

            def pe_cv(e, ib=ib, ti=ti):
                W(e, "ac", c.v[f"a_evx_{ti}_{ib}"])
                W(e, "ve", ve_dgw)
                W(e, "ac", c.v.get(f"a_u_{ti}_{ib - 2}", 0))
                xp = xpe[ib % 2]
                pcv = pf[2 + ib % 2]
                for k in range(4):
                    mm = e.matmul(pcv, dgw[:, (k * NI + ib) * 128:(k * NI + ib + 1) * 128],
                                  xp[:, k:k + 512], start=(k == 0), stop=(k == 3))
                TK(mm, "pe")
            emit("tensor", pe_cv)
            c.v[f"a_cv_{ti}_{ib}"] = c.val("pe")

            def ac_silu(e, ib=ib, ti=ti):
                W(e, "pe", c.v[f"a_cv_{ti}_{ib}"])
                W(e, "us", c.v.get(f"a_udma_{ti - 1}", 0))
                SW(e, "ac")
                TK(e.activation(ubf[:, ib * TT:(ib + 1) * TT], pf[2 + ib % 2],
                                AF.Silu, bias=cb_sb[:, ib:ib + 1]), "ac")
            emit("scalar", ac_silu)
            c.v[f"a_u_{ti}_{ib}"] = c.val("ac")

        def st_u(e, ti=ti):
            W(e, "ac", c.v[f"a_u_{ti}_{NI - 1}"])
            e.dma_start(out=u_hbm.ap()[ti * 128:(ti + 1) * 128, :], in_=ubf
                        ).then_inc(sem["us"], 16)
        c.v[f"a_udma_{ti}"] = c.inc("us", 16)
        emit("sync", st_u)

    for k in range(NT + 2):
        if 0 <= k - 2 < NT:
            p1a_s3(k - 2)
        if 0 <= k - 1 < NT:
            p1a_s2(k - 1)
        if k < NT:
            p1a_s1(k)

    emit("sync", lambda e, v=c.val("us"): e.wait_ge(sem["us"], v))
    barrier()

    ar.reset()
    w_x = ar.take("w_x", NI * 1056, BF16)
    w_dt = ar.take("w_dt", ND * I, BF16)
    uin = [ar.take(f"uin{i}", NI * TT, BF16) for i in range(2)]
    dtin = [ar.take(f"dtin{i}", ND * TT, BF16) for i in range(2)]
    bcb = ar.take("bcb", 512, BF16)
    iotae_c = ar.take("iotae", 128, F32)
    eb_c = ar.take("eb", 2 * S * 128, BF16)
    iotae = iotae_c[0:32, :]
    eb = eb_c[0:32, :]
    brep_off = (ar.off + 63) & ~63
    brep = ar.take("brep", S * TT, BF16)
    crep = ar.take("crep", S * TT, BF16)
    wda = ar.take("wda", 4 * 512, F32)
    dub = ar.take("dub", 512, BF16)
    da2 = [ar.take(f"da2{i}", 2 * 512, F32) for i in range(3)]
    dbu2 = [ar.take(f"dbu2{i}", 2 * 512, BF16) for i in range(2)]
    h2 = [ar.take(f"h2{i}", 2 * 512, BF16) for i in range(2)]
    hk2 = [ar.take(f"hk2{i}", 2 * 512, BF16) for i in range(3)]
    y2st = ar.take("y2st", 4 * 512, BF16)
    wstage_b = ar.at(brep_off, "wstage_b", 2 * 2048, F32)

    def b_iota(e):
        e.iota(iotae, [[0, 128]], channel_multiplier=1,
               allow_small_or_imprecise_dtypes=True).then_inc(sem["gp"], 1)
    gp_eb = c.inc("gp", 1)
    emit("gpsimd", b_iota)

    def b_eb(e):
        W(e, "gp", gp_eb)
        SW(e, "ve")
        for s in range(2 * S):
            TK(e.tensor_scalar(eb[:, s * 128:(s + 1) * 128], iotae, float(s), None,
                              ALU.is_equal), "ve")
    emit("vector", b_eb)

    weight_load_cast(w_x, lambda kb: w_x_h.ap()[kb * 128:(kb + 1) * 128, :],
                     wstage_b, 2048, NI, 1056)
    weight_load_cast(w_dt, lambda kb: w_dt_h.ap()[kb * 128:(kb + 1) * 128, :],
                     wstage_b, 2048, ND, I)
    ve_wb = c.val("ve")
    ac_wb = c.val("ac")

    def p1b_init(e):
        SW(e, "ve")
        TK(e.memset(hc, 0.0), "ve")
    emit("vector", p1b_init)

    def emit_ld_u(ti):
        ub_ = uin[ti % 2]
        usem = f"ud{ti % 2}"

        def ld_u(e, ti=ti, ub_=ub_, usem=usem):
            W(e, "ac", c.v.get(f"b_y2_{ti - 2}_{NI - 1}", 0))
            W(e, "ve", c.v.get(f"b_du_{ti - 2}_{NI - 1}", 0))
            e.dma_start(out=ub_, in_=u_hbm.ap()[ti * 128:(ti + 1) * 128, :]
                        ).then_inc(sem[usem], 16)
        c.v[f"b_u_{ti}"] = c.inc(usem, 16)
        c.v[f"b_usem_{ti}"] = usem
        emit("sync", ld_u)

    def emit_wx_one(ti, nb):
        ub_ = uin[ti % 2]
        dtb = dtin[ti % 2]
        pwk = pf[nb % 2]

        def pe_wx(e, nb=nb, pwk=pwk, ti=ti, ub_=ub_):
            W(e, "ac", ac_wb)
            W(e, c.v[f"b_usem_{ti}"], c.v[f"b_u_{ti}"])
            W(e, "ac", c.v.get(f"b_dtin_{ti}_{nb - 2}",
                               c.v.get(f"b_dtin_{ti - 1}_{ND - 1}", 0)))
            for kb in range(NI):
                mm = e.matmul(pwk, w_x[:, kb * 1056 + nb * 128:kb * 1056 + (nb + 1) * 128],
                              ub_[:, kb * TT:(kb + 1) * TT],
                              start=(kb == 0), stop=(kb == NI - 1))
            TK(mm, "pe")
        emit("tensor", pe_wx)
        c.v[f"b_wx_{ti}_{nb}"] = c.val("pe")

        def ac_dtin(e, nb=nb, pwk=pwk, ti=ti, dtb=dtb):
            W(e, "pe", c.v[f"b_wx_{ti}_{nb}"])
            if nb == 0:
                W(e, "pe", c.v.get(f"b_wdt_{ti - 2}_{NI - 1}", 0))
            SW(e, "ac")
            TK(e.copy(dtb[:, nb * TT:(nb + 1) * TT], pwk), "ac")
        emit("scalar", ac_dtin)
        c.v[f"b_dtin_{ti}_{nb}"] = c.val("ac")

    def emit_y2(ti, ib):
        def ac_y2(e, ib=ib, ti=ti):
            pyk = pf[4 + (ib % 2)]
            W(e, "pe", c.v[f"b_idmm_{ti}_{ib}_{7}"])
            W(e, "ys", c.v.get(f"b_ydma_{ti}_{ib // 4 - 1}",
                               c.v.get(f"b_ydma_{ti - 1}_{3}", 0)))
            SW(e, "ac")
            TK(e.copy(y2st[:, (ib % 4) * 512:(ib % 4 + 1) * 512], pyk), "ac")
        emit("scalar", ac_y2)
        c.v[f"b_y2_{ti}_{ib}"] = c.val("ac")

        if ib % 4 == 3:
            def st_y2(e, ib=ib, ti=ti):
                W(e, "ac", c.v[f"b_y2_{ti}_{ib}"])
                e.dma_start(
                    out=y2_hbm.ap()[ti * 128:(ti + 1) * 128,
                                    (ib - 3) * TT:(ib + 1) * TT],
                    in_=y2st).then_inc(sem["ys"], 16)
            c.v[f"b_ydma_{ti}_{ib // 4}"] = c.inc("ys", 16)
            emit("sync", st_y2)

    emit_ld_u(0)
    for nb0 in range(ND):
        emit_wx_one(0, nb0)

    for ti in range(NT):
        ub = uin[ti % 2]
        first = (ti % (NT // BL)) == 0
        dtb = dtin[ti % 2]

        if first and ti > 0:
            def rhc(e):
                SW(e, "ve")
                TK(e.memset(hc, 0.0), "ve")
            emit("vector", rhc)

        def pe_bc(e, ti=ti):
            W(e, "ve", c.v.get(f"b_bcev_{ti - 1}", 0))
            W(e, "ac", c.v.get(f"b_y2_{ti - 1}_{NI - 2}", 0))
            for kb in range(NI):
                mm = e.matmul(pf[4][0:32, :], w_x[:, kb * 1056 + 1024:kb * 1056 + 1056],
                              ub[:, kb * TT:(kb + 1) * TT],
                              start=(kb == 0), stop=(kb == NI - 1))
            TK(mm, "pe")
        emit("tensor", pe_bc)
        c.v[f"b_bc_{ti}"] = c.val("pe")

        def ve_bcev(e, ti=ti):
            W(e, "pe", c.v[f"b_bc_{ti}"])
            SW(e, "ve")
            TK(e.tensor_copy(bcb[0:32, 0:512], pf[4][0:32, :]), "ve")
        emit("vector", ve_bcev)
        c.v[f"b_bcev_{ti}"] = c.val("ve")

        for s in range(S):
            def pe_br(e, s=s, ti=ti):
                W(e, "ve", c.v[f"b_bcev_{ti}"])
                W(e, "ve", c.v.get(f"b_brev_{ti}_{s - 1}", 0))
                if s == 0:
                    W(e, "ac", c.v.get(f"b_y2_{ti - 1}_{NI - 1}", 0))
                e.matmul(pf[4], eb[:, s * 128:(s + 1) * 128], bcb[0:32, 0:512],
                         start=True, stop=True)
                TK(e.matmul(pf[5], eb[:, (S + s) * 128:(S + s + 1) * 128],
                           bcb[0:32, 0:512], start=True, stop=True), "pe")
            emit("tensor", pe_br)
            c.v[f"b_br_{ti}_{s}"] = c.val("pe")

            def ve_brev(e, s=s, ti=ti):
                W(e, "pe", c.v[f"b_br_{ti}_{s}"])
                SW(e, "ve")
                TK(e.tensor_copy(brep[:, s * TT:(s + 1) * TT], pf[4]), "ve")
                TK(e.tensor_copy(crep[:, s * TT:(s + 1) * TT], pf[5]), "ve")
            emit("vector", ve_brev)
            c.v[f"b_brev_{ti}_{s}"] = c.val("ve")
        ve_brep_done = c.val("ve")

        def SCNv(tj, ib, g):
            while g < 0:
                g += 8
                ib -= 1
            while ib < 0:
                ib += NI
                tj -= 1
            if tj < 0:
                return 0
            return c.v.get(f"b_scan_{tj}_{ib}_{g}", 0)

        def YMv(tj, ib, g):
            while g < 0:
                g += 8
                ib -= 1
            while ib < 0:
                ib += NI
                tj -= 1
            if tj < 0:
                return 0
            return c.v.get(f"b_ym_{tj}_{ib}_{g}", 0)

        def IDv(tj, ib, g):
            while g < 0:
                g += 8
                ib -= 1
            while ib < 0:
                ib += NI
                tj -= 1
            if tj < 0:
                return 0
            return c.v.get(f"b_idmm_{tj}_{ib}_{g}", 0)

        def emit_hy(e, tj, ib, g):
            hb = h2[g % 2]
            hck = hk2[g % 3]
            TK(e.tensor_copy(hc[:, ib * S + g * 2:ib * S + g * 2 + 2],
                             hb.rearrange("p (s t) -> p s t", s=2)[:, :, 511]),
               "ve")
            TK(e.tensor_tensor(hck, hb,
                               crep[:, (g * 2) * TT:(g * 2 + 2) * TT],
                               ALU.mult), "ve")

        def emit_id(tj, ib, g):
            pyk = pf[4 + (ib % 2)]
            hck = hk2[g % 3]

            def pe_id(e, ib=ib, g=g, hck=hck, pyk=pyk, tj=tj):
                W(e, "ve", c.v[f"b_ym_{tj}_{ib}_{g}"])
                if g == 0 and ib >= 2:
                    W(e, "ac", c.v[f"b_y2_{tj}_{ib - 2}"])
                for s in range(2):
                    mm = e.matmul(pyk, idbf, hck[:, s * 512:(s + 1) * 512],
                                  start=(g == 0 and s == 0), stop=False)
                if g == 7:
                    mm = e.matmul(pyk, idbf, ub[:, ib * TT:(ib + 1) * TT],
                                  start=False, stop=True)
                TK(mm, "pe")
            emit("tensor", pe_id)
            c.v[f"b_idmm_{tj}_{ib}_{g}"] = c.val("pe")

        for q in range(4):
            for ib in range(4 * q, 4 * q + 4):
                pwk = pf[2 + ib % 2]

                def pe_wdt(e, ib=ib, pwk=pwk, ti=ti, dtb=dtb):
                    W(e, "ac", c.v[f"b_dtin_{ti}_{ND - 1}"])
                    W(e, "ac", c.v.get(f"b_sp_{ti}_{ib - 2}",
                                       c.v.get(f"b_sp_{ti - 1}_{ib + 14}", 0)))
                    for kb in range(ND):
                        mm = e.matmul(pwk, w_dt[:, kb * I + ib * 128:kb * I + (ib + 1) * 128],
                                      dtb[:, kb * TT:(kb + 1) * TT],
                                      start=(kb == 0), stop=(kb == ND - 1))
                    TK(mm, "pe")
                emit("tensor", pe_wdt)
                c.v[f"b_wdt_{ti}_{ib}"] = c.val("pe")

                def ac_spexp(e, ib=ib, pwk=pwk, ti=ti):
                    W(e, "pe", c.v[f"b_wdt_{ti}_{ib}"])
                    W(e, "ve", c.v.get(f"b_du_{ti}_{ib - 4}",
                                       c.v.get(f"b_du_{ti - 1}_{ib + 12}", 0)))
                    SW(e, "ac")
                    TK(e.activation(wda[:, (ib % 4) * 512:(ib % 4 + 1) * 512], pwk,
                                    AF.Exp, bias=bdt_sb[:, ib:ib + 1]), "ac")
                emit("scalar", ac_spexp)
                c.v[f"b_sp_{ti}_{ib}"] = c.val("ac")

            def ac_lnq(e, q=q, ti=ti):
                SW(e, "ac")
                TK(e.activation(wda, wda, AF.Ln, bias=1.0), "ac")
            emit("scalar", ac_lnq)
            c.v[f"b_lnq_{ti}_{q}"] = c.val("ac")

            def prep(e, ib, g, ti, with_du):
                if with_du:
                    W(e, "ac", c.v[f"b_lnq_{ti}_{ib // 4}"])
                    W(e, c.v[f"b_usem_{ti}"], c.v[f"b_u_{ti}"])
                    dsl = wda[:, (ib % 4) * 512:(ib % 4 + 1) * 512]
                    TK(e.tensor_tensor(dub, dsl, ub[:, ib * TT:(ib + 1) * TT],
                                       ALU.mult), "ve")
                    SW(e, "ve")
                bs2 = brep[:, (g * 2) * TT:(g * 2 + 2) * TT].rearrange(
                    "p (s t) -> p s t", s=2)
                TK(e.tensor_tensor(dbu2[g % 2].rearrange("p (s t) -> p s t", s=2),
                                   dub.unsqueeze(1).to_broadcast([128, 2, TT]),
                                   bs2, ALU.mult), "ve")

            for ib in range(4 * q, 4 * q + 4):
                dsl = wda[:, (ib % 4) * 512:(ib % 4 + 1) * 512]

                for g in range(8):
                    dab = da2[g % 3]

                    def ac_da(e, ib=ib, g=g, dab=dab, ti=ti, dsl=dsl):
                        W(e, "ve", SCNv(ti, ib, g - 3))
                        SW(e, "ac")
                        for s in range(2):
                            TK(e.activation(dab[:, s * 512:(s + 1) * 512], dsl, AF.Exp,
                                            scale=a_sb[:, ib * S + g * 2 + s:ib * S + g * 2 + s + 1]),
                               "ac")
                    emit("scalar", ac_da)
                    c.v[f"b_da_{ti}_{ib}_{g}"] = c.val("ac")

                    if g == 2 and c.v.get("y2_pending", None) is not None:
                        emit_y2(ti, c.v["y2_pending"])
                        c.v["y2_pending"] = None

                    hb = h2[g % 2]
                    hck = hk2[g % 3]
                    qstart = (ib % 4 == 0 and g == 0)
                    pend = c.v.get("hy_pending", None)

                    def ve_sc(e, ib=ib, g=g, dab=dab, hb=hb, ti=ti,
                              qstart=qstart, pend=pend):
                        if pend is not None:
                            emit_hy(e, ti, *pend)
                        if qstart:
                            prep(e, ib, 0, ti, with_du=True)
                            SW(e, "ve")
                        W(e, "ac", c.v[f"b_da_{ti}_{ib}_{g}"])
                        W(e, "pe", IDv(ti, ib, g - 4))
                        for s in range(2):
                            TK(e.tensor_tensor_scan(
                                hb[:, s * 512:(s + 1) * 512],
                                dab[:, s * 512:(s + 1) * 512],
                                dbu2[g % 2][:, s * 512:(s + 1) * 512],
                                hc[:, ib * S + g * 2 + s:ib * S + g * 2 + s + 1],
                                ALU.mult, ALU.add), "ve")
                        if g < 7:
                            prep(e, ib, g + 1, ti, with_du=False)
                        elif ib % 4 != 3:
                            prep(e, ib + 1, 0, ti, with_du=True)
                    emit("vector", ve_sc)
                    c.v[f"b_scan_{ti}_{ib}_{g}"] = c.val("ve")
                    if pend is not None:
                        c.v[f"b_ym_{ti}_{pend[0]}_{pend[1]}"] = c.val("ve")
                        emit_id(ti, pend[0], pend[1])
                    if qstart:
                        c.v[f"b_du_{ti}_{ib}"] = c.val("ve")
                    if g == 7 and ib % 4 != 3:
                        c.v[f"b_du_{ti}_{ib + 1}"] = c.val("ve")
                    c.v["hy_pending"] = (ib, g)

                c.v["y2_pending"] = ib

                if ti + 1 < NT and 4 <= ib < 12:
                    emit_wx_one(ti + 1, ib - 4)

            if q == 0 and ti + 1 < NT:
                emit_ld_u(ti + 1)

        def ve_flush(e, ti=ti):
            emit_hy(e, ti, NI - 1, 7)
        emit("vector", ve_flush)
        c.v[f"b_ym_{ti}_{NI - 1}_{7}"] = c.val("ve")
        c.v["hy_pending"] = None
        emit_id(ti, NI - 1, 7)
        emit_y2(ti, c.v["y2_pending"])
        c.v["y2_pending"] = None

    emit("sync", lambda e, v=c.val("ys"): e.wait_ge(sem["ys"], v))
    barrier()

    ar.reset()
    w_in_z = ar.take("w_in_z", ND * I, BF16)
    w_o = ar.take("w_o", NI * D, BF16)
    wq = ar.take("wq", ND * D, BF16)
    xb2 = [ar.take(f"x2_{i}", D, F32) for i in range(5)]
    xnbf2 = [ar.take(f"xnbf2_{i}", D, BF16) for i in range(2)]
    xnf2 = ar.take("xnf2", ND * TT, BF16)
    xqf = ar.take("xqf", ND * TT, BF16)
    y2b_off = (ar.off + 63) & ~63
    y2b = ar.take("y2b", NI * TT, BF16)
    yop = ar.take("yop", NI * TT, BF16)
    szp = [ar.take(f"szp{i}", 512, BF16) for i in range(2)]
    xq_t = [ar.take(f"xqt{i}", D, BF16) for i in range(2)]
    q1s = ar.take("q1s", D, F32)
    attn_sb = ar.take("attn_sb", 512, F32)
    o_f = ar.take("o_f", 512, F32)
    outsb = [ar.take(f"outsb{i}", D, F32) for i in range(4)]
    wstage2 = ar.at(y2b_off, "wstage2", 2 * 2048, F32)
    fwstage = ar.at(y2b_off + 16 * KB, "fwstage", D, F32)
    awstage = ar.at(y2b_off + 20 * KB, "awstage", D, F32)
    bsum = ar.take("bsum", 16, F32)

    weight_load_cast(w_in_z, lambda kb: w_in_h.ap()[kb * 128:(kb + 1) * 128, I:2 * I],
                     wstage2, 2048, ND, I)
    weight_load_cast(w_o, lambda kb: w_out_h.ap()[kb * 128:(kb + 1) * 128, :],
                     wstage2, 2048, NI, D)
    ve_w2 = c.val("ve")
    ac_w2 = c.val("ac")

    def g_dma(e):
        e.dma_start(out=smll[0:1, 0:128], in_=grun[:, 0:1]).then_inc(sem["md"], 16)
    md_g = c.inc("md", 16)
    emit("sync", g_dma)

    def g_fin(e, md_g=md_g):
        W(e, "md", md_g)
        SW(e, "ve")
        TK(e.tensor_reduce(tvec[0:1, 0:1], smll[0:1, 0:128], AX.X, ALU.max), "ve")
        SW(e, "ve")
        TK(e.reciprocal(tvec[0:1, 1:2], tvec[0:1, 0:1]), "ve")
        SW(e, "ve")
        TK(e.tensor_scalar(tvec[0:1, 1:2], tvec[0:1, 1:2], QB, None, ALU.mult), "ve")
    emit("vector", g_fin)

    for kb in range(ND):
        def fw_ld(e, kb=kb):
            W(e, "ve", c.v.get(f"c_abs_{kb - 1}", 0))
            e.dma_start(out=fwstage, in_=fw_h.ap()[kb * 128:(kb + 1) * 128, :]
                        ).then_inc(sem["fs"], 16)
        wd_v = c.inc("fs", 16)
        emit("sync", fw_ld)

        def ve_abs(e, kb=kb, wd_v=wd_v):
            W(e, "fs", wd_v)
            SW(e, "ve")
            TK(e.scalar_tensor_tensor(awstage, fwstage, -1.0, fwstage,
                                     ALU.mult, ALU.max), "ve")
            SW(e, "ve")
            TK(e.tensor_reduce(bsum[:, kb:kb + 1], awstage, AX.X, ALU.add), "ve")
        emit("vector", ve_abs)
        c.v[f"c_abs_{kb}"] = c.val("ve")

    def ve_bsum(e):
        SW(e, "ve")
        TK(e.reduce_sum(bsum[:, 8:9], bsum[:, 0:8], AX.X), "ve")
    emit("vector", ve_bsum)
    ve_bs = c.val("ve")

    def pe_bsum(e, ve_bs=ve_bs):
        W(e, "ve", ve_bs)
        TK(e.matmul(pf[2][0:1, 0:1], onesc, bsum[:, 8:9], start=True, stop=True),
          "pe")
    emit("tensor", pe_bsum)
    pe_bs = c.val("pe")

    def ve_beta(e, pe_bs=pe_bs):
        W(e, "pe", pe_bs)
        SW(e, "ve")
        TK(e.tensor_scalar(tvec[0:1, 2:3], pf[2][0:1, 0:1], 1.0 / (D * D), None,
                          ALU.mult), "ve")
        SW(e, "ve")
        TK(e.tensor_scalar(tvec[0:1, 3:4], tvec[0:1, 2:3], W_EPS, 0.5,
                          ALU.add, ALU.mult), "ve")
        TK(e.tensor_tensor(tvec[0:1, 4:5], tvec[0:1, 2:3], tvec[0:1, 0:1],
                          ALU.mult), "ve")
        SW(e, "ve")
        TK(e.tensor_scalar(tvec[0:1, 4:5], tvec[0:1, 4:5], 1.0 / QB, None,
                          ALU.mult), "ve")
    emit("vector", ve_beta)
    ve_beta_v = c.val("ve")

    def pe_bcastv(e, ve_beta_v=ve_beta_v):
        W(e, "ve", ve_beta_v)
        TK(e.matmul(pf[2][:, 0:5], ones1, tvec[0:1, 0:5], start=True, stop=True),
          "pe")
    emit("tensor", pe_bcastv)
    pe_bv = c.val("pe")

    def ve_vecs(e, pe_bv=pe_bv):
        W(e, "pe", pe_bv)
        SW(e, "ve")
        TK(e.tensor_copy(vec4[:, 0:1], pf[2][:, 1:2]), "ve")
        TK(e.tensor_copy(vec4[:, 1:2], pf[2][:, 3:4]), "ve")
        TK(e.tensor_copy(vec4[:, 2:3], pf[2][:, 4:5]), "ve")
        TK(e.tensor_copy(vec4[:, 3:4], pf[2][:, 2:3]), "ve")
        SW(e, "ve")
        TK(e.tensor_scalar(g_rep, g_rep, vec4[:, 0:1], None, ALU.mult), "ve")
        TK(e.tensor_scalar(b_rep, b_rep, vec4[:, 0:1], None, ALU.mult), "ve")
    emit("vector", ve_vecs)
    ve_vecs_v = c.val("ve")

    for kb in range(ND):
        def fw_ld2(e, kb=kb):
            W(e, "ve", c.v.get(f"c_wq_{kb - 1}", ve_vecs_v))
            W(e, "ac", c.v.get(f"c_sgn_{kb - 1}", 0))
            e.dma_start(out=fwstage, in_=fw_h.ap()[kb * 128:(kb + 1) * 128, :]
                        ).then_inc(sem["fs"], 16)
        wd_v = c.inc("fs", 16)
        emit("sync", fw_ld2)

        def ac_sgn(e, kb=kb, wd_v=wd_v):
            W(e, "fs", wd_v)
            SW(e, "ac")
            TK(e.activation(awstage, fwstage, AF.Sign), "ac")
        emit("scalar", ac_sgn)
        c.v[f"c_sgn_{kb}"] = c.val("ac")

        def ve_wq(e, kb=kb, wd_v=wd_v):
            W(e, "fs", wd_v)
            W(e, "ve", ve_vecs_v)
            W(e, "ac", c.v[f"c_sgn_{kb}"])
            SW(e, "ve")
            TK(e.scalar_tensor_tensor(q1s, fwstage, -1.0, fwstage, ALU.mult,
                                     ALU.max), "ve")
            SW(e, "ve")
            TK(e.tensor_scalar(q1s, q1s, vec4[:, 1:2], None, ALU.is_gt), "ve")
            SW(e, "ve")
            TK(e.tensor_tensor(wq[:, kb * D:(kb + 1) * D], q1s, awstage, ALU.mult),
              "ve")
        emit("vector", ve_wq)
        c.v[f"c_wq_{kb}"] = c.val("ve")

    for ti in range(NT):
        b = ti // (NT // BL)
        t0 = (ti % (NT // BL)) * TT

        for tb in range(NTB):
            xb = xb2[(ti * NTB + tb) % 5]
            xs = f"xd{(ti * NTB + tb) % 5}"

            def ld(e, b=b, t0=t0, tb=tb, xb=xb, ti=ti, xs=xs):
                W(e, "ve", c.v.get(f"c_oadd_{ti - 1}", 0))
                e.dma_start(out=xb, in_=x_h.ap()[b, t0 + tb * 128:t0 + (tb + 1) * 128, :]
                            ).then_inc(sem[xs], 16)
            c.v[f"c_x_{ti}_{tb}"] = c.inc(xs, 16)
            c.v[f"c_xsem_{ti}_{tb}"] = xs
            emit("sync", ld)

        def ld_y2(e, ti=ti):
            W(e, "pe", c.v.get(f"c_attn_{ti - 1}_{ND - 1}", 0))
            W(e, "ve", c.v[f"c_wq_{ND - 1}"])
            e.dma_start(out=y2b, in_=y2_hbm.ap()[ti * 128:(ti + 1) * 128, :]
                        ).then_inc(sem["yl"], 16)
        c.v[f"c_y2_{ti}"] = c.inc("yl", 16)
        emit("sync", ld_y2)

        for tb in range(NTB):
            sti = ti * NTB + tb
            xb = xb2[sti % 5]
            xn = xnbf2[tb % 2]

            def ve_xn2(e, xb=xb, xn=xn, sti=sti, ti=ti, tb=tb):
                W(e, c.v[f"c_xsem_{ti}_{tb}"], c.v[f"c_x_{ti}_{tb}"])
                W(e, "pe", c.v.get(f"c_tp_{ti}_{tb - 2}", 0))
                SW(e, "ve")
                TK(e.tensor_scalar(xn, xb, stash[:, sti:sti + 1],
                                  stash[:, NST + sti:NST + sti + 1],
                                  ALU.add, ALU.mult), "ve")
            emit("vector", ve_xn2)
            c.v[f"c_xn_{ti}_{tb}"] = c.val("ve")

            pbk = pb[tb % 2]

            def pe_tp(e, xn=xn, pbk=pbk, ti=ti, tb=tb):
                W(e, "ve", c.v[f"c_xn_{ti}_{tb}"])
                W(e, "ve", c.v.get(f"c_tpev_{ti}_{tb - 2}", 0))
                for db in range(ND):
                    TK(e.transpose(pbk[:, db * 128:(db + 1) * 128],
                                  xn[:, db * 128:(db + 1) * 128], idbf), "pe")
            emit("tensor", pe_tp)
            c.v[f"c_tp_{ti}_{tb}"] = c.val("pe")

            def ve_tpev(e, pbk=pbk, tb=tb, ti=ti):
                W(e, "pe", c.v[f"c_tp_{ti}_{tb}"])
                SW(e, "ve")
                src = pbk.rearrange("p (d t) -> p d t", d=ND)
                dst = xnf2.rearrange("p (d t) -> p d t", d=ND)[:, :, tb * 128:(tb + 1) * 128]
                TK(e.tensor_copy(dst, src), "ve")
            emit("vector", ve_tpev)
            c.v[f"c_tpev_{ti}_{tb}"] = c.val("ve")

        for ib in range(NI):
            pwk = pf[ib % 2]

            def pe_z(e, ib=ib, pwk=pwk, ti=ti):
                W(e, "ac", ac_w2)
                W(e, "ve", c.v[f"c_tpev_{ti}_{NTB - 1}"])
                W(e, "ac", c.v.get(f"c_sz_{ti}_{ib - 2}",
                                   c.v.get(f"c_sz_{ti - 1}_{ib + 14}", 0)))
                for kb in range(ND):
                    mm = e.matmul(pwk, w_in_z[:, (kb * NI + ib) * 128:(kb * NI + ib + 1) * 128],
                                  xnf2[:, kb * TT:(kb + 1) * TT],
                                  start=(kb == 0), stop=(kb == ND - 1))
                TK(mm, "pe")
            emit("tensor", pe_z)
            c.v[f"c_z_{ti}_{ib}"] = c.val("pe")

            def ac_sz(e, ib=ib, pwk=pwk, ti=ti):
                W(e, "pe", c.v[f"c_z_{ti}_{ib}"])
                W(e, "ve", c.v.get(f"c_yopk_{ti}_{ib - 2}", 0))
                SW(e, "ac")
                TK(e.activation(szp[ib % 2], pwk, AF.Silu), "ac")
            emit("scalar", ac_sz)
            c.v[f"c_sz_{ti}_{ib}"] = c.val("ac")

            def ve_yop(e, ib=ib, ti=ti):
                W(e, "ac", c.v[f"c_sz_{ti}_{ib}"])
                W(e, "yl", c.v[f"c_y2_{ti}"])
                SW(e, "ve")
                TK(e.tensor_tensor(yop[:, ib * TT:(ib + 1) * TT],
                                  y2b[:, ib * TT:(ib + 1) * TT], szp[ib % 2],
                                  ALU.mult), "ve")
            emit("vector", ve_yop)
            c.v[f"c_yopk_{ti}_{ib}"] = c.val("ve")
        c.v[f"c_yop_{ti}"] = c.val("ve")

        for tb in range(NTB):
            sti = ti * NTB + tb
            xn = xnbf2[tb % 2]
            xqt = xq_t[tb % 2]

            def ve_xq(e, xn=xn, sti=sti, xqt=xqt, ti=ti, tb=tb):
                xb = xb2[sti % 5]
                W(e, "pe", c.v.get(f"c_tpq_{ti}_{tb - 2}", 0))
                SW(e, "ve")
                TK(e.tensor_scalar(xn, xb, stash[:, sti:sti + 1],
                                  stash[:, NST + sti:NST + sti + 1],
                                  ALU.add, ALU.mult), "ve")
                SW(e, "ve")
                TK(e.scalar_tensor_tensor(q1s, xn,
                                         stash[:, 2 * NST + sti:2 * NST + sti + 1],
                                         g_rep, ALU.mult, ALU.mult), "ve")
                SW(e, "ve")
                TK(e.tensor_add(q1s, q1s, b_rep), "ve")
                SW(e, "ve")
                TK(e.tensor_scalar(xqt, q1s, CLIPV, -CLIPV, ALU.min, ALU.max), "ve")
            emit("vector", ve_xq)
            c.v[f"c_xq_{ti}_{tb}"] = c.val("ve")

            pbk = pb[tb % 2]

            def pe_tpq(e, xqt=xqt, pbk=pbk, ti=ti, tb=tb):
                W(e, "ve", c.v[f"c_xq_{ti}_{tb}"])
                W(e, "ve", c.v.get(f"c_tpqev_{ti}_{tb - 2}", 0))
                for db in range(ND):
                    TK(e.transpose(pbk[:, db * 128:(db + 1) * 128],
                                  xqt[:, db * 128:(db + 1) * 128], idbf), "pe")
            emit("tensor", pe_tpq)
            c.v[f"c_tpq_{ti}_{tb}"] = c.val("pe")

            def ve_tpqev(e, pbk=pbk, tb=tb, ti=ti):
                W(e, "pe", c.v[f"c_tpq_{ti}_{tb}"])
                SW(e, "ve")
                src = pbk.rearrange("p (d t) -> p d t", d=ND)
                dst = xqf.rearrange("p (d t) -> p d t", d=ND)[:, :, tb * 128:(tb + 1) * 128]
                TK(e.tensor_copy(dst, src), "ve")
            emit("vector", ve_tpqev)
            c.v[f"c_tpqev_{ti}_{tb}"] = c.val("ve")

        for db in range(ND):
            pa = pf[4 + db % 2]
            pq = pf[2 + db % 2]
            ptr = pf[4 + db % 2]

            def pe_attn(e, db=db, pa=pa, ti=ti):
                W(e, "ve", c.v[f"c_yop_{ti}"])
                W(e, "ve", c.v.get(f"c_oaddp_{ti}_{db - 2}",
                                   c.v.get(f"c_oaddp_{ti - 1}_{db + 6}", 0)))
                for kb in range(NI):
                    mm = e.matmul(pa, w_o[:, kb * D + db * 128:kb * D + (db + 1) * 128],
                                  yop[:, kb * TT:(kb + 1) * TT],
                                  start=(kb == 0), stop=(kb == NI - 1))
                TK(mm, "pe")
            emit("tensor", pe_attn)
            c.v[f"c_attn_{ti}_{db}"] = c.val("pe")

            def pe_ffn(e, db=db, pq=pq, ti=ti):
                W(e, "ve", c.v[f"c_tpqev_{ti}_{NTB - 1}"])
                W(e, "ve", c.v.get(f"c_of_{ti}_{db - 2}",
                                   c.v.get(f"c_of_{ti - 1}_{ND - 1}", 0)))
                for kb in range(ND):
                    mm = e.matmul(pq, wq[:, kb * D + db * 128:kb * D + (db + 1) * 128],
                                  xqf[:, kb * TT:(kb + 1) * TT],
                                  start=(kb == 0), stop=(kb == ND - 1))
                TK(mm, "pe")
            emit("tensor", pe_ffn)
            c.v[f"c_ffn_{ti}_{db}"] = c.val("pe")

            def ac_attnev(e, pa=pa, ti=ti, db=db):
                W(e, "pe", c.v[f"c_attn_{ti}_{db}"])
                W(e, "ve", c.v.get(f"c_of_{ti}_{db - 1}", 0))
                SW(e, "ac")
                TK(e.copy(attn_sb, pa), "ac")
            emit("scalar", ac_attnev)
            c.v[f"c_attnev_{ti}_{db}"] = c.val("ac")

            def ve_of(e, pq=pq, ti=ti, db=db):
                W(e, "pe", c.v[f"c_ffn_{ti}_{db}"])
                W(e, "ac", c.v[f"c_attnev_{ti}_{db}"])
                W(e, "pe", c.v.get(f"c_tro_{ti}_{db - 1}", 0))
                SW(e, "ve")
                TK(e.scalar_tensor_tensor(o_f, pq, vec4[:, 2:3], attn_sb,
                                         ALU.mult, ALU.add), "ve")
            emit("vector", ve_of)
            c.v[f"c_of_{ti}_{db}"] = c.val("ve")

            def pe_tro(e, db=db, ptr=ptr, ti=ti):
                W(e, "ve", c.v[f"c_of_{ti}_{db}"])
                W(e, "ve", c.v.get(f"c_oaddp_{ti}_{db - 2}",
                                   c.v.get(f"c_oaddp_{ti - 1}_{ND - 1}", 0)))
                for tb in range(NTB):
                    TK(e.transpose(ptr[:, tb * 128:(tb + 1) * 128],
                                  o_f[:, tb * 128:(tb + 1) * 128], idf), "pe")
            emit("tensor", pe_tro)
            c.v[f"c_tro_{ti}_{db}"] = c.val("pe")

            def ve_outadd(e, db=db, ptr=ptr, ti=ti):
                W(e, "pe", c.v[f"c_tro_{ti}_{db}"])
                if db == 0:
                    for q in range(NTB):
                        W(e, f"od{q}", c.v.get(f"c_odma_{ti - 1}_{q}", 0))
                SW(e, "ve")
                for tb in range(NTB):
                    xb = xb2[(ti * NTB + tb) % 5]
                    TK(e.tensor_tensor(outsb[tb][:, db * 128:(db + 1) * 128],
                                      xb[:, db * 128:(db + 1) * 128],
                                      ptr[:, tb * 128:(tb + 1) * 128], ALU.add),
                      "ve")
            emit("vector", ve_outadd)
            c.v[f"c_oaddp_{ti}_{db}"] = c.val("ve")
        c.v[f"c_oadd_{ti}"] = c.val("ve")

        for tb in range(NTB):
            def st_out(e, tb=tb, b=b, t0=t0, ti=ti):
                W(e, "ve", c.v[f"c_oadd_{ti}"])
                e.dma_start(out=out_h.ap()[b, t0 + tb * 128:t0 + (tb + 1) * 128, :],
                            in_=outsb[tb]).then_inc(sem[f"od{tb}"], 16)
            c.v[f"c_odma_{ti}_{tb}"] = c.inc(f"od{tb}", 16)
            emit("sync", st_out)

    def fin(e):
        for q in range(NTB):
            W(e, f"od{q}", c.val(f"od{q}"))
    emit("sync", fin)

    flush_sections()
    es.close()
    return nc


def kernel(**inputs):
    x = np.ascontiguousarray(np.asarray(inputs["L0_attn_inputs"], np.float32))
    nc = build_program()
    common = dict(
        w_in=np.ascontiguousarray(inputs["L0_attn_W_in"], np.float32),
        conv_w=np.ascontiguousarray(inputs["L0_attn_conv_w"], np.float32),
        conv_b=np.ascontiguousarray(inputs["L0_attn_conv_b"], np.float32),
        w_x=np.ascontiguousarray(inputs["L0_attn_W_x"], np.float32),
        w_dt=np.ascontiguousarray(inputs["L0_attn_W_dt"], np.float32),
        b_dt=np.ascontiguousarray(inputs["L0_attn_b_dt"], np.float32),
        a_log=np.ascontiguousarray(inputs["L0_attn_A_log"], np.float32),
        w_out=np.ascontiguousarray(inputs["L0_attn_W_out"], np.float32),
        fw=np.ascontiguousarray(inputs["L0_ffn_weight"], np.float32),
        lng=np.ascontiguousarray(inputs["L0_ffn_ln_gamma"], np.float32),
        lnb=np.ascontiguousarray(inputs["L0_ffn_ln_beta"], np.float32),
    )
    in_maps = [dict(common, x=np.ascontiguousarray(x[2 * cc:2 * cc + 2]))
               for cc in range(NCORES)]
    res = run_bass_kernel_spmd(nc, in_maps, core_ids=list(range(NCORES)))
    return np.concatenate([r["out"] for r in res.results], axis=0)


if __name__ == "__main__":
    build_program()
    print("build OK")
